# version 06
# baseline (speedup 1.0000x reference)
import numpy as np
import ml_dtypes
import concourse.bass as bass
import concourse.mybir as mybir
from concourse.bass_utils import run_bass_kernel_spmd
from contextlib import ExitStack

F32 = mybir.dt.float32
BF16 = mybir.dt.bfloat16
AF = mybir.ActivationFunctionType
OP = mybir.AluOpType
AX = mybir.AxisListType
BFD = ml_dtypes.bfloat16

D = 1024
S = 4096
BATCH = 4
H = 16
DH = 64
NCH = 8
NTB = 8
OWN = 2048
NOJ = 4
NEG = -3.0e38


_ENG_SEM = {"tensor": "pe", "vector": "dve", "scalar": "act",
            "gpsimd": "gp", "sync": "sp"}
SAME_ENGINE_WAITS = True


class Prog:

    ENGS = ("tensor", "vector", "scalar", "gpsimd", "sync")

    def __init__(self, nc, ctx):
        self.nc = nc
        self.ops = {e: [] for e in self.ENGS}
        self.sems = {}
        for s in ("pe", "dve", "act", "gp", "sp", "gdma", "sdma",
                  "sd0", "sd1", "sd2", "sd3", "sd4", "sd5",
                  "gd0", "gd1", "gd2"):
            h = ctx.enter_context(nc.semaphore(s))
            self.sems[s] = [h, 0]
        self.waited = {e: {} for e in self.ENGS}

    def _wait(self, eng, sname, v):
        if v is None:
            v = self.sems[sname][1]
        if v <= 0:
            return
        own = _ENG_SEM[eng]
        if sname == own and not SAME_ENGINE_WAITS:
            return
        w = self.waited[eng]
        if w.get(sname, 0) >= v:
            return
        w[sname] = v
        h = self.sems[sname][0]
        self.ops[eng].append(lambda e, h=h, v=v: e.wait_ge(h, v))

    def waits(self, eng, deps):
        for (sname, v) in deps:
            self._wait(eng, sname, v)

    def op(self, eng, fn, dma=False, dsem=None):
        if dma:
            sname = dsem or ("gdma" if eng == "gpsimd" else "sdma")
            k = 16
            prev = self.sems[sname][1]
            if prev > 0:
                self._wait(eng, sname, prev)
        else:
            sname = _ENG_SEM[eng]
            k = 1
        h = self.sems[sname][0]
        self.ops[eng].append(lambda e, fn=fn, h=h, k=k: fn(e).then_inc(h, k))
        self.sems[sname][1] += k
        if dma:
            return (sname, None)
        return (sname, self.sems[sname][1])

    def finalize(self, block):
        for eng in self.ENGS:
            if self.ops[eng]:
                getattr(block, eng)(
                    lambda e, eng=eng: [f(e) for f in self.ops[eng]])


class Buf:

    def __init__(self):
        self.w = None
        self.w2 = None
        self.r = {}

    def deps_for_read(self):
        d = []
        if self.w:
            d.append(self.w)
        if self.w2:
            d.append(self.w2)
        return d

    def deps_for_write(self):
        d = list(self.r.items())
        if self.w:
            d.append(self.w)
        if self.w2:
            d.append(self.w2)
        return d

    def did_read(self, ms):
        s, v = ms
        cur = self.r.get(s, 0)
        if cur is None:
            return
        if v is None or cur < v:
            self.r[s] = v

    def did_write(self, ms, ms2=None):
        self.w = ms
        self.w2 = ms2
        self.r = {}


def rd(p, eng, fn, *bufs, dma=False):
    for b in bufs:
        p.waits(eng, b.deps_for_read())
    ms = p.op(eng, fn, dma=dma)
    for b in bufs:
        b.did_read(ms)
    return ms


def op(p, eng, fn, reads=(), writes=(), dma=False, dsem=None):
    for b in reads:
        p.waits(eng, b.deps_for_read())
    for b in writes:
        p.waits(eng, b.deps_for_write())
    ms = p.op(eng, fn, dma=dma, dsem=dsem)
    for b in reads:
        b.did_read(ms)
    for b in writes:
        b.did_write(ms)
    return ms


def chunk_ap(t, W, crow, col0, n):
    return bass.AP(t, crow * 128 * W + col0, [[W, 128], [1, n]])


def w_ap(t, din, dout):
    return bass.AP(t, 0, [[dout, 128], [128 * dout, din // 128], [1, dout]])


def bcast_ap(t, offset, npart, n):
    return bass.AP(t, offset, [[1, 1], [0, npart], [1, n]])


def emit_ln(p, env, xsrc, W, col0, ydst_fn, n=512, post_fn=None):
    cs, csb = env["cs"], env["cs_b"]
    c2, c2b = env["cs2"], env["cs2_b"]
    pst, pstb = env["p_stat"], env["p_stat_b"]
    if "p_stat2" in env:
        pst2, pst2b = env["p_stat2"], env["p_stat2_b"]
        pst2ap = pst2[0:1, 0:n]
    else:
        pst2ap = pst[0:1, n:2 * n]
        pst2b = pstb
    ones = env["ones"]
    onesb = env["ones_b"]
    use_xfull = "xfull" in env
    if use_xfull:
        xf = env["xfull"]
        nxf = len(xf)
        xi = env["xfull_i"][0] = (env["xfull_i"][0] + 1) % nxf
        xt_, xtb = xf[xi], env["xfull_b"][xi]
        op(p, "sync", lambda e: e.dma_start(
            out=xt_[:, 0:NCH * n],
            in_=bass.AP(xsrc, col0, [[W, 128], [128 * W, NCH], [1, n]])),
            writes=[xtb], dma=True, dsem=f"sd{xi}")
        xs_of = lambda c: (xt_[:, c * n:(c + 1) * n], xtb)
    else:
        xs, xsb = env["xs"], env["xs_b"]

        def xs_of(c):
            i = c % len(xs)
            op(p, "sync", lambda e, i=i, c=c: e.dma_start(
                out=xs[i][:, 0:n], in_=chunk_ap(xsrc, W, c, col0, n)),
                writes=[xsb[i]], dma=True, dsem=f"sd{i}")
            return xs[i][:, 0:n], xsb[i]
    srcs = []
    for c in range(NCH):
        xap, xb = xs_of(c)
        srcs.append((xap, xb))
        op(p, "vector", lambda e, xap=xap, ci=c % len(cs): e.tensor_copy(
            cs[ci][:, 0:n], xap),
            reads=[xb], writes=[csb[c % len(cs)]])
        op(p, "scalar", lambda e, xap=xap, ci=c % len(c2): e.activation(
            c2[ci][:, 0:n], xap, AF.Square, bias=0.0, scale=1.0),
            reads=[xb], writes=[c2b[c % len(c2)]])
        op(p, "tensor", lambda e, ci=c % len(cs), c=c: e.matmul(
            pst[0:1, 0:n], ones[:, 0:1], cs[ci][:, 0:n],
            start=(c == 0), stop=(c == NCH - 1)),
            reads=[csb[c % len(cs)], onesb],
            writes=([pstb] if c == 0 else ()))
        mst = op(p, "tensor", lambda e, ci=c % len(c2), c=c, pst2ap=pst2ap:
                 e.matmul(
            pst2ap, ones[:, 0:1], c2[ci][:, 0:n],
            start=(c == 0), stop=(c == NCH - 1)),
            reads=[c2b[c % len(c2)], onesb],
            writes=([pst2b] if (c == 0 and pst2b is not pstb) else ()))
    pstb.did_write(mst)
    if pst2b is not pstb:
        pst2b.did_write(mst)
    r1, r1b = env["row1"], env["row1_b"]
    r2, r2b = env["row2"], env["row2_b"]
    r3, r3b = env["row3"], env["row3_b"]
    inv = 1.0 / D
    op(p, "vector", lambda e: e.tensor_scalar(
        r1[0:1, 0:n], pst[0:1, 0:n], inv, None, op0=OP.mult),
        reads=[pstb], writes=[r1b])
    op(p, "vector", lambda e, pst2ap=pst2ap: e.tensor_scalar(
        r2[0:1, 0:n], pst2ap, inv, None, op0=OP.mult),
        reads=[pst2b], writes=[r2b])
    op(p, "vector", lambda e: e.tensor_tensor(
        r3[0:1, 0:n], r1[0:1, 0:n], r1[0:1, 0:n], op=OP.mult),
        reads=[r1b], writes=[r3b])
    op(p, "vector", lambda e: e.tensor_tensor(
        r2[0:1, 0:n], r2[0:1, 0:n], r3[0:1, 0:n], op=OP.subtract),
        reads=[r3b], writes=[r2b])
    op(p, "vector", lambda e: e.tensor_scalar(
        r2[0:1, 0:n], r2[0:1, 0:n], 1e-5, None, op0=OP.add),
        reads=[r2b], writes=[r2b])
    op(p, "scalar", lambda e: e.activation(
        r3[0:1, 0:n], r2[0:1, 0:n], AF.Sqrt, bias=0.0, scale=1.0),
        reads=[r2b], writes=[r3b])
    op(p, "vector", lambda e: e.reciprocal(r3[0:1, 0:n], r3[0:1, 0:n]),
        reads=[r3b], writes=[r3b])
    op(p, "vector", lambda e: e.tensor_tensor(
        r2[0:1, 0:n], r1[0:1, 0:n], r3[0:1, 0:n], op=OP.mult),
        reads=[r1b, r3b], writes=[r2b])
    Br, Brb = env["B_rstd"], env["B_rstd_b"]
    Bm, Bmb = env["B_mur"], env["B_mur_b"]
    op(p, "sync", lambda e: e.dma_start(
        out=Br[:, 0:n], in_=bcast_ap(r3, 0, 128, n)),
        reads=[r3b], writes=[Brb], dma=True, dsem="gd0")
    op(p, "sync", lambda e: e.dma_start(
        out=Bm[:, 0:n], in_=bcast_ap(r2, 0, 128, n)),
        reads=[r2b], writes=[Bmb], dma=True, dsem="gd0")
    if ydst_fn is None:
        return
    nr, nrb = env["nrm"], env["nrm_b"]
    for c in range(NCH):
        if use_xfull:
            xap, xb = srcs[c]
        else:
            xap, xb = xs_of(c)
        ni = c % len(nr)
        op(p, "vector", lambda e, xap=xap, ni=ni: e.tensor_tensor(
            nr[ni][:, 0:n], xap, Br[:, 0:n], op=OP.mult),
            reads=[xb, Brb], writes=[nrb[ni]])
        dst, dstbuf = ydst_fn(c)
        op(p, "vector", lambda e, ni=ni, dst=dst: e.tensor_tensor(
            dst, nr[ni][:, 0:n], Bm[:, 0:n], op=OP.subtract),
            reads=[nrb[ni], Bmb], writes=([dstbuf] if dstbuf else ()))
        if post_fn is not None:
            post_fn(c)


def make_ln_env(p, ctx, nc, env, nxs=3, nxfull=0):
    sb = lambda name, shp, dt: ctx.enter_context(nc.sbuf_tensor(name, shp, dt))
    env["xs"] = [sb(f"ln_xs{i}", [128, 512], F32) for i in range(nxs)]
    env["xs_b"] = [Buf() for _ in range(nxs)]
    if nxfull:
        env["xfull"] = [sb(f"ln_xf{i}", [128, NCH * 512], F32)
                        for i in range(nxfull)]
        env["xfull_b"] = [Buf() for _ in range(nxfull)]
        env["xfull_i"] = [0]
    env["cs"] = [sb(f"ln_cs{i}", [128, 512], BF16) for i in range(2)]
    env["cs_b"] = [Buf() for _ in range(2)]
    env["cs2"] = [sb(f"ln_c2{i}", [128, 512], BF16) for i in range(2)]
    env["cs2_b"] = [Buf() for _ in range(2)]
    env["nrm"] = [sb(f"ln_nr{i}", [128, 512], F32) for i in range(2)]
    env["nrm_b"] = [Buf() for _ in range(2)]
    env["row1"] = sb("ln_r1", [1, 512], F32)
    env["row1_b"] = Buf()
    env["row2"] = sb("ln_r2", [1, 512], F32)
    env["row2_b"] = Buf()
    env["row3"] = sb("ln_r3", [1, 512], F32)
    env["row3_b"] = Buf()
    env["B_rstd"] = sb("ln_Br", [128, 512], F32)
    env["B_rstd_b"] = Buf()
    env["B_mur"] = sb("ln_Bm", [128, 512], F32)
    env["B_mur_b"] = Buf()
    env["ones"] = sb("ones", [128, 2], BF16)
    env["ones_b"] = Buf()
    ms = p.op("gpsimd", lambda e: e.memset(env["ones"][:, :], 1.0))
    env["ones_b"].did_write(ms)


def build_ffn(ncycles=3, debug=False, gf_tbs=NTB):
    nc = bass.Bass("TRN2", target_bir_lowering=False)
    xt = nc.dram_tensor("xt", [D, S], F32, kind="ExternalInput")
    xo = nc.dram_tensor("xo", [D, OWN], F32, kind="ExternalInput")
    w1 = nc.dram_tensor("w1", [D, D], F32, kind="ExternalInput")
    w2 = nc.dram_tensor("w2", [D, 512], F32, kind="ExternalInput")
    w3 = nc.dram_tensor("w3", [512, 256], F32, kind="ExternalInput")
    w4 = nc.dram_tensor("w4", [256, 1], F32, kind="ExternalInput")
    gw1 = nc.dram_tensor("gw1", [D, 512], F32, kind="ExternalInput")
    gw2 = nc.dram_tensor("gw2", [512, 256], F32, kind="ExternalInput")
    gw3 = nc.dram_tensor("gw3", [256, 1], F32, kind="ExternalInput")
    b1 = nc.dram_tensor("b1", [128, 8], F32, kind="ExternalInput")
    b2 = nc.dram_tensor("b2", [128, 4], F32, kind="ExternalInput")
    b3 = nc.dram_tensor("b3", [128, 2], F32, kind="ExternalInput")
    b4 = nc.dram_tensor("b4", [1, 1], F32, kind="ExternalInput")
    gb1 = nc.dram_tensor("gb1", [128, 4], F32, kind="ExternalInput")
    gb2 = nc.dram_tensor("gb2", [128, 2], F32, kind="ExternalInput")
    gb3 = nc.dram_tensor("gb3", [1, 1], F32, kind="ExternalInput")
    ar = nc.dram_tensor("ar", [1, 1], F32, kind="ExternalInput")
    upd = nc.dram_tensor("upd", [D, OWN], F32, kind="ExternalOutput")
    if debug:
        dbg_cells = nc.dram_tensor("dbg_cells", [1, 8], F32, kind="ExternalOutput")
        dbg_trow = nc.dram_tensor("dbg_trow", [1, 512], F32, kind="ExternalOutput")
        dbg_h1 = nc.dram_tensor("dbg_h1", [128, 512], F32, kind="ExternalOutput")
        dbg_mean = nc.dram_tensor("dbg_mean", [128, 8], F32, kind="ExternalOutput")
        dbg_ycs = nc.dram_tensor("dbg_ycs", [128, 512], F32, kind="ExternalOutput")
        dbg_red = nc.dram_tensor("dbg_red", [128, 1], F32, kind="ExternalOutput")
        dbg_hcol = nc.dram_tensor("dbg_hcol", [128, 8], F32, kind="ExternalOutput")

    with ExitStack() as ctx:
        sb = lambda name, shp, dt: ctx.enter_context(nc.sbuf_tensor(name, shp, dt))
        p = Prog(nc, ctx)
        env = {}
        make_ln_env(p, ctx, nc, env, nxs=2, nxfull=2)

        state = sb("state", [128, NCH * OWN], F32)
        state_bj = [Buf() for _ in range(NOJ)]
        sbf = sb("sbf", [128, NCH * 512], BF16)
        sbf_b = Buf()
        w1b, w1bb = sb("w1b", [128, NCH * D], BF16), Buf()
        w2b, w2bb = sb("w2b", [128, NCH * 512], BF16), Buf()
        w3b, w3bb = sb("w3b", [128, 4 * 256], BF16), Buf()
        w4b, w4bb = sb("w4b", [128, 2], BF16), Buf()
        gw1b, gw1bb = sb("gw1b", [128, NCH * 512], BF16), Buf()
        gw2b, gw2bb = sb("gw2b", [128, 4 * 256], BF16), Buf()
        gw3b, gw3bb = sb("gw3b", [128, 2], BF16), Buf()
        h1, h1b = sb("h1", [128, NCH * 512], BF16), Buf()
        h2, h2b = sb("h2", [128, 4 * 512], BF16), Buf()
        h3, h3b = sb("h3", [128, 2 * 512], BF16), Buf()
        b1c, b1cb = sb("b1c", [128, 8], F32), Buf()
        b2c, b2cb = sb("b2c", [128, 4], F32), Buf()
        b3c, b3cb = sb("b3c", [128, 2], F32), Buf()
        b4c, b4cb = sb("b4c", [1, 1], F32), Buf()
        gb1c, gb1cb = sb("gb1c", [128, 4], F32), Buf()
        gb2c, gb2cb = sb("gb2c", [128, 2], F32), Buf()
        gb3c, gb3cb = sb("gb3c", [1, 1], F32), Buf()
        arc, arcb = sb("arc", [1, 1], F32), Buf()
        ycs, ycsb = sb("ycs", [128, 512], BF16), Buf()
        junk, junkb = sb("junk", [128, 512], BF16), Buf()
        red, redb = sb("red", [128, 1], F32), Buf()
        colsum, colsumb = sb("colsum", [128, 8], F32), Buf()
        mean_bf, mean_bfb = sb("mean_bf", [128, 8], BF16), Buf()
        gcell, gcellb = sb("gcell", [1, 8], F32), Buf()
        acell, acellb = sb("acell", [1, 1], F32), Buf()
        bcell, bcellb = sb("bcell", [1, 1], F32), Buf()
        hcol, hcolb = sb("hcol", [128, 8], BF16), Buf()
        trow2 = [sb(f"trow{i}", [1, 512], F32) for i in range(2)]
        trowb2 = [Buf() for _ in range(2)]
        nrow2 = [sb(f"nrow{i}", [1, 512], F32) for i in range(2)]
        nrowb2 = [Buf() for _ in range(2)]
        B_cp2 = [sb(f"B_cp{i}", [128, 512], F32) for i in range(2)]
        B_cpb2 = [Buf() for _ in range(2)]
        B_cn2 = [sb(f"B_cn{i}", [128, 512], F32) for i in range(2)]
        B_cnb2 = [Buf() for _ in range(2)]
        tpr, tprb = sb("tpr", [128, 512], F32), Buf()
        t2, t2b = sb("t2", [128, 512], F32), Buf()
        tprg, tprgb = sb("tprg", [128, 512], F32), Buf()
        t2g, t2gb = sb("t2g", [128, 512], F32), Buf()
        us = [sb(f"us{i}", [128, 512], F32) for i in range(2)]
        usb = [Buf() for _ in range(2)]

        P_A = ctx.enter_context(nc.psum_tensor("P_A", [128, 1024], F32))
        P_B = ctx.enter_context(nc.psum_tensor("P_B", [128, 1024], F32))
        P_C = ctx.enter_context(nc.psum_tensor("P_C", [128, 512], F32))
        P_D = ctx.enter_context(nc.psum_tensor("P_D", [2, 1024], F32))
        P_E = ctx.enter_context(nc.psum_tensor("P_E", [128, 512], F32))
        pab = [Buf(), Buf()]
        pbb = [Buf(), Buf()]
        pcb, peb = Buf(), Buf()
        env["p_stat"], env["p_stat_b"] = P_D, Buf()
        pdb = env["p_stat_b"]

        for (dst, dstb, src, di, do) in (
                (w1b, w1bb, w1, D, D), (w2b, w2bb, w2, D, 512),
                (w3b, w3bb, w3, 512, 256), (w4b, w4bb, w4, 256, 1),
                (gw1b, gw1bb, gw1, D, 512), (gw2b, gw2bb, gw2, 512, 256),
                (gw3b, gw3bb, gw3, 256, 1)):
            op(p, "gpsimd", lambda e, dst=dst, src=src, di=di, do=do:
               e.dma_start(out=dst[:, :], in_=w_ap(src, di, do),
                           allow_slow_non_contiguous=(do == 1)),
               writes=[dstb], dma=True, dsem="gd1")
        for (dst, dstb, src) in ((b1c, b1cb, b1), (b2c, b2cb, b2),
                                 (b3c, b3cb, b3), (b4c, b4cb, b4),
                                 (gb1c, gb1cb, gb1), (gb2c, gb2cb, gb2),
                                 (gb3c, gb3cb, gb3), (arc, arcb, ar)):
            op(p, "sync", lambda e, dst=dst, src=src:
               e.dma_start(out=dst[:, :], in_=src[:, :]),
               writes=[dstb], dma=True)

        ms = p.op("gpsimd", lambda e: e.memset(colsum[:, :], 0.0))
        colsumb.did_write(ms)
        for tb in range(gf_tbs):
            def ydst(c):
                return ycs[:, :], ycsb

            def post(c):
                op(p, "scalar", lambda e: e.activation(
                    junk[:, :], ycs[:, :], AF.Copy, bias=0.0,
                    scale=1.0, accum_out=red[:, 0:1]),
                    reads=[ycsb], writes=[redb, junkb])
                op(p, "gpsimd", lambda e, c=c: e.tensor_tensor(
                    colsum[:, c:c + 1], colsum[:, c:c + 1], red[:, 0:1], op=OP.add),
                    reads=[redb, colsumb], writes=[colsumb])
            emit_ln(p, env, xt, S, tb * 512, ydst, post_fn=post)

        op(p, "vector", lambda e: e.tensor_scalar(
            mean_bf[:, :], colsum[:, :], 1.0 / S, None, op0=OP.mult),
            reads=[colsumb], writes=[mean_bfb])

        for o in range(4):
            for c in range(NCH):
                op(p, "tensor", lambda e, o=o, c=c: e.matmul(
                    P_C[:, o:o + 1],
                    gw1b[:, c * 512 + o * 128: c * 512 + (o + 1) * 128],
                    mean_bf[:, c:c + 1], start=(c == 0), stop=(c == NCH - 1)),
                    reads=[gw1bb, mean_bfb], writes=([pcb] if (o == 0 and c == 0) else ()))
        pcb.did_write(("pe", p.sems["pe"][1]))
        for o in range(4):
            op(p, "scalar", lambda e, o=o: e.activation(
                hcol[:, o:o + 1], P_C[:, o:o + 1], AF.Gelu,
                bias=gb1c[:, o:o + 1], scale=1.0),
                reads=[pcb, gb1cb], writes=([hcolb] if o == 0 else ()))
        hcolb.did_write(("act", p.sems["act"][1]))
        for o in range(2):
            for c in range(4):
                op(p, "tensor", lambda e, o=o, c=c: e.matmul(
                    P_E[:, o:o + 1],
                    gw2b[:, c * 256 + o * 128: c * 256 + (o + 1) * 128],
                    hcol[:, c:c + 1], start=(c == 0), stop=(c == 3)),
                    reads=[gw2bb, hcolb], writes=([peb] if (o == 0 and c == 0) else ()))
        peb.did_write(("pe", p.sems["pe"][1]))
        for o in range(2):
            op(p, "scalar", lambda e, o=o: e.activation(
                hcol[:, 4 + o:5 + o], P_E[:, o:o + 1], AF.Gelu,
                bias=gb2c[:, o:o + 1], scale=1.0),
                reads=[peb, gb2cb], writes=[hcolb])
        for c in range(2):
            op(p, "tensor", lambda e, c=c: e.matmul(
                P_D[0:1, 0:1], gw3b[:, c:c + 1], hcol[:, 4 + c:5 + c],
                start=(c == 0), stop=(c == 1)),
                reads=[gw3bb, hcolb], writes=([pdb] if c == 0 else ()))
        pdb.did_write(("pe", p.sems["pe"][1]))
        op(p, "scalar", lambda e: e.activation(
            gcell[0:1, 0:1], P_D[0:1, 0:1], AF.Sigmoid, bias=gb3c[0:1, 0:1],
            scale=1.0), reads=[pdb, gb3cb], writes=[gcellb])
        op(p, "gpsimd", lambda e: e.tensor_tensor(
            gcell[0:1, 1:2], arc[0:1, 0:1], gcell[0:1, 0:1], op=OP.mult),
            reads=[arcb, gcellb], writes=[gcellb])
        op(p, "gpsimd", lambda e: e.tensor_scalar(
            gcell[0:1, 2:3], gcell[0:1, 1:2], 1.4, None, op0=OP.mult),
            reads=[gcellb], writes=[gcellb])
        op(p, "gpsimd", lambda e: e.tensor_tensor(
            gcell[0:1, 3:4], gcell[0:1, 2:3], arc[0:1, 0:1], op=OP.subtract),
            reads=[gcellb, arcb], writes=[gcellb])
        op(p, "gpsimd", lambda e: e.tensor_scalar(
            acell[0:1, 0:1], gcell[0:1, 3:4], 0.1, None, op0=OP.add),
            reads=[gcellb], writes=[acellb])
        op(p, "gpsimd", lambda e: e.tensor_scalar(
            bcell[0:1, 0:1], arc[0:1, 0:1], 0.6, None, op0=OP.mult),
            reads=[arcb], writes=[bcellb])
        if debug:
            op(p, "sync", lambda e: e.dma_start(out=dbg_cells[0:1, 0:1], in_=gcell[0:1, 0:1]),
               reads=[gcellb], dma=True)
            op(p, "sync", lambda e: e.dma_start(out=dbg_cells[0:1, 1:2], in_=acell[0:1, 0:1]),
               reads=[acellb], dma=True)
            op(p, "sync", lambda e: e.dma_start(out=dbg_cells[0:1, 2:3], in_=bcell[0:1, 0:1]),
               reads=[bcellb], dma=True)
            op(p, "vector", lambda e: e.tensor_copy(env["nrm"][0][:, 0:8], colsum[:, :]),
               reads=[colsumb], writes=[env["nrm_b"][0]])
            op(p, "vector", lambda e: e.tensor_copy(env["nrm"][1][:, 0:8], hcol[:, :]),
               reads=[hcolb], writes=[env["nrm_b"][1]])
            op(p, "sync", lambda e: e.dma_start(out=dbg_hcol[:, :], in_=env["nrm"][1][:, 0:8]),
               reads=[env["nrm_b"][1]], dma=True)
            op(p, "sync", lambda e: e.dma_start(out=dbg_mean[:, :], in_=env["nrm"][0][:, 0:8]),
               reads=[env["nrm_b"][0]], dma=True)

        for j in range(NOJ):
            def ydst(c, j=j):
                a = (c * NOJ + j) * 512
                return state[:, a:a + 512], None
            emit_ln(p, env, xo, OWN, j * 512, ydst)
            state_bj[j].did_write(("dve", p.sems["dve"][1]))

        def coupling_net(cyc, j):
            jp = j % 2
            trow, trowb = trow2[jp], trowb2[jp]
            nrow, nrowb = nrow2[jp], nrowb2[jp]
            for c in range(NCH):
                a = (c * NOJ + j) * 512
                op(p, "vector", lambda e, a=a, c=c: e.tensor_copy(
                    sbf[:, c * 512:(c + 1) * 512], state[:, a:a + 512]),
                    reads=[state_bj[j]], writes=([sbf_b] if c == 0 else ()))
            sbf_b.did_write(("dve", p.sems["dve"][1]))
            for o in range(NCH):
                pt = P_A if o % 2 == 0 else P_B
                ptb = (pab if o % 2 == 0 else pbb)[(o // 2) % 2]
                hf = ((o // 2) % 2) * 512
                for c in range(NCH):
                    op(p, "tensor", lambda e, pt=pt, hf=hf, o=o, c=c: e.matmul(
                        pt[:, hf:hf + 512],
                        w1b[:, c * D + o * 128: c * D + (o + 1) * 128],
                        sbf[:, c * 512:(c + 1) * 512],
                        start=(c == 0), stop=(c == NCH - 1)),
                        reads=[w1bb, sbf_b], writes=([ptb] if c == 0 else ()))
                ptb.did_write(("pe", p.sems["pe"][1]))
                op(p, "scalar", lambda e, pt=pt, hf=hf, o=o: e.activation(
                    h1[:, o * 512:(o + 1) * 512], pt[:, hf:hf + 512],
                    AF.Gelu, bias=b1c[:, o:o + 1], scale=1.0),
                    reads=[ptb, b1cb], writes=([h1b] if o == 0 else ()))
            h1b.did_write(("act", p.sems["act"][1]))
            for o in range(4):
                pt = P_A if o % 2 == 0 else P_B
                ptb = (pab if o % 2 == 0 else pbb)[0]
                for c in range(NCH):
                    op(p, "tensor", lambda e, pt=pt, o=o, c=c: e.matmul(
                        pt[:, 0:512],
                        w2b[:, c * 512 + o * 128: c * 512 + (o + 1) * 128],
                        h1[:, c * 512:(c + 1) * 512],
                        start=(c == 0), stop=(c == NCH - 1)),
                        reads=[w2bb, h1b], writes=([ptb] if c == 0 else ()))
                ptb.did_write(("pe", p.sems["pe"][1]))
                op(p, "scalar", lambda e, pt=pt, o=o: e.activation(
                    h2[:, o * 512:(o + 1) * 512], pt[:, 0:512],
                    AF.Gelu, bias=b2c[:, o:o + 1], scale=1.0),
                    reads=[ptb, b2cb], writes=([h2b] if o == 0 else ()))
            h2b.did_write(("act", p.sems["act"][1]))
            for o in range(2):
                pt = P_A if o % 2 == 0 else P_B
                ptb = (pab if o % 2 == 0 else pbb)[0]
                for c in range(4):
                    op(p, "tensor", lambda e, pt=pt, o=o, c=c: e.matmul(
                        pt[:, 0:512],
                        w3b[:, c * 256 + o * 128: c * 256 + (o + 1) * 128],
                        h2[:, c * 512:(c + 1) * 512],
                        start=(c == 0), stop=(c == 3)),
                        reads=[w3bb, h2b], writes=([ptb] if c == 0 else ()))
                ptb.did_write(("pe", p.sems["pe"][1]))
                op(p, "scalar", lambda e, pt=pt, o=o: e.activation(
                    h3[:, o * 512:(o + 1) * 512], pt[:, 0:512],
                    AF.Gelu, bias=b3c[:, o:o + 1], scale=1.0),
                    reads=[ptb, b3cb], writes=([h3b] if o == 0 else ()))
            h3b.did_write(("act", p.sems["act"][1]))
            for c in range(2):
                op(p, "tensor", lambda e, c=c: e.matmul(
                    P_D[0:1, 0:512], w4b[:, c:c + 1],
                    h3[:, c * 512:(c + 1) * 512],
                    start=(c == 0), stop=(c == 1)),
                    reads=[w4bb, h3b], writes=([pdb] if c == 0 else ()))
            pdb.did_write(("pe", p.sems["pe"][1]))
            op(p, "scalar", lambda e, trow=trow: e.activation(
                trow[0:1, :], P_D[0:1, 0:512], AF.Sigmoid,
                bias=b4c[0:1, 0:1], scale=1.0),
                reads=[pdb, b4cb], writes=[trowb])
            op(p, "vector", lambda e, trow=trow: e.tensor_scalar(
                trow[0:1, :], trow[0:1, :], bcell[0:1, 0:1], acell[0:1, 0:1],
                op0=OP.mult, op1=OP.add),
                reads=[trowb, bcellb, acellb], writes=[trowb])
            op(p, "vector", lambda e, trow=trow, nrow=nrow: e.tensor_scalar(
                nrow[0:1, :], trow[0:1, :], -1.0, None, op0=OP.mult),
                reads=[trowb], writes=[nrowb])
            op(p, "sync", lambda e, trow=trow, jp=jp: e.dma_start(
                out=B_cp2[jp][:, :], in_=bcast_ap(trow, 0, 128, 512)),
                reads=[trowb], writes=[B_cpb2[jp]], dma=True, dsem="gd2")
            op(p, "sync", lambda e, nrow=nrow, jp=jp: e.dma_start(
                out=B_cn2[jp][:, :], in_=bcast_ap(nrow, 0, 128, 512)),
                reads=[nrowb], writes=[B_cnb2[jp]], dma=True, dsem="gd2")

        def twist(cyc, j):
            jp = j % 2
            B_cp, B_cpb = B_cp2[jp], B_cpb2[jp]
            B_cn, B_cnb = B_cn2[jp], B_cnb2[jp]
            for m in range(4):
                eng = "vector"
                tp_, tpb_ = (tpr, tprb) if m < 2 else (tprg, tprgb)
                t_, tb_ = (t2, t2b) if m < 2 else (t2g, t2gb)
                sA, sAb = (B_cp, B_cpb) if m < 2 else (B_cn, B_cnb)
                sB, sBb = (B_cn, B_cnb) if m < 2 else (B_cp, B_cpb)
                a0 = (m * NOJ + j) * 512
                a4 = ((m + 4) * NOJ + j) * 512
                op(p, eng, lambda e, a0=a0, tp_=tp_: e.tensor_copy(
                    tp_[:, :], state[:, a0:a0 + 512]),
                    reads=[state_bj[j]], writes=[tpb_])
                op(p, eng, lambda e, a4=a4, sA=sA, t_=t_: e.tensor_tensor(
                    t_[:, :], state[:, a4:a4 + 512], sA[:, :], op=OP.mult),
                    reads=[state_bj[j], sAb], writes=[tb_])
                op(p, eng, lambda e, a0=a0, t_=t_: e.tensor_tensor(
                    state[:, a0:a0 + 512], state[:, a0:a0 + 512], t_[:, :],
                    op=OP.add), reads=[tb_, state_bj[j]], writes=())
                op(p, eng, lambda e, sB=sB, tp_=tp_, t_=t_: e.tensor_tensor(
                    t_[:, :], tp_[:, :], sB[:, :], op=OP.mult),
                    reads=[tpb_, sBb], writes=[tb_])
                op(p, eng, lambda e, a4=a4, t_=t_: e.tensor_tensor(
                    state[:, a4:a4 + 512], state[:, a4:a4 + 512], t_[:, :],
                    op=OP.add), reads=[tb_, state_bj[j]], writes=())
            state_bj[j].did_write(("dve", p.sems["dve"][1]))

        for cyc in range(ncycles):
            for j in range(NOJ):
                coupling_net(cyc, j)
                if j > 0:
                    twist(cyc, j - 1)
            twist(cyc, NOJ - 1)

        for j in range(NOJ):
            for c in range(NCH):
                i = c % 2
                u = c % 2
                a = (c * NOJ + j) * 512
                op(p, "sync", lambda e, i=i, c=c, j=j: e.dma_start(
                    out=env["xs"][i][:, :], in_=chunk_ap(xo, OWN, c, j * 512, 512)),
                    writes=[env["xs_b"][i]], dma=True, dsem=f"sd{i}")
                op(p, "vector", lambda e, i=i, u=u, a=a: e.tensor_tensor(
                    us[u][:, :], env["xs"][i][:, :], state[:, a:a + 512],
                    op=OP.add),
                    reads=[env["xs_b"][i], state_bj[j]], writes=[usb[u]])
                op(p, "sync", lambda e, u=u, c=c, j=j: e.dma_start(
                    out=bass.AP(upd, c * 128 * OWN + j * 512, [[OWN, 128], [1, 512]]),
                    in_=us[u][:, :]),
                    reads=[usb[u]], dma=True)
        p._wait("sync", "sdma", p.sems["sdma"][1])

        with nc.Block("ffn") as blk:
            p.finalize(blk)
    return nc


def _prep_bias_pc(b, nch):
    return np.ascontiguousarray(b.reshape(nch, 128).T).astype(np.float32)


def _own_cols(r):
    idx = []
    for j in range(NOJ):
        g = 2 * j + r
        idx.append(np.arange(g * 512, (g + 1) * 512))
    return np.concatenate(idx)


def run_ffn_launch(nc_ffn, xt_state, weights, want_res=False):
    in_maps = []
    for core in range(8):
        b, r = core // 2, core % 2
        cols = _own_cols(r)
        m = {
            "xt": xt_state[b],
            "xo": np.ascontiguousarray(xt_state[b][:, cols]),
            "w1": weights["cn_w1"], "w2": weights["cn_w2"],
            "w3": weights["cn_w3"], "w4": weights["cn_w4"],
            "gw1": weights["gc_w1"], "gw2": weights["gc_w2"],
            "gw3": weights["gc_w3"],
            "b1": _prep_bias_pc(weights["cn_b1"], 8),
            "b2": _prep_bias_pc(weights["cn_b2"], 4),
            "b3": _prep_bias_pc(weights["cn_b3"], 2),
            "b4": weights["cn_b4"].reshape(1, 1).astype(np.float32),
            "gb1": _prep_bias_pc(weights["gc_b1"], 4),
            "gb2": _prep_bias_pc(weights["gc_b2"], 2),
            "gb3": weights["gc_b3"].reshape(1, 1).astype(np.float32),
            "ar": weights["adaptive_range"].reshape(1, 1).astype(np.float32),
        }
        in_maps.append(m)
    res = run_bass_kernel_spmd(nc_ffn, in_maps, core_ids=list(range(8)))
    out = [x.copy() for x in xt_state]
    for core in range(8):
        b, r = core // 2, core % 2
        u = res.results[core]["upd"]
        for j in range(NOJ):
            g = 2 * j + r
            out[b][:, g * 512:(g + 1) * 512] = u[:, j * 512:(j + 1) * 512]
    if want_res:
        return out, res
    return out




def build_attn_a():
    nc = bass.Bass("TRN2", target_bir_lowering=False)
    xt = nc.dram_tensor("xt", [D, S], F32, kind="ExternalInput")
    xo = nc.dram_tensor("xo", [D, OWN], F32, kind="ExternalInput")
    masks = nc.dram_tensor("masks", [128, 8 * 512], BF16, kind="ExternalInput")
    wq = nc.dram_tensor("wq", [D, D], F32, kind="ExternalInput")
    wk = nc.dram_tensor("wk", [D, D], F32, kind="ExternalInput")
    wv = nc.dram_tensor("wv", [D, D], F32, kind="ExternalInput")
    wo = nc.dram_tensor("wo", [D, D], F32, kind="ExternalInput")
    bq = nc.dram_tensor("bq", [128, 8], F32, kind="ExternalInput")
    bk = nc.dram_tensor("bk", [128, 8], F32, kind="ExternalInput")
    bv = nc.dram_tensor("bv", [128, 8], F32, kind="ExternalInput")
    bo = nc.dram_tensor("bo", [128, 8], F32, kind="ExternalInput")
    upd = nc.dram_tensor("upd", [D, OWN], F32, kind="ExternalOutput")
    K_dram = nc.dram_tensor("K_dram", [NCH, 128, S], BF16, kind="Internal")

    with ExitStack() as ctx:
        sb = lambda name, shp, dt: ctx.enter_context(nc.sbuf_tensor(name, shp, dt))
        p = Prog(nc, ctx)
        env = {}
        make_ln_env(p, ctx, nc, env, nxs=2, nxfull=1)

        kslot = [sb(f"kslot{i}", [128, 512], BF16) for i in range(2)]
        kslotb = [Buf() for _ in range(2)]
        Kst = [sb(f"Kst{i}", [128, S], BF16) for i in range(2)]
        Kstb = [Buf() for _ in range(2)]
        Kdb = Buf()
        V_sb, Vb = sb("V_sb", [128, 32 * D], BF16), Buf()
        qbuf, Qb = sb("qbuf", [128, NCH * 512], BF16), Buf()
        ybuf2 = [sb(f"ybuf{i}", [128, NCH * 512], BF16) for i in range(2)]
        Yb2 = [Buf() for i in range(2)]
        wA, wAb = sb("wA", [128, NCH * D], BF16), Buf()
        wB, wBb = sb("wB", [128, NCH * D], BF16), Buf()
        msk, mskb = sb("msk", [128, 8 * 512], BF16), Buf()
        Ebuf = sb("Ebuf", [128, 3 * 1024], BF16)
        Eb = [Buf() for _ in range(3)]
        Z_sb, Zb = sb("Z_sb", [128, NCH * 512], BF16), Buf()
        Brec, Brecb = sb("Brec", [128, 512], F32), Buf()
        Brec2, Brec2b = sb("Brec2", [128, 512], F32), Buf()
        rrow, rrowb = sb("rrow", [1, 512], F32), Buf()
        rrow2, rrow2b = sb("rrow2", [1, 512], F32), Buf()
        bq_c, bq_cb = sb("bq_c", [128, 8], F32), Buf()
        bk_c, bk_cb = sb("bk_c", [128, 8], F32), Buf()
        bo_c, bo_cb = sb("bo_c", [128, 8], F32), Buf()
        bv_bf, bv_bfb = sb("bv_bf", [128, 8], BF16), Buf()
        btot, btotb = sb("btot", [128, 8], F32), Buf()
        us = [env["nrm"][i] for i in range(2)]
        usb = [env["nrm_b"][i] for i in range(2)]

        P_A = ctx.enter_context(nc.psum_tensor("P_A", [128, 1024], F32))
        P_B = ctx.enter_context(nc.psum_tensor("P_B", [128, 1024], F32))
        P_C = ctx.enter_context(nc.psum_tensor("P_C", [128, 512], F32))
        P_D = ctx.enter_context(nc.psum_tensor("P_D", [2, 1024], F32))
        P_E = ctx.enter_context(nc.psum_tensor("P_E", [128, 512], F32))
        pab, pbb, pcb, peb = Buf(), Buf(), Buf(), Buf()
        env["p_stat"], env["p_stat_b"] = P_D, Buf()
        pdb = env["p_stat_b"]
        rot = [(P_A, 0, pab), (P_A, 512, pab), (P_B, 0, pbb), (P_B, 512, pbb)]
        rotb = [Buf() for _ in range(4)]

        op(p, "gpsimd", lambda e: e.dma_start(out=wA[:, :], in_=w_ap(wk, D, D)),
           writes=[wAb], dma=True, dsem="gd1")
        op(p, "gpsimd", lambda e: e.dma_start(out=wB[:, :], in_=w_ap(wv, D, D)),
           writes=[wBb], dma=True, dsem="gd1")
        op(p, "gpsimd", lambda e: e.dma_start(out=bv_bf[:, :], in_=bv[:, :]),
           writes=[bv_bfb], dma=True, dsem="gd1")
        op(p, "sync", lambda e: e.dma_start(out=msk[:, :], in_=masks[:, :]),
           writes=[mskb], dma=True)
        for (dst, dstb, src) in ((bq_c, bq_cb, bq), (bk_c, bk_cb, bk),
                                 (bo_c, bo_cb, bo)):
            op(p, "sync", lambda e, dst=dst, src=src:
               e.dma_start(out=dst[:, :], in_=src[:, :]), writes=[dstb], dma=True)

        for tb in range(NTB):
            ybuf, Yb = ybuf2[tb % 2], Yb2[tb % 2]
            def ydst(c, ybuf=ybuf, Yb=Yb):
                return ybuf[:, c * 512:(c + 1) * 512], (Yb if c == 0 else None)
            emit_ln(p, env, xt, S, tb * 512, ydst)
            Yb.did_write(("dve", p.sems["dve"][1]))
            for o in range(NCH):
                pt, off, ptb = rot[o % 4][0], rot[o % 4][1], rotb[o % 4]
                for c in range(NCH):
                    op(p, "tensor", lambda e, pt=pt, off=off, o=o, c=c, ybuf=ybuf:
                       e.matmul(
                        pt[:, off:off + 512],
                        wA[:, c * D + o * 128: c * D + (o + 1) * 128],
                        ybuf[:, c * 512:(c + 1) * 512],
                        start=(c == 0), stop=(c == NCH - 1)),
                        reads=[wAb, Yb], writes=([ptb] if c == 0 else ()))
                ptb.did_write(("pe", p.sems["pe"][1]))
                ks = o % 2
                op(p, "scalar", lambda e, pt=pt, off=off, o=o, ks=ks: e.activation(
                    kslot[ks][:, :], pt[:, off:off + 512], AF.Identity,
                    bias=bk_c[:, o:o + 1], scale=1.0),
                    reads=[ptb, bk_cb], writes=[kslotb[ks]])
                op(p, "sync", lambda e, o=o, tb=tb, ks=ks: e.dma_start(
                    out=bass.AP(K_dram, o * 128 * S + tb * 512,
                                [[S, 128], [1, 512]]),
                    in_=kslot[ks][:, :]),
                    reads=[kslotb[ks]], writes=[Kdb], dma=True, dsem="sd5")
            for tt in range(4):
                for dblk in range(2):
                    k4 = (tt * 2 + dblk) % 4
                    pt, off, ptb = rot[k4][0], rot[k4][1], rotb[k4]
                    for c in range(NCH):
                        op(p, "tensor", lambda e, pt=pt, off=off, tt=tt, dblk=dblk,
                           c=c, ybuf=ybuf: e.matmul(
                            pt[:, off:off + 512],
                            ybuf[:, c * 512 + tt * 128: c * 512 + (tt + 1) * 128],
                            wB[:, c * D + dblk * 512: c * D + (dblk + 1) * 512],
                            start=(c == 0), stop=(c == NCH - 1)),
                            reads=[wBb, Yb], writes=([ptb] if c == 0 else ()))
                    ptb.did_write(("pe", p.sems["pe"][1]))
                    op(p, "scalar", lambda e, pt=pt, off=off, tt=tt, dblk=dblk, tb=tb:
                       e.activation(
                        V_sb[:, (tb * 4 + tt) * D + dblk * 512:
                             (tb * 4 + tt) * D + (dblk + 1) * 512],
                        pt[:, off:off + 512], AF.Copy, bias=0.0, scale=1.0),
                        reads=[ptb], writes=[Vb])

        op(p, "gpsimd", lambda e: e.dma_start(out=wA[:, :], in_=w_ap(wq, D, D)),
           writes=[wAb], dma=True, dsem="gd1")
        op(p, "gpsimd", lambda e: e.dma_start(out=wB[:, :], in_=w_ap(wo, D, D)),
           writes=[wBb], dma=True, dsem="gd1")
        for o in range(NCH):
            for c in range(NCH):
                op(p, "tensor", lambda e, o=o, c=c: e.matmul(
                    P_E[:, o:o + 1],
                    wB[:, c * D + o * 128: c * D + (o + 1) * 128],
                    bv_bf[:, c:c + 1], start=(c == 0), stop=(c == NCH - 1)),
                    reads=[wBb, bv_bfb],
                    writes=([peb] if (o == 0 and c == 0) else ()))
        peb.did_write(("pe", p.sems["pe"][1]))
        op(p, "vector", lambda e: e.tensor_tensor(
            btot[:, :], P_E[:, 0:8], bo_c[:, :], op=OP.add),
            reads=[peb, bo_cb], writes=[btotb])

        for j in range(NOJ):
            ybuf, Yb = ybuf2[j % 2], Yb2[j % 2]
            def ydst(c, ybuf=ybuf, Yb=Yb):
                return ybuf[:, c * 512:(c + 1) * 512], (Yb if c == 0 else None)
            emit_ln(p, env, xo, OWN, j * 512, ydst)
            Yb.did_write(("dve", p.sems["dve"][1]))
            for o in range(NCH):
                pt, off, ptb = rot[o % 4][0], rot[o % 4][1], rotb[o % 4]
                for c in range(NCH):
                    op(p, "tensor", lambda e, pt=pt, off=off, o=o, c=c, ybuf=ybuf:
                       e.matmul(
                        pt[:, off:off + 512],
                        wA[:, c * D + o * 128: c * D + (o + 1) * 128],
                        ybuf[:, c * 512:(c + 1) * 512],
                        start=(c == 0), stop=(c == NCH - 1)),
                        reads=[wAb, Yb], writes=([ptb] if c == 0 else ()))
                ptb.did_write(("pe", p.sems["pe"][1]))
                op(p, "scalar", lambda e, pt=pt, off=off, o=o: e.activation(
                    qbuf[:, o * 512:(o + 1) * 512], pt[:, off:off + 512],
                    AF.Identity, bias=bq_c[:, o:o + 1], scale=1.0),
                    reads=[ptb, bq_cb], writes=[Qb])
            Qb.did_write(("act", p.sems["act"][1]))
            nst = 8 * j + 8
            op(p, "sync", lambda e, nst=nst: e.dma_start(
                out=Kst[0][:, 0:nst * 128],
                in_=bass.AP(K_dram, 0, [[S, 128], [1, nst * 128]])),
                reads=[Kdb], writes=[Kstb[0]], dma=True, dsem="sd3")
            for hp in range(NCH):
                kb = hp % 2
                if hp + 1 < NCH:
                    op(p, "sync", lambda e, hp=hp, nst=nst: e.dma_start(
                        out=Kst[(hp + 1) % 2][:, 0:nst * 128],
                        in_=bass.AP(K_dram, (hp + 1) * 128 * S,
                                    [[S, 128], [1, nst * 128]])),
                        reads=[Kdb], writes=[Kstb[(hp + 1) % 2]], dma=True,
                        dsem=f"sd{3 + (hp + 1) % 2}")
                pz, pzb = (P_C, pcb) if hp % 2 == 0 else (P_E, peb)

                def sc_group(st, kb=kb, hp=hp, j=j):
                    psc, pscb = (P_A, pab) if st % 2 == 0 else (P_B, pbb)
                    op(p, "tensor", lambda e, psc=psc, kb=kb, hp=hp, st=st: e.matmul(
                        psc[:, 0:512],
                        Kst[kb][0:64, st * 128:(st + 1) * 128],
                        qbuf[0:64, hp * 512:(hp + 1) * 512],
                        start=True, stop=True, tile_position=(0, 0)),
                        reads=[Kstb[kb], Qb], writes=[pscb])
                    op(p, "tensor", lambda e, psc=psc, kb=kb, hp=hp, st=st: e.matmul(
                        psc[:, 512:1024],
                        Kst[kb][64:128, st * 128:(st + 1) * 128],
                        qbuf[64:128, hp * 512:(hp + 1) * 512],
                        start=True, stop=True, tile_position=(64, 0)),
                        reads=[Kstb[kb], Qb])
                    pscb.did_write(("pe", p.sems["pe"][1]))
                    if st >= 8 * j:
                        m = st - 8 * j
                        op(p, "vector", lambda e, psc=psc, m=m: e.tensor_tensor(
                            psc[:, 0:512], psc[:, 0:512],
                            msk[:, m * 512:(m + 1) * 512], op=OP.add),
                            reads=[mskb, pscb], writes=())
                        op(p, "vector", lambda e, psc=psc, m=m: e.tensor_tensor(
                            psc[:, 512:1024], psc[:, 512:1024],
                            msk[:, m * 512:(m + 1) * 512], op=OP.add),
                            reads=[mskb], writes=())
                        pscb.did_write(("dve", p.sems["dve"][1]))

                def consume(st, hp=hp, nst=nst, pz=pz, pzb=pzb):
                    psc, pscb = (P_A, pab) if st % 2 == 0 else (P_B, pbb)
                    es = st % 3
                    op(p, "scalar", lambda e, psc=psc, es=es: e.activation(
                        Ebuf[:, es * 1024:(es + 1) * 1024], psc[:, 0:1024],
                        AF.Exp, bias=0.0, scale=0.125),
                        reads=[pscb], writes=[Eb[es]])
                    op(p, "tensor", lambda e, es=es, st=st, nst=nst: e.matmul(
                        P_D[0:1, 0:512], env["ones"][:, 0:1],
                        Ebuf[:, es * 1024: es * 1024 + 512],
                        start=(st == 0), stop=(st == nst - 1)),
                        reads=[Eb[es], env["ones_b"]],
                        writes=([pdb] if st == 0 else ()))
                    op(p, "tensor", lambda e, es=es, st=st, nst=nst: e.matmul(
                        P_D[0:1, 512:1024], env["ones"][:, 0:1],
                        Ebuf[:, es * 1024 + 512: es * 1024 + 1024],
                        start=(st == 0), stop=(st == nst - 1)),
                        reads=[Eb[es], env["ones_b"]])
                    op(p, "tensor", lambda e, es=es, st=st, hp=hp, nst=nst, pz=pz:
                       e.matmul(
                        pz[0:64, 0:512],
                        V_sb[:, st * D + (2 * hp) * 64: st * D + (2 * hp) * 64 + 64],
                        Ebuf[:, es * 1024: es * 1024 + 512],
                        start=(st == 0), stop=(st == nst - 1),
                        tile_position=(0, 0), skip_group_check=True),
                        reads=[Eb[es], Vb], writes=([pzb] if st == 0 else ()))
                    op(p, "tensor", lambda e, es=es, st=st, hp=hp, nst=nst, pz=pz:
                       e.matmul(
                        pz[64:128, 0:512],
                        V_sb[:, st * D + (2 * hp + 1) * 64:
                             st * D + (2 * hp + 1) * 64 + 64],
                        Ebuf[:, es * 1024 + 512: es * 1024 + 1024],
                        start=(st == 0), stop=(st == nst - 1),
                        tile_position=(0, 64), skip_group_check=True),
                        reads=[Eb[es], Vb])

                sc_group(0)
                for st in range(nst):
                    if st + 1 < nst:
                        sc_group(st + 1)
                    consume(st)
                pdb.did_write(("pe", p.sems["pe"][1]))
                pzb.did_write(("pe", p.sems["pe"][1]))
                op(p, "vector", lambda e: e.reciprocal(
                    rrow[0:1, :], P_D[0:1, 0:512]), reads=[pdb], writes=[rrowb])
                op(p, "vector", lambda e: e.reciprocal(
                    rrow2[0:1, :], P_D[0:1, 512:1024]), reads=[pdb], writes=[rrow2b])
                op(p, "sync", lambda e: e.dma_start(
                    out=Brec[:, :], in_=bcast_ap(rrow, 0, 128, 512)),
                    reads=[rrowb], writes=[Brecb], dma=True, dsem="gd2")
                op(p, "sync", lambda e: e.dma_start(
                    out=Brec2[:, :], in_=bcast_ap(rrow2, 0, 128, 512)),
                    reads=[rrow2b], writes=[Brec2b], dma=True, dsem="gd2")
                op(p, "vector", lambda e, hp=hp, pz=pz: e.tensor_tensor(
                    Z_sb[0:64, hp * 512:(hp + 1) * 512], pz[0:64, 0:512],
                    Brec[0:64, :], op=OP.mult),
                    reads=[pzb, Brecb], writes=[Zb])
                op(p, "vector", lambda e, hp=hp, pz=pz: e.tensor_tensor(
                    Z_sb[64:128, hp * 512:(hp + 1) * 512], pz[64:128, 0:512],
                    Brec2[64:128, :], op=OP.mult),
                    reads=[pzb, Brec2b], writes=())
                Zb.did_write(("dve", p.sems["dve"][1]))
            for o in range(NCH):
                for hc in range(NCH):
                    op(p, "tensor", lambda e, o=o, hc=hc: e.matmul(
                        P_A[:, 0:512],
                        wB[:, hc * D + o * 128: hc * D + (o + 1) * 128],
                        Z_sb[:, hc * 512:(hc + 1) * 512],
                        start=(hc == 0), stop=(hc == NCH - 1)),
                        reads=[wBb, Zb], writes=([pab] if hc == 0 else ()))
                pab.did_write(("pe", p.sems["pe"][1]))
                i = o % 2
                u = o % 2
                op(p, "sync", lambda e, i=i, o=o, j=j: e.dma_start(
                    out=env["xs"][i][:, :], in_=chunk_ap(xo, OWN, o, j * 512, 512)),
                    writes=[env["xs_b"][i]], dma=True, dsem=f"sd{i}")
                op(p, "vector", lambda e, i=i, u=u, o=o: e.scalar_tensor_tensor(
                    us[u][:, :], P_A[:, 0:512], btot[:, o:o + 1],
                    env["xs"][i][:, :], op0=OP.add, op1=OP.add),
                    reads=[pab, btotb, env["xs_b"][i]], writes=[usb[u]])
                op(p, "sync", lambda e, u=u, o=o, j=j: e.dma_start(
                    out=bass.AP(upd, o * 128 * OWN + j * 512, [[OWN, 128], [1, 512]]),
                    in_=us[u][:, :]),
                    reads=[usb[u]], dma=True)
        p._wait("sync", "sdma", p.sems["sdma"][1])

        with nc.Block("attn_a") as blk:
            p.finalize(blk)
    return nc


def make_masks(r):
    out = np.zeros((128, 8 * 512), np.float32)
    i = np.arange(128)[:, None]
    jc = np.arange(512)[None, :]
    for m in range(8):
        if r == 0:
            if m < 4:
                pat = np.where(128 * m + i <= jc, 0.0, NEG)
            else:
                pat = np.full((128, 512), NEG, np.float32)
        else:
            if m < 4:
                pat = np.zeros((128, 512), np.float32)
            else:
                pat = np.where(128 * (m - 4) + i <= jc, 0.0, NEG)
        out[:, m * 512:(m + 1) * 512] = pat
    return out.astype(BFD)


def run_attn_a_launch(nc_a, xt_state, weights):
    in_maps = []
    for core in range(8):
        b, r = core // 2, core % 2
        cols = _own_cols(r)
        m = {
            "xt": xt_state[b],
            "xo": np.ascontiguousarray(xt_state[b][:, cols]),
            "masks": make_masks(r),
            "wq": weights["Wq"], "wk": weights["Wk"],
            "wv": weights["Wv"], "wo": weights["Wo"],
            "bq": _prep_bias_pc(weights["bq"], 8),
            "bk": _prep_bias_pc(weights["bk"], 8),
            "bv": _prep_bias_pc(weights["bv"], 8),
            "bo": _prep_bias_pc(weights["bo"], 8),
        }
        in_maps.append(m)
    res = run_bass_kernel_spmd(nc_a, in_maps, core_ids=list(range(8)))
    out = [x.copy() for x in xt_state]
    for core in range(8):
        b, r = core // 2, core % 2
        u = res.results[core]["upd"]
        for j in range(NOJ):
            g = 2 * j + r
            out[b][:, g * 512:(g + 1) * 512] = u[:, j * 512:(j + 1) * 512]
    return out




def build_attn_b():
    nc = bass.Bass("TRN2", target_bir_lowering=False)
    xt = nc.dram_tensor("xt", [D, S], F32, kind="ExternalInput")
    xo = nc.dram_tensor("xo", [D, OWN], F32, kind="ExternalInput")
    w = nc.dram_tensor("w", [H * D, D], F32, kind="ExternalInput")
    upd = nc.dram_tensor("upd", [D, OWN], F32, kind="ExternalOutput")
    wbf = nc.dram_tensor("wbf", [H, 128, NCH * D], BF16, kind="Internal")
    TBB = 256
    NJ2 = OWN // TBB

    with ExitStack() as ctx:
        sb = lambda name, shp, dt: ctx.enter_context(nc.sbuf_tensor(name, shp, dt))
        p = Prog(nc, ctx)
        env = {}
        make_ln_env(p, ctx, nc, env, nxs=2)

        Y_sb, Yb = sb("Y_sb", [128, NCH * S], BF16), Buf()
        Ytm, Ytmb = sb("Ytm", [128, 32 * D], BF16), Buf()
        yq, yqb = sb("yq", [128, NCH * TBB], BF16), Buf()
        Ebuf = sb("Ebuf", [128, 6 * TBB], BF16)
        Eb = [Buf() for _ in range(3)]
        G_sb = [sb(f"G_sb{i}", [128, NCH * TBB], BF16) for i in range(2)]
        Gb = [Buf() for _ in range(2)]
        outacc, oaccb = sb("outacc", [128, NCH * TBB], F32), Buf()
        wh = [sb(f"wh{i}", [128, NCH * D], BF16) for i in range(2)]
        whb = [Buf() for _ in range(2)]
        wdb = Buf()
        Brec, Brecb = sb("Brec", [128, TBB], F32), Buf()
        rrow, rrowb = sb("rrow", [1, TBB], F32), Buf()
        ident, identb = sb("ident", [128, 128], BF16), Buf()
        onesb_t, onesb_tb = sb("ones_bf", [128, 128], BF16), Buf()
        us = [env["nrm"][i] for i in range(2)]
        usb = [env["nrm_b"][i] for i in range(2)]

        P_A = ctx.enter_context(nc.psum_tensor("P_A", [128, 1024], F32))
        P_B = ctx.enter_context(nc.psum_tensor("P_B", [128, 1024], F32))
        P_C = ctx.enter_context(nc.psum_tensor("P_C", [128, 512], F32))
        P_E = ctx.enter_context(nc.psum_tensor("P_E", [128, 512], F32))
        P_F = ctx.enter_context(nc.psum_tensor("P_F", [128, 512], F32))
        P_D = ctx.enter_context(nc.psum_tensor("P_D", [2, 512], F32))
        pab, pbb, pcb, peb, pfb = Buf(), Buf(), Buf(), Buf(), Buf()
        env["p_stat"], env["p_stat_b"] = P_D, Buf()
        pdb = env["p_stat_b"]
        env["p_stat2"], env["p_stat2_b"] = P_F, pfb

        ms = p.op("gpsimd", lambda e: e.memset(onesb_t[:, :], 1.0))
        onesb_tb.did_write(ms)
        op(p, "gpsimd", lambda e: e.affine_select(
            ident[:, :], onesb_t[:, :], [[1, 128]], OP.is_equal, 0.0,
            base=0, channel_multiplier=-1),
            reads=[onesb_tb], writes=[identb])

        for h in range(H):
            k = h % 2
            op(p, "gpsimd", lambda e, h=h, k=k: e.dma_start(
                out=wh[k][:, :],
                in_=bass.AP(w, h * D * D, [[D, 128], [128 * D, NCH], [1, D]])),
                writes=[whb[k]], dma=True, dsem="gd1")
            op(p, "sync", lambda e, h=h, k=k: e.dma_start(
                out=bass.AP(wbf, h * 128 * NCH * D, [[NCH * D, 128], [1, NCH * D]]),
                in_=wh[k][:, :]),
                reads=[whb[k]], writes=([wdb] if h == 0 else ()), dma=True, dsem="sd5")
        wdb.did_write(("sd5", None))

        for tb in range(NTB):
            def ydst(c, tb=tb):
                return (Y_sb[:, c * S + tb * 512: c * S + (tb + 1) * 512],
                        (Yb if (c == 0 and tb == 0) else None))
            emit_ln(p, env, xt, S, tb * 512, ydst)
        Yb.did_write(("dve", p.sems["dve"][1]))

        trb = [pcb, peb]
        trt = [P_C, P_E]
        for tt in range(32):
            for c in range(NCH):
                k = (tt * NCH + c) % 2
                pv = trt[k][:, 0:64].bitcast(BF16)
                op(p, "tensor", lambda e, pv=pv, tt=tt, c=c: e.transpose(
                    pv, Y_sb[:, c * S + tt * 128: c * S + (tt + 1) * 128],
                    ident[:, :]),
                    reads=[Yb, identb], writes=[trb[k]])
                op(p, "vector", lambda e, pv=pv, tt=tt, c=c: e.tensor_copy(
                    Ytm[:, tt * D + c * 128: tt * D + (c + 1) * 128], pv),
                    reads=[trb[k]], writes=([Ytmb] if (tt == 0 and c == 0) else ()))
        Ytmb.did_write(("dve", p.sems["dve"][1]))

        def load_wh(h):
            k = h % 2
            op(p, "sync", lambda e, h=h, k=k: e.dma_start(
                out=wh[k][:, :],
                in_=bass.AP(wbf, h * 128 * NCH * D,
                            [[NCH * D, 128], [1, NCH * D]])),
                reads=[wdb], writes=[whb[k]], dma=True, dsem=f"sd{3 + k}")

        for j2 in range(NJ2):
            def ydst(c):
                return (yq[:, c * TBB:(c + 1) * TBB], (yqb if c == 0 else None))
            emit_ln(p, env, xo, OWN, j2 * TBB, ydst, n=TBB)
            yqb.did_write(("dve", p.sems["dve"][1]))
            op(p, "gpsimd", lambda e: e.memset(outacc[:, :], 0.0),
               writes=[oaccb])
            if j2 == 0:
                load_wh(0)

            def g_phase(h):
                gb = h % 2
                hr = (h % 2) * 64
                hc = h // 2

                def sc_pair(q):
                    psc = P_C if q % 2 == 0 else P_E
                    pscb2 = pcb if q % 2 == 0 else peb
                    for k2 in range(2):
                        st = 2 * q + k2
                        op(p, "tensor", lambda e, psc=psc, hr=hr, hc=hc, st=st,
                           k2=k2: e.matmul(
                            psc[:, k2 * TBB:(k2 + 1) * TBB],
                            Y_sb[hr:hr + 64,
                                 hc * S + st * 128: hc * S + (st + 1) * 128],
                            yq[hr:hr + 64, hc * TBB:(hc + 1) * TBB],
                            start=True, stop=True,
                            tile_position=(hr, 0)),
                            reads=[Yb, yqb],
                            writes=([pscb2] if k2 == 0 else ()))
                    pscb2.did_write(("pe", p.sems["pe"][1]))

                def consume_pair(q):
                    psc = P_C if q % 2 == 0 else P_E
                    pscb2 = pcb if q % 2 == 0 else peb
                    es = q % 3
                    op(p, "scalar", lambda e, psc=psc, es=es: e.activation(
                        Ebuf[:, es * 512:(es + 1) * 512], psc[:, 0:512],
                        AF.Exp, bias=0.0, scale=0.125),
                        reads=[pscb2], writes=[Eb[es]])
                    for k2 in range(2):
                        st = 2 * q + k2
                        eoff = es * 512 + k2 * TBB
                        op(p, "tensor", lambda e, eoff=eoff, st=st: e.matmul(
                            P_D[0:1, 0:TBB], env["ones"][:, 0:1],
                            Ebuf[:, eoff:eoff + TBB],
                            start=(st == 0), stop=(st == 31)),
                            reads=[Eb[es], env["ones_b"]],
                            writes=([pdb] if st == 0 else ()))
                        for c in range(NCH):
                            pt = P_A if c < 4 else P_B
                            ptb = pab if c < 4 else pbb
                            off = (c % 4) * TBB
                            op(p, "tensor", lambda e, pt=pt, off=off, st=st,
                               c=c, eoff=eoff: e.matmul(
                                pt[:, off:off + TBB],
                                Ytm[:, st * D + c * 128: st * D + (c + 1) * 128],
                                Ebuf[:, eoff:eoff + TBB],
                                start=(st == 0 and c % 2 == 0),
                                stop=(st == 31 and c % 2 == 1)),
                                reads=[Eb[es], Ytmb],
                                writes=([ptb] if (st == 0 and c == 0) else ()))

                sc_pair(0)
                for q in range(16):
                    if q < 15:
                        sc_pair(q + 1)
                    consume_pair(q)
                pab.did_write(("pe", p.sems["pe"][1]))
                pbb.did_write(("pe", p.sems["pe"][1]))
                pdb.did_write(("pe", p.sems["pe"][1]))
                for b2 in range(4):
                    pt = P_A if b2 < 2 else P_B
                    ptb = pab if b2 < 2 else pbb
                    off = (b2 % 2) * 512
                    op(p, "scalar", lambda e, pt=pt, off=off, b2=b2, gb=gb:
                       e.activation(
                        G_sb[gb][:, b2 * 512:(b2 + 1) * 512],
                        pt[:, off:off + 512], AF.Copy, bias=0.0, scale=1.0),
                        reads=[ptb], writes=([Gb[gb]] if b2 == 0 else ()))
                Gb[gb].did_write(("act", p.sems["act"][1]))

            def norm_phase(h):
                gb = h % 2
                op(p, "vector", lambda e: e.reciprocal(
                    rrow[0:1, :], P_D[0:1, 0:TBB]), reads=[pdb], writes=[rrowb])
                op(p, "sync", lambda e: e.dma_start(
                    out=Brec[:, :], in_=bcast_ap(rrow, 0, 128, TBB)),
                    reads=[rrowb], writes=[Brecb], dma=True, dsem="gd2")
                for c in range(NCH):
                    op(p, "vector", lambda e, c=c, gb=gb: e.tensor_tensor(
                        G_sb[gb][:, c * TBB:(c + 1) * TBB],
                        G_sb[gb][:, c * TBB:(c + 1) * TBB], Brec[:, :],
                        op=OP.mult),
                        reads=[Gb[gb], Brecb], writes=())
                Gb[gb].did_write(("dve", p.sems["dve"][1]))

            def w_phase(h):
                gb = h % 2
                for q in range(4):
                    for dc in (2 * q, 2 * q + 1):
                        for c in range(NCH):
                            op(p, "tensor", lambda e, dc=dc, c=c, gb=gb, h=h:
                               e.matmul(
                                P_F[:, (dc % 2) * TBB:((dc % 2) + 1) * TBB],
                                wh[h % 2][:, c * D + dc * 128: c * D + (dc + 1) * 128],
                                G_sb[gb][:, c * TBB:(c + 1) * TBB],
                                start=(dc % 2 == 0 and c == 0),
                                stop=(dc % 2 == 1 and c == NCH - 1)),
                                reads=[whb[h % 2], Gb[gb]],
                                writes=([pfb] if (dc % 2 == 0 and c == 0) else ()))
                    pfb.did_write(("pe", p.sems["pe"][1]))
                    op(p, "vector", lambda e, q=q: e.tensor_tensor(
                        outacc[:, 2 * q * TBB:(2 * q + 2) * TBB],
                        outacc[:, 2 * q * TBB:(2 * q + 2) * TBB],
                        P_F[:, 0:512], op=OP.add),
                        reads=[pfb, oaccb], writes=())
                oaccb.did_write(("dve", p.sems["dve"][1]))

            for h in range(H):
                g_phase(h)
                if h > 0:
                    w_phase(h - 1)
                if not (j2 == NJ2 - 1 and h == H - 1):
                    load_wh((h + 1) % H)
                norm_phase(h)
            w_phase(H - 1)
            for o in range(NCH):
                i = o % 2
                u = o % 2
                op(p, "sync", lambda e, i=i, o=o, j2=j2: e.dma_start(
                    out=env["xs"][i][:, 0:TBB],
                    in_=chunk_ap(xo, OWN, o, j2 * TBB, TBB)),
                    writes=[env["xs_b"][i]], dma=True, dsem=f"sd{i}")
                op(p, "vector", lambda e, i=i, u=u, o=o: e.tensor_tensor(
                    us[u][:, 0:TBB], env["xs"][i][:, 0:TBB],
                    outacc[:, o * TBB:(o + 1) * TBB], op=OP.add),
                    reads=[env["xs_b"][i], oaccb], writes=[usb[u]])
                op(p, "sync", lambda e, u=u, o=o, j2=j2: e.dma_start(
                    out=bass.AP(upd, o * 128 * OWN + j2 * TBB,
                                [[OWN, 128], [1, TBB]]),
                    in_=us[u][:, 0:TBB]),
                    reads=[usb[u]], dma=True)
        p._wait("sync", "sdma", p.sems["sdma"][1])

        with nc.Block("attn_b") as blk:
            p.finalize(blk)
    return nc


def run_attn_b_launch(nc_b, xt_state, weights):
    in_maps = []
    for core in range(8):
        b, r = core // 2, core % 2
        cols = _own_cols(r)
        in_maps.append({
            "xt": xt_state[b],
            "xo": np.ascontiguousarray(xt_state[b][:, cols]),
            "w": weights["weight"],
        })
    res = run_bass_kernel_spmd(nc_b, in_maps, core_ids=list(range(8)))
    out = [x.copy() for x in xt_state]
    for core in range(8):
        b, r = core // 2, core % 2
        u = res.results[core]["upd"]
        for j in range(NOJ):
            g = 2 * j + r
            out[b][:, g * 512:(g + 1) * 512] = u[:, j * 512:(j + 1) * 512]
    return out


_NC_CACHE = {}


def _get_nc(name, builder):
    if name not in _NC_CACHE:
        _NC_CACHE[name] = builder()
    return _NC_CACHE[name]


def kernel(**inputs):
    x = np.asarray(inputs["L0_attn_a_x"], np.float32)
    xt_state = [np.ascontiguousarray(x[b].T) for b in range(BATCH)]

    attn_a_w = {
        "Wq": inputs["L0_attn_a_Wq"], "bq": inputs["L0_attn_a_bq"],
        "Wk": inputs["L0_attn_a_Wk"], "bk": inputs["L0_attn_a_bk"],
        "Wv": inputs["L0_attn_a_Wv"], "bv": inputs["L0_attn_a_bv"],
        "Wo": inputs["L0_attn_a_Wo"], "bo": inputs["L0_attn_a_bo"],
    }
    ffn_w = {k[len("L0_ffn_"):]: np.asarray(v) for k, v in inputs.items()
             if k.startswith("L0_ffn_")}
    attn_b_w = {"weight": inputs["L0_attn_b_weight"]}

    nc_a = _get_nc("attn_a", build_attn_a)
    nc_f = _get_nc("ffn", build_ffn)
    nc_b = _get_nc("attn_b", build_attn_b)

    xt_state = run_attn_a_launch(nc_a, xt_state, attn_a_w)
    xt_state = run_ffn_launch(nc_f, xt_state, ffn_w)
    xt_state = run_attn_b_launch(nc_b, xt_state, attn_b_w)
    xt_state = run_ffn_launch(nc_f, xt_state, ffn_w)

    return np.ascontiguousarray(
        np.stack([xt.T for xt in xt_state])).astype(np.float32)


# version 07
# speedup vs baseline: 1.0383x; 1.0383x over previous
import numpy as np
import ml_dtypes
import concourse.bass as bass
import concourse.mybir as mybir
from concourse.bass_utils import run_bass_kernel_spmd
from contextlib import ExitStack

F32 = mybir.dt.float32
BF16 = mybir.dt.bfloat16
AF = mybir.ActivationFunctionType
OP = mybir.AluOpType
AX = mybir.AxisListType
BFD = ml_dtypes.bfloat16

D = 1024
S = 4096
BATCH = 4
H = 16
DH = 64
NCH = 8
NTB = 8
OWN = 2048
NOJ = 4
NEG = -3.0e38


_ENG_SEM = {"tensor": "pe", "vector": "dve", "scalar": "act",
            "gpsimd": "gp", "sync": "sp"}
SAME_ENGINE_WAITS = True


class Prog:

    ENGS = ("tensor", "vector", "scalar", "gpsimd", "sync")

    def __init__(self, nc, ctx):
        self.nc = nc
        self.ops = {e: [] for e in self.ENGS}
        self.sems = {}
        for s in ("pe", "dve", "act", "gp", "sp", "gdma", "sdma",
                  "sd0", "sd1", "sd2", "sd3", "sd4", "sd5",
                  "gd0", "gd1", "gd2"):
            h = ctx.enter_context(nc.semaphore(s))
            self.sems[s] = [h, 0]
        self.waited = {e: {} for e in self.ENGS}

    def _wait(self, eng, sname, v):
        if v is None:
            v = self.sems[sname][1]
        if v <= 0:
            return
        own = _ENG_SEM[eng]
        if sname == own and not SAME_ENGINE_WAITS:
            return
        w = self.waited[eng]
        if w.get(sname, 0) >= v:
            return
        w[sname] = v
        h = self.sems[sname][0]
        self.ops[eng].append(lambda e, h=h, v=v: e.wait_ge(h, v))

    def waits(self, eng, deps):
        for (sname, v) in deps:
            self._wait(eng, sname, v)

    def op(self, eng, fn, dma=False, dsem=None):
        if dma:
            sname = dsem or ("gdma" if eng == "gpsimd" else "sdma")
            k = 16
            prev = self.sems[sname][1]
            if prev > 0:
                self._wait(eng, sname, prev)
        else:
            sname = _ENG_SEM[eng]
            k = 1
        h = self.sems[sname][0]
        self.ops[eng].append(lambda e, fn=fn, h=h, k=k: fn(e).then_inc(h, k))
        self.sems[sname][1] += k
        if dma:
            return (sname, None)
        return (sname, self.sems[sname][1])

    def finalize(self, block):
        for eng in self.ENGS:
            if self.ops[eng]:
                getattr(block, eng)(
                    lambda e, eng=eng: [f(e) for f in self.ops[eng]])


class Buf:

    def __init__(self):
        self.w = None
        self.w2 = None
        self.r = {}

    def deps_for_read(self):
        d = []
        if self.w:
            d.append(self.w)
        if self.w2:
            d.append(self.w2)
        return d

    def deps_for_write(self):
        d = list(self.r.items())
        if self.w:
            d.append(self.w)
        if self.w2:
            d.append(self.w2)
        return d

    def did_read(self, ms):
        s, v = ms
        cur = self.r.get(s, 0)
        if cur is None:
            return
        if v is None or cur < v:
            self.r[s] = v

    def did_write(self, ms, ms2=None):
        self.w = ms
        self.w2 = ms2
        self.r = {}


def rd(p, eng, fn, *bufs, dma=False):
    for b in bufs:
        p.waits(eng, b.deps_for_read())
    ms = p.op(eng, fn, dma=dma)
    for b in bufs:
        b.did_read(ms)
    return ms


def op(p, eng, fn, reads=(), writes=(), dma=False, dsem=None):
    for b in reads:
        p.waits(eng, b.deps_for_read())
    for b in writes:
        p.waits(eng, b.deps_for_write())
    ms = p.op(eng, fn, dma=dma, dsem=dsem)
    for b in reads:
        b.did_read(ms)
    for b in writes:
        b.did_write(ms)
    return ms


def chunk_ap(t, W, crow, col0, n):
    return bass.AP(t, crow * 128 * W + col0, [[W, 128], [1, n]])


def w_ap(t, din, dout):
    return bass.AP(t, 0, [[dout, 128], [128 * dout, din // 128], [1, dout]])


def bcast_ap(t, offset, npart, n):
    return bass.AP(t, offset, [[1, 1], [0, npart], [1, n]])


def emit_ln(p, env, xsrc, W, col0, ydst_fn, n=512, post_fn=None):
    cs, csb = env["cs"], env["cs_b"]
    c2, c2b = env["cs2"], env["cs2_b"]
    pst, pstb = env["p_stat"], env["p_stat_b"]
    if "p_stat2" in env:
        pst2, pst2b = env["p_stat2"], env["p_stat2_b"]
        pst2ap = pst2[0:1, 0:n]
    else:
        pst2ap = pst[0:1, n:2 * n]
        pst2b = pstb
    ones = env["ones"]
    onesb = env["ones_b"]
    use_xfull = "xfull" in env
    if use_xfull:
        xf = env["xfull"]
        nxf = len(xf)
        xi = env["xfull_i"][0] = (env["xfull_i"][0] + 1) % nxf
        xt_, xtb = xf[xi], env["xfull_b"][xi]
        op(p, "sync", lambda e: e.dma_start(
            out=xt_[:, 0:NCH * n],
            in_=bass.AP(xsrc, col0, [[W, 128], [128 * W, NCH], [1, n]])),
            writes=[xtb], dma=True, dsem=f"sd{xi}")
        xs_of = lambda c: (xt_[:, c * n:(c + 1) * n], xtb)
    else:
        xs, xsb = env["xs"], env["xs_b"]

        def xs_of(c):
            i = c % len(xs)
            op(p, "sync", lambda e, i=i, c=c: e.dma_start(
                out=xs[i][:, 0:n], in_=chunk_ap(xsrc, W, c, col0, n)),
                writes=[xsb[i]], dma=True, dsem=f"sd{i}")
            return xs[i][:, 0:n], xsb[i]
    srcs = []
    for c in range(NCH):
        xap, xb = xs_of(c)
        srcs.append((xap, xb))
        op(p, "vector", lambda e, xap=xap, ci=c % len(cs): e.tensor_copy(
            cs[ci][:, 0:n], xap),
            reads=[xb], writes=[csb[c % len(cs)]])
        op(p, "scalar", lambda e, xap=xap, ci=c % len(c2): e.activation(
            c2[ci][:, 0:n], xap, AF.Square, bias=0.0, scale=1.0),
            reads=[xb], writes=[c2b[c % len(c2)]])
        op(p, "tensor", lambda e, ci=c % len(cs), c=c: e.matmul(
            pst[0:1, 0:n], ones[:, 0:1], cs[ci][:, 0:n],
            start=(c == 0), stop=(c == NCH - 1)),
            reads=[csb[c % len(cs)], onesb],
            writes=([pstb] if c == 0 else ()))
        mst = op(p, "tensor", lambda e, ci=c % len(c2), c=c, pst2ap=pst2ap:
                 e.matmul(
            pst2ap, ones[:, 0:1], c2[ci][:, 0:n],
            start=(c == 0), stop=(c == NCH - 1)),
            reads=[c2b[c % len(c2)], onesb],
            writes=([pst2b] if (c == 0 and pst2b is not pstb) else ()))
    pstb.did_write(mst)
    if pst2b is not pstb:
        pst2b.did_write(mst)
    r1, r1b = env["row1"], env["row1_b"]
    r2, r2b = env["row2"], env["row2_b"]
    r3, r3b = env["row3"], env["row3_b"]
    inv = 1.0 / D
    op(p, "vector", lambda e: e.tensor_scalar(
        r1[0:1, 0:n], pst[0:1, 0:n], inv, None, op0=OP.mult),
        reads=[pstb], writes=[r1b])
    op(p, "vector", lambda e, pst2ap=pst2ap: e.tensor_scalar(
        r2[0:1, 0:n], pst2ap, inv, None, op0=OP.mult),
        reads=[pst2b], writes=[r2b])
    op(p, "vector", lambda e: e.tensor_tensor(
        r3[0:1, 0:n], r1[0:1, 0:n], r1[0:1, 0:n], op=OP.mult),
        reads=[r1b], writes=[r3b])
    op(p, "vector", lambda e: e.tensor_tensor(
        r2[0:1, 0:n], r2[0:1, 0:n], r3[0:1, 0:n], op=OP.subtract),
        reads=[r3b], writes=[r2b])
    op(p, "vector", lambda e: e.tensor_scalar(
        r2[0:1, 0:n], r2[0:1, 0:n], 1e-5, None, op0=OP.add),
        reads=[r2b], writes=[r2b])
    op(p, "scalar", lambda e: e.activation(
        r3[0:1, 0:n], r2[0:1, 0:n], AF.Sqrt, bias=0.0, scale=1.0),
        reads=[r2b], writes=[r3b])
    op(p, "vector", lambda e: e.reciprocal(r3[0:1, 0:n], r3[0:1, 0:n]),
        reads=[r3b], writes=[r3b])
    op(p, "vector", lambda e: e.tensor_tensor(
        r2[0:1, 0:n], r1[0:1, 0:n], r3[0:1, 0:n], op=OP.mult),
        reads=[r1b, r3b], writes=[r2b])
    Br, Brb = env["B_rstd"], env["B_rstd_b"]
    Bm, Bmb = env["B_mur"], env["B_mur_b"]
    op(p, "sync", lambda e: e.dma_start(
        out=Br[:, 0:n], in_=bcast_ap(r3, 0, 128, n)),
        reads=[r3b], writes=[Brb], dma=True, dsem="gd0")
    op(p, "sync", lambda e: e.dma_start(
        out=Bm[:, 0:n], in_=bcast_ap(r2, 0, 128, n)),
        reads=[r2b], writes=[Bmb], dma=True, dsem="gd0")
    if ydst_fn is None:
        return
    nr, nrb = env["nrm"], env["nrm_b"]
    for c in range(NCH):
        if use_xfull:
            xap, xb = srcs[c]
        else:
            xap, xb = xs_of(c)
        ni = c % len(nr)
        op(p, "vector", lambda e, xap=xap, ni=ni: e.tensor_tensor(
            nr[ni][:, 0:n], xap, Br[:, 0:n], op=OP.mult),
            reads=[xb, Brb], writes=[nrb[ni]])
        dst, dstbuf = ydst_fn(c)
        op(p, "vector", lambda e, ni=ni, dst=dst: e.tensor_tensor(
            dst, nr[ni][:, 0:n], Bm[:, 0:n], op=OP.subtract),
            reads=[nrb[ni], Bmb], writes=([dstbuf] if dstbuf else ()))
        if post_fn is not None:
            post_fn(c)


def make_ln_env(p, ctx, nc, env, nxs=3, nxfull=0):
    sb = lambda name, shp, dt: ctx.enter_context(nc.sbuf_tensor(name, shp, dt))
    env["xs"] = [sb(f"ln_xs{i}", [128, 512], F32) for i in range(nxs)]
    env["xs_b"] = [Buf() for _ in range(nxs)]
    if nxfull:
        env["xfull"] = [sb(f"ln_xf{i}", [128, NCH * 512], F32)
                        for i in range(nxfull)]
        env["xfull_b"] = [Buf() for _ in range(nxfull)]
        env["xfull_i"] = [0]
    env["cs"] = [sb(f"ln_cs{i}", [128, 512], BF16) for i in range(2)]
    env["cs_b"] = [Buf() for _ in range(2)]
    env["cs2"] = [sb(f"ln_c2{i}", [128, 512], BF16) for i in range(2)]
    env["cs2_b"] = [Buf() for _ in range(2)]
    env["nrm"] = [sb(f"ln_nr{i}", [128, 512], F32) for i in range(2)]
    env["nrm_b"] = [Buf() for _ in range(2)]
    env["row1"] = sb("ln_r1", [1, 512], F32)
    env["row1_b"] = Buf()
    env["row2"] = sb("ln_r2", [1, 512], F32)
    env["row2_b"] = Buf()
    env["row3"] = sb("ln_r3", [1, 512], F32)
    env["row3_b"] = Buf()
    env["B_rstd"] = sb("ln_Br", [128, 512], F32)
    env["B_rstd_b"] = Buf()
    env["B_mur"] = sb("ln_Bm", [128, 512], F32)
    env["B_mur_b"] = Buf()
    env["ones"] = sb("ones", [128, 2], BF16)
    env["ones_b"] = Buf()
    ms = p.op("gpsimd", lambda e: e.memset(env["ones"][:, :], 1.0))
    env["ones_b"].did_write(ms)


def build_ffn(ncycles=3, debug=False, gf_tbs=NTB):
    nc = bass.Bass("TRN2", target_bir_lowering=False)
    xt = nc.dram_tensor("xt", [D, S], F32, kind="ExternalInput")
    w1 = nc.dram_tensor("w1", [D, D], F32, kind="ExternalInput")
    w2 = nc.dram_tensor("w2", [D, 512], F32, kind="ExternalInput")
    w3 = nc.dram_tensor("w3", [512, 256], F32, kind="ExternalInput")
    w4 = nc.dram_tensor("w4", [256, 1], F32, kind="ExternalInput")
    gw1 = nc.dram_tensor("gw1", [D, 512], F32, kind="ExternalInput")
    gw2 = nc.dram_tensor("gw2", [512, 256], F32, kind="ExternalInput")
    gw3 = nc.dram_tensor("gw3", [256, 1], F32, kind="ExternalInput")
    b1 = nc.dram_tensor("b1", [128, 8], F32, kind="ExternalInput")
    b2 = nc.dram_tensor("b2", [128, 4], F32, kind="ExternalInput")
    b3 = nc.dram_tensor("b3", [128, 2], F32, kind="ExternalInput")
    b4 = nc.dram_tensor("b4", [1, 1], F32, kind="ExternalInput")
    gb1 = nc.dram_tensor("gb1", [128, 4], F32, kind="ExternalInput")
    gb2 = nc.dram_tensor("gb2", [128, 2], F32, kind="ExternalInput")
    gb3 = nc.dram_tensor("gb3", [1, 1], F32, kind="ExternalInput")
    ar = nc.dram_tensor("ar", [1, 1], F32, kind="ExternalInput")
    upd = nc.dram_tensor("upd", [D, OWN], F32, kind="ExternalOutput")
    if debug:
        dbg_cells = nc.dram_tensor("dbg_cells", [1, 8], F32, kind="ExternalOutput")
        dbg_trow = nc.dram_tensor("dbg_trow", [1, 512], F32, kind="ExternalOutput")
        dbg_h1 = nc.dram_tensor("dbg_h1", [128, 512], F32, kind="ExternalOutput")
        dbg_mean = nc.dram_tensor("dbg_mean", [128, 8], F32, kind="ExternalOutput")
        dbg_ycs = nc.dram_tensor("dbg_ycs", [128, 512], F32, kind="ExternalOutput")
        dbg_red = nc.dram_tensor("dbg_red", [128, 1], F32, kind="ExternalOutput")
        dbg_hcol = nc.dram_tensor("dbg_hcol", [128, 8], F32, kind="ExternalOutput")

    with ExitStack() as ctx:
        sb = lambda name, shp, dt: ctx.enter_context(nc.sbuf_tensor(name, shp, dt))
        p = Prog(nc, ctx)
        env = {}
        make_ln_env(p, ctx, nc, env, nxs=2, nxfull=2)

        state = sb("state", [128, NCH * OWN], F32)
        state_bj = [Buf() for _ in range(NOJ)]
        sbf = sb("sbf", [128, NCH * 512], BF16)
        sbf_b = Buf()
        w1b, w1bb = sb("w1b", [128, NCH * D], BF16), Buf()
        w2b, w2bb = sb("w2b", [128, NCH * 512], BF16), Buf()
        w3b, w3bb = sb("w3b", [128, 4 * 256], BF16), Buf()
        w4b, w4bb = sb("w4b", [128, 2], BF16), Buf()
        gw1b, gw1bb = sb("gw1b", [128, NCH * 512], BF16), Buf()
        gw2b, gw2bb = sb("gw2b", [128, 4 * 256], BF16), Buf()
        gw3b, gw3bb = sb("gw3b", [128, 2], BF16), Buf()
        h1, h1b = sb("h1", [128, NCH * 512], BF16), Buf()
        h2, h2b = sb("h2", [128, 4 * 512], BF16), Buf()
        h3, h3b = sb("h3", [128, 2 * 512], BF16), Buf()
        b1c, b1cb = sb("b1c", [128, 8], F32), Buf()
        b2c, b2cb = sb("b2c", [128, 4], F32), Buf()
        b3c, b3cb = sb("b3c", [128, 2], F32), Buf()
        b4c, b4cb = sb("b4c", [1, 1], F32), Buf()
        gb1c, gb1cb = sb("gb1c", [128, 4], F32), Buf()
        gb2c, gb2cb = sb("gb2c", [128, 2], F32), Buf()
        gb3c, gb3cb = sb("gb3c", [1, 1], F32), Buf()
        arc, arcb = sb("arc", [1, 1], F32), Buf()
        ycs, ycsb = sb("ycs", [128, 512], BF16), Buf()
        junk, junkb = sb("junk", [128, 512], BF16), Buf()
        red, redb = sb("red", [128, 1], F32), Buf()
        colsum, colsumb = sb("colsum", [128, 8], F32), Buf()
        mean_bf, mean_bfb = sb("mean_bf", [128, 8], BF16), Buf()
        gcell, gcellb = sb("gcell", [1, 8], F32), Buf()
        acell, acellb = sb("acell", [1, 1], F32), Buf()
        bcell, bcellb = sb("bcell", [1, 1], F32), Buf()
        hcol, hcolb = sb("hcol", [128, 8], BF16), Buf()
        trow2 = [sb(f"trow{i}", [1, 512], F32) for i in range(2)]
        trowb2 = [Buf() for _ in range(2)]
        nrow2 = [sb(f"nrow{i}", [1, 512], F32) for i in range(2)]
        nrowb2 = [Buf() for _ in range(2)]
        B_cp2 = [sb(f"B_cp{i}", [128, 512], F32) for i in range(2)]
        B_cpb2 = [Buf() for _ in range(2)]
        B_cn2 = [sb(f"B_cn{i}", [128, 512], F32) for i in range(2)]
        B_cnb2 = [Buf() for _ in range(2)]
        tpr, tprb = sb("tpr", [128, 512], F32), Buf()
        t2, t2b = sb("t2", [128, 512], F32), Buf()
        tprg, tprgb = sb("tprg", [128, 512], F32), Buf()
        t2g, t2gb = sb("t2g", [128, 512], F32), Buf()
        us = [sb(f"us{i}", [128, 512], F32) for i in range(2)]
        usb = [Buf() for _ in range(2)]

        P_A = ctx.enter_context(nc.psum_tensor("P_A", [128, 1024], F32))
        P_B = ctx.enter_context(nc.psum_tensor("P_B", [128, 1024], F32))
        P_C = ctx.enter_context(nc.psum_tensor("P_C", [128, 512], F32))
        P_D = ctx.enter_context(nc.psum_tensor("P_D", [2, 1024], F32))
        P_E = ctx.enter_context(nc.psum_tensor("P_E", [128, 512], F32))
        pab = [Buf(), Buf()]
        pbb = [Buf(), Buf()]
        pcb, peb = Buf(), Buf()
        env["p_stat"], env["p_stat_b"] = P_D, Buf()
        pdb = env["p_stat_b"]

        for (dst, dstb, src, di, do) in (
                (w1b, w1bb, w1, D, D), (w2b, w2bb, w2, D, 512),
                (w3b, w3bb, w3, 512, 256), (w4b, w4bb, w4, 256, 1),
                (gw1b, gw1bb, gw1, D, 512), (gw2b, gw2bb, gw2, 512, 256),
                (gw3b, gw3bb, gw3, 256, 1)):
            op(p, "gpsimd", lambda e, dst=dst, src=src, di=di, do=do:
               e.dma_start(out=dst[:, :], in_=w_ap(src, di, do),
                           allow_slow_non_contiguous=(do == 1)),
               writes=[dstb], dma=True, dsem="gd1")
        for (dst, dstb, src) in ((b1c, b1cb, b1), (b2c, b2cb, b2),
                                 (b3c, b3cb, b3), (b4c, b4cb, b4),
                                 (gb1c, gb1cb, gb1), (gb2c, gb2cb, gb2),
                                 (gb3c, gb3cb, gb3), (arc, arcb, ar)):
            op(p, "sync", lambda e, dst=dst, src=src:
               e.dma_start(out=dst[:, :], in_=src[:, :]),
               writes=[dstb], dma=True)

        ms = p.op("gpsimd", lambda e: e.memset(colsum[:, :], 0.0))
        colsumb.did_write(ms)
        for tb in range(gf_tbs):
            own = tb < NOJ
            if own:
                def ydst(c, tb=tb):
                    a = (c * NOJ + tb) * 512
                    return state[:, a:a + 512], None
            else:
                def ydst(c):
                    return ycs[:, :], ycsb

            def post(c, tb=tb, own=own):
                if own:
                    a = (c * NOJ + tb) * 512
                    src_ap = state[:, a:a + 512]
                    tmpb = Buf()
                    tmpb.did_write(("dve", p.sems["dve"][1]))
                    rb = [tmpb]
                else:
                    src_ap = ycs[:, :]
                    rb = [ycsb]
                op(p, "scalar", lambda e, src_ap=src_ap: e.activation(
                    junk[:, :], src_ap, AF.Copy, bias=0.0,
                    scale=1.0, accum_out=red[:, 0:1]),
                    reads=rb, writes=[redb, junkb])
                op(p, "gpsimd", lambda e, c=c: e.tensor_tensor(
                    colsum[:, c:c + 1], colsum[:, c:c + 1], red[:, 0:1], op=OP.add),
                    reads=[redb, colsumb], writes=[colsumb])
            emit_ln(p, env, xt, S, tb * 512, ydst, post_fn=post)
            if own:
                state_bj[tb].did_write(("dve", p.sems["dve"][1]))

        op(p, "vector", lambda e: e.tensor_scalar(
            mean_bf[:, :], colsum[:, :], 1.0 / S, None, op0=OP.mult),
            reads=[colsumb], writes=[mean_bfb])

        for o in range(4):
            for c in range(NCH):
                op(p, "tensor", lambda e, o=o, c=c: e.matmul(
                    P_C[:, o:o + 1],
                    gw1b[:, c * 512 + o * 128: c * 512 + (o + 1) * 128],
                    mean_bf[:, c:c + 1], start=(c == 0), stop=(c == NCH - 1)),
                    reads=[gw1bb, mean_bfb], writes=([pcb] if (o == 0 and c == 0) else ()))
        pcb.did_write(("pe", p.sems["pe"][1]))
        for o in range(4):
            op(p, "scalar", lambda e, o=o: e.activation(
                hcol[:, o:o + 1], P_C[:, o:o + 1], AF.Gelu,
                bias=gb1c[:, o:o + 1], scale=1.0),
                reads=[pcb, gb1cb], writes=([hcolb] if o == 0 else ()))
        hcolb.did_write(("act", p.sems["act"][1]))
        for o in range(2):
            for c in range(4):
                op(p, "tensor", lambda e, o=o, c=c: e.matmul(
                    P_E[:, o:o + 1],
                    gw2b[:, c * 256 + o * 128: c * 256 + (o + 1) * 128],
                    hcol[:, c:c + 1], start=(c == 0), stop=(c == 3)),
                    reads=[gw2bb, hcolb], writes=([peb] if (o == 0 and c == 0) else ()))
        peb.did_write(("pe", p.sems["pe"][1]))
        for o in range(2):
            op(p, "scalar", lambda e, o=o: e.activation(
                hcol[:, 4 + o:5 + o], P_E[:, o:o + 1], AF.Gelu,
                bias=gb2c[:, o:o + 1], scale=1.0),
                reads=[peb, gb2cb], writes=[hcolb])
        for c in range(2):
            op(p, "tensor", lambda e, c=c: e.matmul(
                P_D[0:1, 0:1], gw3b[:, c:c + 1], hcol[:, 4 + c:5 + c],
                start=(c == 0), stop=(c == 1)),
                reads=[gw3bb, hcolb], writes=([pdb] if c == 0 else ()))
        pdb.did_write(("pe", p.sems["pe"][1]))
        op(p, "scalar", lambda e: e.activation(
            gcell[0:1, 0:1], P_D[0:1, 0:1], AF.Sigmoid, bias=gb3c[0:1, 0:1],
            scale=1.0), reads=[pdb, gb3cb], writes=[gcellb])
        op(p, "gpsimd", lambda e: e.tensor_tensor(
            gcell[0:1, 1:2], arc[0:1, 0:1], gcell[0:1, 0:1], op=OP.mult),
            reads=[arcb, gcellb], writes=[gcellb])
        op(p, "gpsimd", lambda e: e.tensor_scalar(
            gcell[0:1, 2:3], gcell[0:1, 1:2], 1.4, None, op0=OP.mult),
            reads=[gcellb], writes=[gcellb])
        op(p, "gpsimd", lambda e: e.tensor_tensor(
            gcell[0:1, 3:4], gcell[0:1, 2:3], arc[0:1, 0:1], op=OP.subtract),
            reads=[gcellb, arcb], writes=[gcellb])
        op(p, "gpsimd", lambda e: e.tensor_scalar(
            acell[0:1, 0:1], gcell[0:1, 3:4], 0.1, None, op0=OP.add),
            reads=[gcellb], writes=[acellb])
        op(p, "gpsimd", lambda e: e.tensor_scalar(
            bcell[0:1, 0:1], arc[0:1, 0:1], 0.6, None, op0=OP.mult),
            reads=[arcb], writes=[bcellb])
        if debug:
            op(p, "sync", lambda e: e.dma_start(out=dbg_cells[0:1, 0:1], in_=gcell[0:1, 0:1]),
               reads=[gcellb], dma=True)
            op(p, "sync", lambda e: e.dma_start(out=dbg_cells[0:1, 1:2], in_=acell[0:1, 0:1]),
               reads=[acellb], dma=True)
            op(p, "sync", lambda e: e.dma_start(out=dbg_cells[0:1, 2:3], in_=bcell[0:1, 0:1]),
               reads=[bcellb], dma=True)
            op(p, "vector", lambda e: e.tensor_copy(env["nrm"][0][:, 0:8], colsum[:, :]),
               reads=[colsumb], writes=[env["nrm_b"][0]])
            op(p, "vector", lambda e: e.tensor_copy(env["nrm"][1][:, 0:8], hcol[:, :]),
               reads=[hcolb], writes=[env["nrm_b"][1]])
            op(p, "sync", lambda e: e.dma_start(out=dbg_hcol[:, :], in_=env["nrm"][1][:, 0:8]),
               reads=[env["nrm_b"][1]], dma=True)
            op(p, "sync", lambda e: e.dma_start(out=dbg_mean[:, :], in_=env["nrm"][0][:, 0:8]),
               reads=[env["nrm_b"][0]], dma=True)


        def coupling_net(cyc, j):
            jp = j % 2
            trow, trowb = trow2[jp], trowb2[jp]
            nrow, nrowb = nrow2[jp], nrowb2[jp]
            for c in range(NCH):
                a = (c * NOJ + j) * 512
                op(p, "vector", lambda e, a=a, c=c: e.tensor_copy(
                    sbf[:, c * 512:(c + 1) * 512], state[:, a:a + 512]),
                    reads=[state_bj[j]], writes=([sbf_b] if c == 0 else ()))
            sbf_b.did_write(("dve", p.sems["dve"][1]))
            for o in range(NCH):
                pt = P_A if o % 2 == 0 else P_B
                ptb = (pab if o % 2 == 0 else pbb)[(o // 2) % 2]
                hf = ((o // 2) % 2) * 512
                for c in range(NCH):
                    op(p, "tensor", lambda e, pt=pt, hf=hf, o=o, c=c: e.matmul(
                        pt[:, hf:hf + 512],
                        w1b[:, c * D + o * 128: c * D + (o + 1) * 128],
                        sbf[:, c * 512:(c + 1) * 512],
                        start=(c == 0), stop=(c == NCH - 1)),
                        reads=[w1bb, sbf_b], writes=([ptb] if c == 0 else ()))
                ptb.did_write(("pe", p.sems["pe"][1]))
                op(p, "scalar", lambda e, pt=pt, hf=hf, o=o: e.activation(
                    h1[:, o * 512:(o + 1) * 512], pt[:, hf:hf + 512],
                    AF.Gelu, bias=b1c[:, o:o + 1], scale=1.0),
                    reads=[ptb, b1cb], writes=([h1b] if o == 0 else ()))
            h1b.did_write(("act", p.sems["act"][1]))
            for o in range(4):
                pt = P_A if o % 2 == 0 else P_B
                ptb = (pab if o % 2 == 0 else pbb)[0]
                for c in range(NCH):
                    op(p, "tensor", lambda e, pt=pt, o=o, c=c: e.matmul(
                        pt[:, 0:512],
                        w2b[:, c * 512 + o * 128: c * 512 + (o + 1) * 128],
                        h1[:, c * 512:(c + 1) * 512],
                        start=(c == 0), stop=(c == NCH - 1)),
                        reads=[w2bb, h1b], writes=([ptb] if c == 0 else ()))
                ptb.did_write(("pe", p.sems["pe"][1]))
                op(p, "scalar", lambda e, pt=pt, o=o: e.activation(
                    h2[:, o * 512:(o + 1) * 512], pt[:, 0:512],
                    AF.Gelu, bias=b2c[:, o:o + 1], scale=1.0),
                    reads=[ptb, b2cb], writes=([h2b] if o == 0 else ()))
            h2b.did_write(("act", p.sems["act"][1]))
            for o in range(2):
                pt = P_A if o % 2 == 0 else P_B
                ptb = (pab if o % 2 == 0 else pbb)[0]
                for c in range(4):
                    op(p, "tensor", lambda e, pt=pt, o=o, c=c: e.matmul(
                        pt[:, 0:512],
                        w3b[:, c * 256 + o * 128: c * 256 + (o + 1) * 128],
                        h2[:, c * 512:(c + 1) * 512],
                        start=(c == 0), stop=(c == 3)),
                        reads=[w3bb, h2b], writes=([ptb] if c == 0 else ()))
                ptb.did_write(("pe", p.sems["pe"][1]))
                op(p, "scalar", lambda e, pt=pt, o=o: e.activation(
                    h3[:, o * 512:(o + 1) * 512], pt[:, 0:512],
                    AF.Gelu, bias=b3c[:, o:o + 1], scale=1.0),
                    reads=[ptb, b3cb], writes=([h3b] if o == 0 else ()))
            h3b.did_write(("act", p.sems["act"][1]))
            for c in range(2):
                op(p, "tensor", lambda e, c=c: e.matmul(
                    P_D[0:1, 0:512], w4b[:, c:c + 1],
                    h3[:, c * 512:(c + 1) * 512],
                    start=(c == 0), stop=(c == 1)),
                    reads=[w4bb, h3b], writes=([pdb] if c == 0 else ()))
            pdb.did_write(("pe", p.sems["pe"][1]))
            op(p, "scalar", lambda e, trow=trow: e.activation(
                trow[0:1, :], P_D[0:1, 0:512], AF.Sigmoid,
                bias=b4c[0:1, 0:1], scale=1.0),
                reads=[pdb, b4cb], writes=[trowb])
            op(p, "vector", lambda e, trow=trow: e.tensor_scalar(
                trow[0:1, :], trow[0:1, :], bcell[0:1, 0:1], acell[0:1, 0:1],
                op0=OP.mult, op1=OP.add),
                reads=[trowb, bcellb, acellb], writes=[trowb])
            op(p, "vector", lambda e, trow=trow, nrow=nrow: e.tensor_scalar(
                nrow[0:1, :], trow[0:1, :], -1.0, None, op0=OP.mult),
                reads=[trowb], writes=[nrowb])
            op(p, "sync", lambda e, trow=trow, jp=jp: e.dma_start(
                out=B_cp2[jp][:, :], in_=bcast_ap(trow, 0, 128, 512)),
                reads=[trowb], writes=[B_cpb2[jp]], dma=True, dsem="gd2")
            op(p, "sync", lambda e, nrow=nrow, jp=jp: e.dma_start(
                out=B_cn2[jp][:, :], in_=bcast_ap(nrow, 0, 128, 512)),
                reads=[nrowb], writes=[B_cnb2[jp]], dma=True, dsem="gd2")

        def twist(cyc, j):
            jp = j % 2
            B_cp, B_cpb = B_cp2[jp], B_cpb2[jp]
            B_cn, B_cnb = B_cn2[jp], B_cnb2[jp]
            for m in range(4):
                eng = "vector"
                tp_, tpb_ = (tpr, tprb) if m < 2 else (tprg, tprgb)
                t_, tb_ = (t2, t2b) if m < 2 else (t2g, t2gb)
                sA, sAb = (B_cp, B_cpb) if m < 2 else (B_cn, B_cnb)
                sB, sBb = (B_cn, B_cnb) if m < 2 else (B_cp, B_cpb)
                a0 = (m * NOJ + j) * 512
                a4 = ((m + 4) * NOJ + j) * 512
                op(p, eng, lambda e, a0=a0, tp_=tp_: e.tensor_copy(
                    tp_[:, :], state[:, a0:a0 + 512]),
                    reads=[state_bj[j]], writes=[tpb_])
                op(p, eng, lambda e, a4=a4, sA=sA, t_=t_: e.tensor_tensor(
                    t_[:, :], state[:, a4:a4 + 512], sA[:, :], op=OP.mult),
                    reads=[state_bj[j], sAb], writes=[tb_])
                op(p, eng, lambda e, a0=a0, t_=t_: e.tensor_tensor(
                    state[:, a0:a0 + 512], state[:, a0:a0 + 512], t_[:, :],
                    op=OP.add), reads=[tb_, state_bj[j]], writes=())
                op(p, eng, lambda e, sB=sB, tp_=tp_, t_=t_: e.tensor_tensor(
                    t_[:, :], tp_[:, :], sB[:, :], op=OP.mult),
                    reads=[tpb_, sBb], writes=[tb_])
                op(p, eng, lambda e, a4=a4, t_=t_: e.tensor_tensor(
                    state[:, a4:a4 + 512], state[:, a4:a4 + 512], t_[:, :],
                    op=OP.add), reads=[tb_, state_bj[j]], writes=())
            state_bj[j].did_write(("dve", p.sems["dve"][1]))

        for cyc in range(ncycles):
            for j in range(NOJ):
                coupling_net(cyc, j)
                if j > 0:
                    twist(cyc, j - 1)
            twist(cyc, NOJ - 1)

        for j in range(NOJ):
            for c in range(NCH):
                i = c % 2
                u = c % 2
                a = (c * NOJ + j) * 512
                op(p, "sync", lambda e, i=i, c=c, j=j: e.dma_start(
                    out=env["xs"][i][:, :], in_=chunk_ap(xt, S, c, j * 512, 512)),
                    writes=[env["xs_b"][i]], dma=True, dsem=f"sd{i}")
                op(p, "vector", lambda e, i=i, u=u, a=a: e.tensor_tensor(
                    us[u][:, :], env["xs"][i][:, :], state[:, a:a + 512],
                    op=OP.add),
                    reads=[env["xs_b"][i], state_bj[j]], writes=[usb[u]])
                op(p, "sync", lambda e, u=u, c=c, j=j: e.dma_start(
                    out=bass.AP(upd, c * 128 * OWN + j * 512, [[OWN, 128], [1, 512]]),
                    in_=us[u][:, :]),
                    reads=[usb[u]], dma=True)
        p._wait("sync", "sdma", p.sems["sdma"][1])

        with nc.Block("ffn") as blk:
            p.finalize(blk)
    return nc


def _prep_bias_pc(b, nch):
    return np.ascontiguousarray(b.reshape(nch, 128).T).astype(np.float32)


def _own_cols(r):
    idx = []
    for j in range(NOJ):
        g = 2 * j + r
        idx.append(np.arange(g * 512, (g + 1) * 512))
    return np.concatenate(idx)


def run_ffn_launch(nc_ffn, xt_state, weights, want_res=False):
    in_maps = []
    for core in range(8):
        b, r = core // 2, core % 2
        cols = _own_cols(r)
        perm = np.concatenate([cols, _own_cols(1 - r)])
        m = {
            "xt": np.ascontiguousarray(xt_state[b][:, perm]),
            "w1": weights["cn_w1"], "w2": weights["cn_w2"],
            "w3": weights["cn_w3"], "w4": weights["cn_w4"],
            "gw1": weights["gc_w1"], "gw2": weights["gc_w2"],
            "gw3": weights["gc_w3"],
            "b1": _prep_bias_pc(weights["cn_b1"], 8),
            "b2": _prep_bias_pc(weights["cn_b2"], 4),
            "b3": _prep_bias_pc(weights["cn_b3"], 2),
            "b4": weights["cn_b4"].reshape(1, 1).astype(np.float32),
            "gb1": _prep_bias_pc(weights["gc_b1"], 4),
            "gb2": _prep_bias_pc(weights["gc_b2"], 2),
            "gb3": weights["gc_b3"].reshape(1, 1).astype(np.float32),
            "ar": weights["adaptive_range"].reshape(1, 1).astype(np.float32),
        }
        in_maps.append(m)
    res = run_bass_kernel_spmd(nc_ffn, in_maps, core_ids=list(range(8)))
    out = [x.copy() for x in xt_state]
    for core in range(8):
        b, r = core // 2, core % 2
        u = res.results[core]["upd"]
        for j in range(NOJ):
            g = 2 * j + r
            out[b][:, g * 512:(g + 1) * 512] = u[:, j * 512:(j + 1) * 512]
    if want_res:
        return out, res
    return out




def build_attn_a():
    nc = bass.Bass("TRN2", target_bir_lowering=False)
    xt = nc.dram_tensor("xt", [D, S], F32, kind="ExternalInput")
    xo = nc.dram_tensor("xo", [D, OWN], F32, kind="ExternalInput")
    masks = nc.dram_tensor("masks", [128, 8 * 512], BF16, kind="ExternalInput")
    wq = nc.dram_tensor("wq", [D, D], F32, kind="ExternalInput")
    wk = nc.dram_tensor("wk", [D, D], F32, kind="ExternalInput")
    wv = nc.dram_tensor("wv", [D, D], F32, kind="ExternalInput")
    wo = nc.dram_tensor("wo", [D, D], F32, kind="ExternalInput")
    bq = nc.dram_tensor("bq", [128, 8], F32, kind="ExternalInput")
    bk = nc.dram_tensor("bk", [128, 8], F32, kind="ExternalInput")
    bv = nc.dram_tensor("bv", [128, 8], F32, kind="ExternalInput")
    bo = nc.dram_tensor("bo", [128, 8], F32, kind="ExternalInput")
    upd = nc.dram_tensor("upd", [D, OWN], F32, kind="ExternalOutput")
    K_dram = nc.dram_tensor("K_dram", [NCH, 128, S], BF16, kind="Internal")

    with ExitStack() as ctx:
        sb = lambda name, shp, dt: ctx.enter_context(nc.sbuf_tensor(name, shp, dt))
        p = Prog(nc, ctx)
        env = {}
        make_ln_env(p, ctx, nc, env, nxs=2, nxfull=1)

        kslot = [sb(f"kslot{i}", [128, 512], BF16) for i in range(2)]
        kslotb = [Buf() for _ in range(2)]
        Kst = [sb(f"Kst{i}", [128, S], BF16) for i in range(2)]
        Kstb = [Buf() for _ in range(2)]
        Kdb = Buf()
        V_sb, Vb = sb("V_sb", [128, 32 * D], BF16), Buf()
        qbuf, Qb = sb("qbuf", [128, NCH * 512], BF16), Buf()
        ybuf2 = [sb(f"ybuf{i}", [128, NCH * 512], BF16) for i in range(2)]
        Yb2 = [Buf() for i in range(2)]
        wA, wAb = sb("wA", [128, NCH * D], BF16), Buf()
        wB, wBb = sb("wB", [128, NCH * D], BF16), Buf()
        msk, mskb = sb("msk", [128, 8 * 512], BF16), Buf()
        Ebuf = sb("Ebuf", [128, 3 * 1024], BF16)
        Eb = [Buf() for _ in range(3)]
        Z_sb, Zb = sb("Z_sb", [128, NCH * 512], BF16), Buf()
        Brec, Brecb = sb("Brec", [128, 512], F32), Buf()
        Brec2, Brec2b = sb("Brec2", [128, 512], F32), Buf()
        rrow, rrowb = sb("rrow", [1, 512], F32), Buf()
        rrow2, rrow2b = sb("rrow2", [1, 512], F32), Buf()
        bq_c, bq_cb = sb("bq_c", [128, 8], F32), Buf()
        bk_c, bk_cb = sb("bk_c", [128, 8], F32), Buf()
        bo_c, bo_cb = sb("bo_c", [128, 8], F32), Buf()
        bv_bf, bv_bfb = sb("bv_bf", [128, 8], BF16), Buf()
        btot, btotb = sb("btot", [128, 8], F32), Buf()
        us = [env["nrm"][i] for i in range(2)]
        usb = [env["nrm_b"][i] for i in range(2)]

        P_A = ctx.enter_context(nc.psum_tensor("P_A", [128, 1024], F32))
        P_B = ctx.enter_context(nc.psum_tensor("P_B", [128, 1024], F32))
        P_C = ctx.enter_context(nc.psum_tensor("P_C", [128, 512], F32))
        P_D = ctx.enter_context(nc.psum_tensor("P_D", [2, 1024], F32))
        P_E = ctx.enter_context(nc.psum_tensor("P_E", [128, 512], F32))
        pab, pbb, pcb, peb = Buf(), Buf(), Buf(), Buf()
        env["p_stat"], env["p_stat_b"] = P_D, Buf()
        pdb = env["p_stat_b"]
        rot = [(P_A, 0, pab), (P_A, 512, pab), (P_B, 0, pbb), (P_B, 512, pbb)]
        rotb = [Buf() for _ in range(4)]

        op(p, "gpsimd", lambda e: e.dma_start(out=wA[:, :], in_=w_ap(wk, D, D)),
           writes=[wAb], dma=True, dsem="gd1")
        op(p, "gpsimd", lambda e: e.dma_start(out=wB[:, :], in_=w_ap(wv, D, D)),
           writes=[wBb], dma=True, dsem="gd1")
        op(p, "gpsimd", lambda e: e.dma_start(out=bv_bf[:, :], in_=bv[:, :]),
           writes=[bv_bfb], dma=True, dsem="gd1")
        op(p, "sync", lambda e: e.dma_start(out=msk[:, :], in_=masks[:, :]),
           writes=[mskb], dma=True)
        for (dst, dstb, src) in ((bq_c, bq_cb, bq), (bk_c, bk_cb, bk),
                                 (bo_c, bo_cb, bo)):
            op(p, "sync", lambda e, dst=dst, src=src:
               e.dma_start(out=dst[:, :], in_=src[:, :]), writes=[dstb], dma=True)

        for tb in range(NTB):
            ybuf, Yb = ybuf2[tb % 2], Yb2[tb % 2]
            def ydst(c, ybuf=ybuf, Yb=Yb):
                return ybuf[:, c * 512:(c + 1) * 512], (Yb if c == 0 else None)
            emit_ln(p, env, xt, S, tb * 512, ydst)
            Yb.did_write(("dve", p.sems["dve"][1]))
            for o in range(NCH):
                pt, off, ptb = rot[o % 4][0], rot[o % 4][1], rotb[o % 4]
                for c in range(NCH):
                    op(p, "tensor", lambda e, pt=pt, off=off, o=o, c=c, ybuf=ybuf:
                       e.matmul(
                        pt[:, off:off + 512],
                        wA[:, c * D + o * 128: c * D + (o + 1) * 128],
                        ybuf[:, c * 512:(c + 1) * 512],
                        start=(c == 0), stop=(c == NCH - 1)),
                        reads=[wAb, Yb], writes=([ptb] if c == 0 else ()))
                ptb.did_write(("pe", p.sems["pe"][1]))
                ks = o % 2
                op(p, "scalar", lambda e, pt=pt, off=off, o=o, ks=ks: e.activation(
                    kslot[ks][:, :], pt[:, off:off + 512], AF.Identity,
                    bias=bk_c[:, o:o + 1], scale=1.0),
                    reads=[ptb, bk_cb], writes=[kslotb[ks]])
                op(p, "sync", lambda e, o=o, tb=tb, ks=ks: e.dma_start(
                    out=bass.AP(K_dram, o * 128 * S + tb * 512,
                                [[S, 128], [1, 512]]),
                    in_=kslot[ks][:, :]),
                    reads=[kslotb[ks]], writes=[Kdb], dma=True, dsem="sd5")
            for tt in range(4):
                for dblk in range(2):
                    k4 = (tt * 2 + dblk) % 4
                    pt, off, ptb = rot[k4][0], rot[k4][1], rotb[k4]
                    for c in range(NCH):
                        op(p, "tensor", lambda e, pt=pt, off=off, tt=tt, dblk=dblk,
                           c=c, ybuf=ybuf: e.matmul(
                            pt[:, off:off + 512],
                            ybuf[:, c * 512 + tt * 128: c * 512 + (tt + 1) * 128],
                            wB[:, c * D + dblk * 512: c * D + (dblk + 1) * 512],
                            start=(c == 0), stop=(c == NCH - 1)),
                            reads=[wBb, Yb], writes=([ptb] if c == 0 else ()))
                    ptb.did_write(("pe", p.sems["pe"][1]))
                    op(p, "scalar", lambda e, pt=pt, off=off, tt=tt, dblk=dblk, tb=tb:
                       e.activation(
                        V_sb[:, (tb * 4 + tt) * D + dblk * 512:
                             (tb * 4 + tt) * D + (dblk + 1) * 512],
                        pt[:, off:off + 512], AF.Copy, bias=0.0, scale=1.0),
                        reads=[ptb], writes=[Vb])

        op(p, "gpsimd", lambda e: e.dma_start(out=wA[:, :], in_=w_ap(wq, D, D)),
           writes=[wAb], dma=True, dsem="gd1")
        op(p, "gpsimd", lambda e: e.dma_start(out=wB[:, :], in_=w_ap(wo, D, D)),
           writes=[wBb], dma=True, dsem="gd1")
        for o in range(NCH):
            for c in range(NCH):
                op(p, "tensor", lambda e, o=o, c=c: e.matmul(
                    P_E[:, o:o + 1],
                    wB[:, c * D + o * 128: c * D + (o + 1) * 128],
                    bv_bf[:, c:c + 1], start=(c == 0), stop=(c == NCH - 1)),
                    reads=[wBb, bv_bfb],
                    writes=([peb] if (o == 0 and c == 0) else ()))
        peb.did_write(("pe", p.sems["pe"][1]))
        op(p, "vector", lambda e: e.tensor_tensor(
            btot[:, :], P_E[:, 0:8], bo_c[:, :], op=OP.add),
            reads=[peb, bo_cb], writes=[btotb])

        for j in range(NOJ):
            ybuf, Yb = ybuf2[j % 2], Yb2[j % 2]
            def ydst(c, ybuf=ybuf, Yb=Yb):
                return ybuf[:, c * 512:(c + 1) * 512], (Yb if c == 0 else None)
            emit_ln(p, env, xo, OWN, j * 512, ydst)
            Yb.did_write(("dve", p.sems["dve"][1]))
            for o in range(NCH):
                pt, off, ptb = rot[o % 4][0], rot[o % 4][1], rotb[o % 4]
                for c in range(NCH):
                    op(p, "tensor", lambda e, pt=pt, off=off, o=o, c=c, ybuf=ybuf:
                       e.matmul(
                        pt[:, off:off + 512],
                        wA[:, c * D + o * 128: c * D + (o + 1) * 128],
                        ybuf[:, c * 512:(c + 1) * 512],
                        start=(c == 0), stop=(c == NCH - 1)),
                        reads=[wAb, Yb], writes=([ptb] if c == 0 else ()))
                ptb.did_write(("pe", p.sems["pe"][1]))
                op(p, "scalar", lambda e, pt=pt, off=off, o=o: e.activation(
                    qbuf[:, o * 512:(o + 1) * 512], pt[:, off:off + 512],
                    AF.Identity, bias=bq_c[:, o:o + 1], scale=1.0),
                    reads=[ptb, bq_cb], writes=[Qb])
            Qb.did_write(("act", p.sems["act"][1]))
            nst = 8 * j + 8
            op(p, "sync", lambda e, nst=nst: e.dma_start(
                out=Kst[0][:, 0:nst * 128],
                in_=bass.AP(K_dram, 0, [[S, 128], [1, nst * 128]])),
                reads=[Kdb], writes=[Kstb[0]], dma=True, dsem="sd3")
            for hp in range(NCH):
                kb = hp % 2
                if hp + 1 < NCH:
                    op(p, "sync", lambda e, hp=hp, nst=nst: e.dma_start(
                        out=Kst[(hp + 1) % 2][:, 0:nst * 128],
                        in_=bass.AP(K_dram, (hp + 1) * 128 * S,
                                    [[S, 128], [1, nst * 128]])),
                        reads=[Kdb], writes=[Kstb[(hp + 1) % 2]], dma=True,
                        dsem=f"sd{3 + (hp + 1) % 2}")
                pz, pzb = (P_C, pcb) if hp % 2 == 0 else (P_E, peb)

                def sc_group(st, kb=kb, hp=hp, j=j):
                    psc, pscb = (P_A, pab) if st % 2 == 0 else (P_B, pbb)
                    op(p, "tensor", lambda e, psc=psc, kb=kb, hp=hp, st=st: e.matmul(
                        psc[:, 0:512],
                        Kst[kb][0:64, st * 128:(st + 1) * 128],
                        qbuf[0:64, hp * 512:(hp + 1) * 512],
                        start=True, stop=True, tile_position=(0, 0)),
                        reads=[Kstb[kb], Qb], writes=[pscb])
                    op(p, "tensor", lambda e, psc=psc, kb=kb, hp=hp, st=st: e.matmul(
                        psc[:, 512:1024],
                        Kst[kb][64:128, st * 128:(st + 1) * 128],
                        qbuf[64:128, hp * 512:(hp + 1) * 512],
                        start=True, stop=True, tile_position=(64, 0)),
                        reads=[Kstb[kb], Qb])
                    pscb.did_write(("pe", p.sems["pe"][1]))
                    if st >= 8 * j:
                        m = st - 8 * j
                        op(p, "vector", lambda e, psc=psc, m=m: e.tensor_tensor(
                            psc[:, 0:512], psc[:, 0:512],
                            msk[:, m * 512:(m + 1) * 512], op=OP.add),
                            reads=[mskb, pscb], writes=())
                        op(p, "vector", lambda e, psc=psc, m=m: e.tensor_tensor(
                            psc[:, 512:1024], psc[:, 512:1024],
                            msk[:, m * 512:(m + 1) * 512], op=OP.add),
                            reads=[mskb], writes=())
                        pscb.did_write(("dve", p.sems["dve"][1]))

                def consume(st, hp=hp, nst=nst, pz=pz, pzb=pzb):
                    psc, pscb = (P_A, pab) if st % 2 == 0 else (P_B, pbb)
                    es = st % 3
                    op(p, "scalar", lambda e, psc=psc, es=es: e.activation(
                        Ebuf[:, es * 1024:(es + 1) * 1024], psc[:, 0:1024],
                        AF.Exp, bias=0.0, scale=0.125),
                        reads=[pscb], writes=[Eb[es]])
                    op(p, "tensor", lambda e, es=es, st=st, nst=nst: e.matmul(
                        P_D[0:1, 0:512], env["ones"][:, 0:1],
                        Ebuf[:, es * 1024: es * 1024 + 512],
                        start=(st == 0), stop=(st == nst - 1)),
                        reads=[Eb[es], env["ones_b"]],
                        writes=([pdb] if st == 0 else ()))
                    op(p, "tensor", lambda e, es=es, st=st, nst=nst: e.matmul(
                        P_D[0:1, 512:1024], env["ones"][:, 0:1],
                        Ebuf[:, es * 1024 + 512: es * 1024 + 1024],
                        start=(st == 0), stop=(st == nst - 1)),
                        reads=[Eb[es], env["ones_b"]])
                    op(p, "tensor", lambda e, es=es, st=st, hp=hp, nst=nst, pz=pz:
                       e.matmul(
                        pz[0:64, 0:512],
                        V_sb[:, st * D + (2 * hp) * 64: st * D + (2 * hp) * 64 + 64],
                        Ebuf[:, es * 1024: es * 1024 + 512],
                        start=(st == 0), stop=(st == nst - 1),
                        tile_position=(0, 0), skip_group_check=True),
                        reads=[Eb[es], Vb], writes=([pzb] if st == 0 else ()))
                    op(p, "tensor", lambda e, es=es, st=st, hp=hp, nst=nst, pz=pz:
                       e.matmul(
                        pz[64:128, 0:512],
                        V_sb[:, st * D + (2 * hp + 1) * 64:
                             st * D + (2 * hp + 1) * 64 + 64],
                        Ebuf[:, es * 1024 + 512: es * 1024 + 1024],
                        start=(st == 0), stop=(st == nst - 1),
                        tile_position=(0, 64), skip_group_check=True),
                        reads=[Eb[es], Vb])

                sc_group(0)
                for st in range(nst):
                    if st + 1 < nst:
                        sc_group(st + 1)
                    consume(st)
                pdb.did_write(("pe", p.sems["pe"][1]))
                pzb.did_write(("pe", p.sems["pe"][1]))
                op(p, "vector", lambda e: e.reciprocal(
                    rrow[0:1, :], P_D[0:1, 0:512]), reads=[pdb], writes=[rrowb])
                op(p, "vector", lambda e: e.reciprocal(
                    rrow2[0:1, :], P_D[0:1, 512:1024]), reads=[pdb], writes=[rrow2b])
                op(p, "sync", lambda e: e.dma_start(
                    out=Brec[:, :], in_=bcast_ap(rrow, 0, 128, 512)),
                    reads=[rrowb], writes=[Brecb], dma=True, dsem="gd2")
                op(p, "sync", lambda e: e.dma_start(
                    out=Brec2[:, :], in_=bcast_ap(rrow2, 0, 128, 512)),
                    reads=[rrow2b], writes=[Brec2b], dma=True, dsem="gd2")
                op(p, "vector", lambda e, hp=hp, pz=pz: e.tensor_tensor(
                    Z_sb[0:64, hp * 512:(hp + 1) * 512], pz[0:64, 0:512],
                    Brec[0:64, :], op=OP.mult),
                    reads=[pzb, Brecb], writes=[Zb])
                op(p, "vector", lambda e, hp=hp, pz=pz: e.tensor_tensor(
                    Z_sb[64:128, hp * 512:(hp + 1) * 512], pz[64:128, 0:512],
                    Brec2[64:128, :], op=OP.mult),
                    reads=[pzb, Brec2b], writes=())
                Zb.did_write(("dve", p.sems["dve"][1]))
            for o in range(NCH):
                for hc in range(NCH):
                    op(p, "tensor", lambda e, o=o, hc=hc: e.matmul(
                        P_A[:, 0:512],
                        wB[:, hc * D + o * 128: hc * D + (o + 1) * 128],
                        Z_sb[:, hc * 512:(hc + 1) * 512],
                        start=(hc == 0), stop=(hc == NCH - 1)),
                        reads=[wBb, Zb], writes=([pab] if hc == 0 else ()))
                pab.did_write(("pe", p.sems["pe"][1]))
                i = o % 2
                u = o % 2
                op(p, "sync", lambda e, i=i, o=o, j=j: e.dma_start(
                    out=env["xs"][i][:, :], in_=chunk_ap(xo, OWN, o, j * 512, 512)),
                    writes=[env["xs_b"][i]], dma=True, dsem=f"sd{i}")
                op(p, "vector", lambda e, i=i, u=u, o=o: e.scalar_tensor_tensor(
                    us[u][:, :], P_A[:, 0:512], btot[:, o:o + 1],
                    env["xs"][i][:, :], op0=OP.add, op1=OP.add),
                    reads=[pab, btotb, env["xs_b"][i]], writes=[usb[u]])
                op(p, "sync", lambda e, u=u, o=o, j=j: e.dma_start(
                    out=bass.AP(upd, o * 128 * OWN + j * 512, [[OWN, 128], [1, 512]]),
                    in_=us[u][:, :]),
                    reads=[usb[u]], dma=True)
        p._wait("sync", "sdma", p.sems["sdma"][1])

        with nc.Block("attn_a") as blk:
            p.finalize(blk)
    return nc


def make_masks(r):
    out = np.zeros((128, 8 * 512), np.float32)
    i = np.arange(128)[:, None]
    jc = np.arange(512)[None, :]
    for m in range(8):
        if r == 0:
            if m < 4:
                pat = np.where(128 * m + i <= jc, 0.0, NEG)
            else:
                pat = np.full((128, 512), NEG, np.float32)
        else:
            if m < 4:
                pat = np.zeros((128, 512), np.float32)
            else:
                pat = np.where(128 * (m - 4) + i <= jc, 0.0, NEG)
        out[:, m * 512:(m + 1) * 512] = pat
    return out.astype(BFD)


def run_attn_a_launch(nc_a, xt_state, weights):
    in_maps = []
    for core in range(8):
        b, r = core // 2, core % 2
        cols = _own_cols(r)
        m = {
            "xt": xt_state[b],
            "xo": np.ascontiguousarray(xt_state[b][:, cols]),
            "masks": make_masks(r),
            "wq": weights["Wq"], "wk": weights["Wk"],
            "wv": weights["Wv"], "wo": weights["Wo"],
            "bq": _prep_bias_pc(weights["bq"], 8),
            "bk": _prep_bias_pc(weights["bk"], 8),
            "bv": _prep_bias_pc(weights["bv"], 8),
            "bo": _prep_bias_pc(weights["bo"], 8),
        }
        in_maps.append(m)
    res = run_bass_kernel_spmd(nc_a, in_maps, core_ids=list(range(8)))
    out = [x.copy() for x in xt_state]
    for core in range(8):
        b, r = core // 2, core % 2
        u = res.results[core]["upd"]
        for j in range(NOJ):
            g = 2 * j + r
            out[b][:, g * 512:(g + 1) * 512] = u[:, j * 512:(j + 1) * 512]
    return out




def build_attn_b():
    nc = bass.Bass("TRN2", target_bir_lowering=False)
    xt = nc.dram_tensor("xt", [D, S], F32, kind="ExternalInput")
    xo = nc.dram_tensor("xo", [D, OWN], F32, kind="ExternalInput")
    w = nc.dram_tensor("w", [H * D, D], F32, kind="ExternalInput")
    upd = nc.dram_tensor("upd", [D, OWN], F32, kind="ExternalOutput")
    wbf = nc.dram_tensor("wbf", [H, 128, NCH * D], BF16, kind="Internal")
    TBB = 256
    NJ2 = OWN // TBB

    with ExitStack() as ctx:
        sb = lambda name, shp, dt: ctx.enter_context(nc.sbuf_tensor(name, shp, dt))
        p = Prog(nc, ctx)
        env = {}
        make_ln_env(p, ctx, nc, env, nxs=2)

        Y_sb, Yb = sb("Y_sb", [128, NCH * S], BF16), Buf()
        Ytm, Ytmb = sb("Ytm", [128, 32 * D], BF16), Buf()
        yq, yqb = sb("yq", [128, NCH * TBB], BF16), Buf()
        Ebuf = sb("Ebuf", [128, 6 * TBB], BF16)
        Eb = [Buf() for _ in range(3)]
        G_sb = [sb(f"G_sb{i}", [128, NCH * TBB], BF16) for i in range(2)]
        Gb = [Buf() for _ in range(2)]
        outacc, oaccb = sb("outacc", [128, NCH * TBB], F32), Buf()
        wh = [sb(f"wh{i}", [128, NCH * D], BF16) for i in range(2)]
        whb = [Buf() for _ in range(2)]
        wdb = Buf()
        Brec, Brecb = sb("Brec", [128, TBB], F32), Buf()
        rrow, rrowb = sb("rrow", [1, TBB], F32), Buf()
        ident, identb = sb("ident", [128, 128], BF16), Buf()
        onesb_t, onesb_tb = sb("ones_bf", [128, 128], BF16), Buf()
        us = [env["nrm"][i] for i in range(2)]
        usb = [env["nrm_b"][i] for i in range(2)]

        P_A = ctx.enter_context(nc.psum_tensor("P_A", [128, 1024], F32))
        P_B = ctx.enter_context(nc.psum_tensor("P_B", [128, 1024], F32))
        P_C = ctx.enter_context(nc.psum_tensor("P_C", [128, 512], F32))
        P_E = ctx.enter_context(nc.psum_tensor("P_E", [128, 512], F32))
        P_F = ctx.enter_context(nc.psum_tensor("P_F", [128, 512], F32))
        P_D = ctx.enter_context(nc.psum_tensor("P_D", [2, 512], F32))
        pab, pbb, pcb, peb, pfb = Buf(), Buf(), Buf(), Buf(), Buf()
        env["p_stat"], env["p_stat_b"] = P_D, Buf()
        pdb = env["p_stat_b"]
        env["p_stat2"], env["p_stat2_b"] = P_F, pfb

        ms = p.op("gpsimd", lambda e: e.memset(onesb_t[:, :], 1.0))
        onesb_tb.did_write(ms)
        op(p, "gpsimd", lambda e: e.affine_select(
            ident[:, :], onesb_t[:, :], [[1, 128]], OP.is_equal, 0.0,
            base=0, channel_multiplier=-1),
            reads=[onesb_tb], writes=[identb])

        for h in range(H):
            k = h % 2
            op(p, "gpsimd", lambda e, h=h, k=k: e.dma_start(
                out=wh[k][:, :],
                in_=bass.AP(w, h * D * D, [[D, 128], [128 * D, NCH], [1, D]])),
                writes=[whb[k]], dma=True, dsem="gd1")
            op(p, "sync", lambda e, h=h, k=k: e.dma_start(
                out=bass.AP(wbf, h * 128 * NCH * D, [[NCH * D, 128], [1, NCH * D]]),
                in_=wh[k][:, :]),
                reads=[whb[k]], writes=([wdb] if h == 0 else ()), dma=True, dsem="sd5")
        wdb.did_write(("sd5", None))

        for tb in range(NTB):
            def ydst(c, tb=tb):
                return (Y_sb[:, c * S + tb * 512: c * S + (tb + 1) * 512],
                        (Yb if (c == 0 and tb == 0) else None))
            emit_ln(p, env, xt, S, tb * 512, ydst)
        Yb.did_write(("dve", p.sems["dve"][1]))

        trb = [pcb, peb]
        trt = [P_C, P_E]
        for tt in range(32):
            for c in range(NCH):
                k = (tt * NCH + c) % 2
                pv = trt[k][:, 0:64].bitcast(BF16)
                op(p, "tensor", lambda e, pv=pv, tt=tt, c=c: e.transpose(
                    pv, Y_sb[:, c * S + tt * 128: c * S + (tt + 1) * 128],
                    ident[:, :]),
                    reads=[Yb, identb], writes=[trb[k]])
                op(p, "vector", lambda e, pv=pv, tt=tt, c=c: e.tensor_copy(
                    Ytm[:, tt * D + c * 128: tt * D + (c + 1) * 128], pv),
                    reads=[trb[k]], writes=([Ytmb] if (tt == 0 and c == 0) else ()))
        Ytmb.did_write(("dve", p.sems["dve"][1]))

        def load_wh(h):
            k = h % 2
            op(p, "sync", lambda e, h=h, k=k: e.dma_start(
                out=wh[k][:, :],
                in_=bass.AP(wbf, h * 128 * NCH * D,
                            [[NCH * D, 128], [1, NCH * D]])),
                reads=[wdb], writes=[whb[k]], dma=True, dsem=f"sd{3 + k}")

        for j2 in range(NJ2):
            def ydst(c):
                return (yq[:, c * TBB:(c + 1) * TBB], (yqb if c == 0 else None))
            emit_ln(p, env, xo, OWN, j2 * TBB, ydst, n=TBB)
            yqb.did_write(("dve", p.sems["dve"][1]))
            op(p, "gpsimd", lambda e: e.memset(outacc[:, :], 0.0),
               writes=[oaccb])
            if j2 == 0:
                load_wh(0)

            def g_phase(h):
                gb = h % 2
                hr = (h % 2) * 64
                hc = h // 2

                def sc_pair(q):
                    psc = P_C if q % 2 == 0 else P_E
                    pscb2 = pcb if q % 2 == 0 else peb
                    for k2 in range(2):
                        st = 2 * q + k2
                        op(p, "tensor", lambda e, psc=psc, hr=hr, hc=hc, st=st,
                           k2=k2: e.matmul(
                            psc[:, k2 * TBB:(k2 + 1) * TBB],
                            Y_sb[hr:hr + 64,
                                 hc * S + st * 128: hc * S + (st + 1) * 128],
                            yq[hr:hr + 64, hc * TBB:(hc + 1) * TBB],
                            start=True, stop=True,
                            tile_position=(hr, 0)),
                            reads=[Yb, yqb],
                            writes=([pscb2] if k2 == 0 else ()))
                    pscb2.did_write(("pe", p.sems["pe"][1]))

                def consume_pair(q):
                    psc = P_C if q % 2 == 0 else P_E
                    pscb2 = pcb if q % 2 == 0 else peb
                    es = q % 3
                    op(p, "scalar", lambda e, psc=psc, es=es: e.activation(
                        Ebuf[:, es * 512:(es + 1) * 512], psc[:, 0:512],
                        AF.Exp, bias=0.0, scale=0.125),
                        reads=[pscb2], writes=[Eb[es]])
                    for k2 in range(2):
                        st = 2 * q + k2
                        eoff = es * 512 + k2 * TBB
                        op(p, "tensor", lambda e, eoff=eoff, st=st: e.matmul(
                            P_D[0:1, 0:TBB], env["ones"][:, 0:1],
                            Ebuf[:, eoff:eoff + TBB],
                            start=(st == 0), stop=(st == 31)),
                            reads=[Eb[es], env["ones_b"]],
                            writes=([pdb] if st == 0 else ()))
                        for c in range(NCH):
                            pt = P_A if c < 4 else P_B
                            ptb = pab if c < 4 else pbb
                            off = (c % 4) * TBB
                            op(p, "tensor", lambda e, pt=pt, off=off, st=st,
                               c=c, eoff=eoff: e.matmul(
                                pt[:, off:off + TBB],
                                Ytm[:, st * D + c * 128: st * D + (c + 1) * 128],
                                Ebuf[:, eoff:eoff + TBB],
                                start=(st == 0 and c % 2 == 0),
                                stop=(st == 31 and c % 2 == 1)),
                                reads=[Eb[es], Ytmb],
                                writes=([ptb] if (st == 0 and c == 0) else ()))

                sc_pair(0)
                for q in range(16):
                    if q < 15:
                        sc_pair(q + 1)
                    consume_pair(q)
                pab.did_write(("pe", p.sems["pe"][1]))
                pbb.did_write(("pe", p.sems["pe"][1]))
                pdb.did_write(("pe", p.sems["pe"][1]))
                for b2 in range(4):
                    pt = P_A if b2 < 2 else P_B
                    ptb = pab if b2 < 2 else pbb
                    off = (b2 % 2) * 512
                    op(p, "scalar", lambda e, pt=pt, off=off, b2=b2, gb=gb:
                       e.activation(
                        G_sb[gb][:, b2 * 512:(b2 + 1) * 512],
                        pt[:, off:off + 512], AF.Copy, bias=0.0, scale=1.0),
                        reads=[ptb], writes=([Gb[gb]] if b2 == 0 else ()))
                Gb[gb].did_write(("act", p.sems["act"][1]))

            def norm_phase(h):
                gb = h % 2
                op(p, "vector", lambda e: e.reciprocal(
                    rrow[0:1, :], P_D[0:1, 0:TBB]), reads=[pdb], writes=[rrowb])
                op(p, "sync", lambda e: e.dma_start(
                    out=Brec[:, :], in_=bcast_ap(rrow, 0, 128, TBB)),
                    reads=[rrowb], writes=[Brecb], dma=True, dsem="gd2")
                for c in range(NCH):
                    op(p, "vector", lambda e, c=c, gb=gb: e.tensor_tensor(
                        G_sb[gb][:, c * TBB:(c + 1) * TBB],
                        G_sb[gb][:, c * TBB:(c + 1) * TBB], Brec[:, :],
                        op=OP.mult),
                        reads=[Gb[gb], Brecb], writes=())
                Gb[gb].did_write(("dve", p.sems["dve"][1]))

            def w_phase(h):
                gb = h % 2
                for q in range(4):
                    for dc in (2 * q, 2 * q + 1):
                        for c in range(NCH):
                            op(p, "tensor", lambda e, dc=dc, c=c, gb=gb, h=h:
                               e.matmul(
                                P_F[:, (dc % 2) * TBB:((dc % 2) + 1) * TBB],
                                wh[h % 2][:, c * D + dc * 128: c * D + (dc + 1) * 128],
                                G_sb[gb][:, c * TBB:(c + 1) * TBB],
                                start=(dc % 2 == 0 and c == 0),
                                stop=(dc % 2 == 1 and c == NCH - 1)),
                                reads=[whb[h % 2], Gb[gb]],
                                writes=([pfb] if (dc % 2 == 0 and c == 0) else ()))
                    pfb.did_write(("pe", p.sems["pe"][1]))
                    op(p, "vector", lambda e, q=q: e.tensor_tensor(
                        outacc[:, 2 * q * TBB:(2 * q + 2) * TBB],
                        outacc[:, 2 * q * TBB:(2 * q + 2) * TBB],
                        P_F[:, 0:512], op=OP.add),
                        reads=[pfb, oaccb], writes=())
                oaccb.did_write(("dve", p.sems["dve"][1]))

            for h in range(H):
                g_phase(h)
                if h > 0:
                    w_phase(h - 1)
                if not (j2 == NJ2 - 1 and h == H - 1):
                    load_wh((h + 1) % H)
                norm_phase(h)
            w_phase(H - 1)
            for o in range(NCH):
                i = o % 2
                u = o % 2
                op(p, "sync", lambda e, i=i, o=o, j2=j2: e.dma_start(
                    out=env["xs"][i][:, 0:TBB],
                    in_=chunk_ap(xo, OWN, o, j2 * TBB, TBB)),
                    writes=[env["xs_b"][i]], dma=True, dsem=f"sd{i}")
                op(p, "vector", lambda e, i=i, u=u, o=o: e.tensor_tensor(
                    us[u][:, 0:TBB], env["xs"][i][:, 0:TBB],
                    outacc[:, o * TBB:(o + 1) * TBB], op=OP.add),
                    reads=[env["xs_b"][i], oaccb], writes=[usb[u]])
                op(p, "sync", lambda e, u=u, o=o, j2=j2: e.dma_start(
                    out=bass.AP(upd, o * 128 * OWN + j2 * TBB,
                                [[OWN, 128], [1, TBB]]),
                    in_=us[u][:, 0:TBB]),
                    reads=[usb[u]], dma=True)
        p._wait("sync", "sdma", p.sems["sdma"][1])

        with nc.Block("attn_b") as blk:
            p.finalize(blk)
    return nc


def run_attn_b_launch(nc_b, xt_state, weights):
    in_maps = []
    for core in range(8):
        b, r = core // 2, core % 2
        cols = _own_cols(r)
        in_maps.append({
            "xt": xt_state[b],
            "xo": np.ascontiguousarray(xt_state[b][:, cols]),
            "w": weights["weight"],
        })
    res = run_bass_kernel_spmd(nc_b, in_maps, core_ids=list(range(8)))
    out = [x.copy() for x in xt_state]
    for core in range(8):
        b, r = core // 2, core % 2
        u = res.results[core]["upd"]
        for j in range(NOJ):
            g = 2 * j + r
            out[b][:, g * 512:(g + 1) * 512] = u[:, j * 512:(j + 1) * 512]
    return out


_NC_CACHE = {}


def _get_nc(name, builder):
    if name not in _NC_CACHE:
        _NC_CACHE[name] = builder()
    return _NC_CACHE[name]


def kernel(**inputs):
    x = np.asarray(inputs["L0_attn_a_x"], np.float32)
    xt_state = [np.ascontiguousarray(x[b].T) for b in range(BATCH)]

    attn_a_w = {
        "Wq": inputs["L0_attn_a_Wq"], "bq": inputs["L0_attn_a_bq"],
        "Wk": inputs["L0_attn_a_Wk"], "bk": inputs["L0_attn_a_bk"],
        "Wv": inputs["L0_attn_a_Wv"], "bv": inputs["L0_attn_a_bv"],
        "Wo": inputs["L0_attn_a_Wo"], "bo": inputs["L0_attn_a_bo"],
    }
    ffn_w = {k[len("L0_ffn_"):]: np.asarray(v) for k, v in inputs.items()
             if k.startswith("L0_ffn_")}
    attn_b_w = {"weight": inputs["L0_attn_b_weight"]}

    nc_a = _get_nc("attn_a", build_attn_a)
    nc_f = _get_nc("ffn", build_ffn)
    nc_b = _get_nc("attn_b", build_attn_b)

    xt_state = run_attn_a_launch(nc_a, xt_state, attn_a_w)
    xt_state = run_ffn_launch(nc_f, xt_state, ffn_w)
    xt_state = run_attn_b_launch(nc_b, xt_state, attn_b_w)
    xt_state = run_ffn_launch(nc_f, xt_state, ffn_w)

    return np.ascontiguousarray(
        np.stack([xt.T for xt in xt_state])).astype(np.float32)


# version 08
# speedup vs baseline: 1.0565x; 1.0175x over previous
import numpy as np
import ml_dtypes
import concourse.bass as bass
import concourse.mybir as mybir
from concourse.bass_utils import run_bass_kernel_spmd
from contextlib import ExitStack

F32 = mybir.dt.float32
BF16 = mybir.dt.bfloat16
AF = mybir.ActivationFunctionType
OP = mybir.AluOpType
AX = mybir.AxisListType
BFD = ml_dtypes.bfloat16

D = 1024
S = 4096
BATCH = 4
H = 16
DH = 64
NCH = 8
NTB = 8
OWN = 2048
NOJ = 4
NEG = -3.0e38


_ENG_SEM = {"tensor": "pe", "vector": "dve", "scalar": "act",
            "gpsimd": "gp", "sync": "sp"}
SAME_ENGINE_WAITS = True


class Prog:

    ENGS = ("tensor", "vector", "scalar", "gpsimd", "sync")

    def __init__(self, nc, ctx):
        self.nc = nc
        self.ops = {e: [] for e in self.ENGS}
        self.sems = {}
        for s in ("pe", "dve", "act", "gp", "sp", "gdma", "sdma",
                  "sd0", "sd1", "sd2", "sd3", "sd4", "sd5",
                  "gd0", "gd1", "gd2"):
            h = ctx.enter_context(nc.semaphore(s))
            self.sems[s] = [h, 0]
        self.waited = {e: {} for e in self.ENGS}

    def _wait(self, eng, sname, v):
        if v is None:
            v = self.sems[sname][1]
        if v <= 0:
            return
        own = _ENG_SEM[eng]
        if sname == own and not SAME_ENGINE_WAITS:
            return
        w = self.waited[eng]
        if w.get(sname, 0) >= v:
            return
        w[sname] = v
        h = self.sems[sname][0]
        self.ops[eng].append(lambda e, h=h, v=v: e.wait_ge(h, v))

    def waits(self, eng, deps):
        for (sname, v) in deps:
            self._wait(eng, sname, v)

    def op(self, eng, fn, dma=False, dsem=None):
        if dma:
            sname = dsem or ("gdma" if eng == "gpsimd" else "sdma")
            k = 16
            prev = self.sems[sname][1]
            if prev > 0:
                self._wait(eng, sname, prev)
        else:
            sname = _ENG_SEM[eng]
            k = 1
        h = self.sems[sname][0]
        self.ops[eng].append(lambda e, fn=fn, h=h, k=k: fn(e).then_inc(h, k))
        self.sems[sname][1] += k
        if dma:
            return (sname, None)
        return (sname, self.sems[sname][1])

    def finalize(self, block):
        for eng in self.ENGS:
            if self.ops[eng]:
                getattr(block, eng)(
                    lambda e, eng=eng: [f(e) for f in self.ops[eng]])


class Buf:

    def __init__(self):
        self.w = None
        self.w2 = None
        self.r = {}

    def deps_for_read(self):
        d = []
        if self.w:
            d.append(self.w)
        if self.w2:
            d.append(self.w2)
        return d

    def deps_for_write(self):
        d = list(self.r.items())
        if self.w:
            d.append(self.w)
        if self.w2:
            d.append(self.w2)
        return d

    def did_read(self, ms):
        s, v = ms
        cur = self.r.get(s, 0)
        if cur is None:
            return
        if v is None or cur < v:
            self.r[s] = v

    def did_write(self, ms, ms2=None):
        self.w = ms
        self.w2 = ms2
        self.r = {}


def rd(p, eng, fn, *bufs, dma=False):
    for b in bufs:
        p.waits(eng, b.deps_for_read())
    ms = p.op(eng, fn, dma=dma)
    for b in bufs:
        b.did_read(ms)
    return ms


def op(p, eng, fn, reads=(), writes=(), dma=False, dsem=None):
    for b in reads:
        p.waits(eng, b.deps_for_read())
    for b in writes:
        p.waits(eng, b.deps_for_write())
    ms = p.op(eng, fn, dma=dma, dsem=dsem)
    for b in reads:
        b.did_read(ms)
    for b in writes:
        b.did_write(ms)
    return ms


def chunk_ap(t, W, crow, col0, n):
    return bass.AP(t, crow * 128 * W + col0, [[W, 128], [1, n]])


def w_ap(t, din, dout):
    return bass.AP(t, 0, [[dout, 128], [128 * dout, din // 128], [1, dout]])


def bcast_ap(t, offset, npart, n):
    return bass.AP(t, offset, [[1, 1], [0, npart], [1, n]])


def emit_ln(p, env, xsrc, W, col0, ydst_fn, n=512, post_fn=None):
    cs, csb = env["cs"], env["cs_b"]
    c2, c2b = env["cs2"], env["cs2_b"]
    pst, pstb = env["p_stat"], env["p_stat_b"]
    if "p_stat2" in env:
        pst2, pst2b = env["p_stat2"], env["p_stat2_b"]
        pst2ap = pst2[0:1, 0:n]
    else:
        pst2ap = pst[0:1, n:2 * n]
        pst2b = pstb
    ones = env["ones"]
    onesb = env["ones_b"]
    use_xfull = "xfull" in env
    if use_xfull:
        xf = env["xfull"]
        nxf = len(xf)
        xi = env["xfull_i"][0] = (env["xfull_i"][0] + 1) % nxf
        xt_, xtb = xf[xi], env["xfull_b"][xi]
        op(p, "sync", lambda e: e.dma_start(
            out=xt_[:, 0:NCH * n],
            in_=bass.AP(xsrc, col0, [[W, 128], [128 * W, NCH], [1, n]])),
            writes=[xtb], dma=True, dsem=f"sd{xi}")
        xs_of = lambda c: (xt_[:, c * n:(c + 1) * n], xtb)
    else:
        xs, xsb = env["xs"], env["xs_b"]

        def xs_of(c):
            i = c % len(xs)
            op(p, "sync", lambda e, i=i, c=c: e.dma_start(
                out=xs[i][:, 0:n], in_=chunk_ap(xsrc, W, c, col0, n)),
                writes=[xsb[i]], dma=True, dsem=f"sd{i}")
            return xs[i][:, 0:n], xsb[i]
    srcs = []
    for c in range(NCH):
        xap, xb = xs_of(c)
        srcs.append((xap, xb))
        op(p, "vector", lambda e, xap=xap, ci=c % len(cs): e.tensor_copy(
            cs[ci][:, 0:n], xap),
            reads=[xb], writes=[csb[c % len(cs)]])
        op(p, "scalar", lambda e, xap=xap, ci=c % len(c2): e.activation(
            c2[ci][:, 0:n], xap, AF.Square, bias=0.0, scale=1.0),
            reads=[xb], writes=[c2b[c % len(c2)]])
        op(p, "tensor", lambda e, ci=c % len(cs), c=c: e.matmul(
            pst[0:1, 0:n], ones[:, 0:1], cs[ci][:, 0:n],
            start=(c == 0), stop=(c == NCH - 1)),
            reads=[csb[c % len(cs)], onesb],
            writes=([pstb] if c == 0 else ()))
        mst = op(p, "tensor", lambda e, ci=c % len(c2), c=c, pst2ap=pst2ap:
                 e.matmul(
            pst2ap, ones[:, 0:1], c2[ci][:, 0:n],
            start=(c == 0), stop=(c == NCH - 1)),
            reads=[c2b[c % len(c2)], onesb],
            writes=([pst2b] if (c == 0 and pst2b is not pstb) else ()))
    pstb.did_write(mst)
    if pst2b is not pstb:
        pst2b.did_write(mst)
    r1, r1b = env["row1"], env["row1_b"]
    r2, r2b = env["row2"], env["row2_b"]
    r3, r3b = env["row3"], env["row3_b"]
    inv = 1.0 / D
    op(p, "vector", lambda e: e.tensor_scalar(
        r1[0:1, 0:n], pst[0:1, 0:n], inv, None, op0=OP.mult),
        reads=[pstb], writes=[r1b])
    op(p, "vector", lambda e, pst2ap=pst2ap: e.tensor_scalar(
        r2[0:1, 0:n], pst2ap, inv, None, op0=OP.mult),
        reads=[pst2b], writes=[r2b])
    op(p, "vector", lambda e: e.tensor_tensor(
        r3[0:1, 0:n], r1[0:1, 0:n], r1[0:1, 0:n], op=OP.mult),
        reads=[r1b], writes=[r3b])
    op(p, "vector", lambda e: e.tensor_tensor(
        r2[0:1, 0:n], r2[0:1, 0:n], r3[0:1, 0:n], op=OP.subtract),
        reads=[r3b], writes=[r2b])
    op(p, "vector", lambda e: e.tensor_scalar(
        r2[0:1, 0:n], r2[0:1, 0:n], 1e-5, None, op0=OP.add),
        reads=[r2b], writes=[r2b])
    op(p, "scalar", lambda e: e.activation(
        r3[0:1, 0:n], r2[0:1, 0:n], AF.Sqrt, bias=0.0, scale=1.0),
        reads=[r2b], writes=[r3b])
    op(p, "vector", lambda e: e.reciprocal(r3[0:1, 0:n], r3[0:1, 0:n]),
        reads=[r3b], writes=[r3b])
    op(p, "vector", lambda e: e.tensor_tensor(
        r2[0:1, 0:n], r1[0:1, 0:n], r3[0:1, 0:n], op=OP.mult),
        reads=[r1b, r3b], writes=[r2b])
    Br, Brb = env["B_rstd"], env["B_rstd_b"]
    Bm, Bmb = env["B_mur"], env["B_mur_b"]
    op(p, "sync", lambda e: e.dma_start(
        out=Br[:, 0:n], in_=bcast_ap(r3, 0, 128, n)),
        reads=[r3b], writes=[Brb], dma=True, dsem="gd0")
    op(p, "sync", lambda e: e.dma_start(
        out=Bm[:, 0:n], in_=bcast_ap(r2, 0, 128, n)),
        reads=[r2b], writes=[Bmb], dma=True, dsem="gd0")
    if ydst_fn is None:
        return
    nr, nrb = env["nrm"], env["nrm_b"]
    for c in range(NCH):
        if use_xfull:
            xap, xb = srcs[c]
        else:
            xap, xb = xs_of(c)
        ni = c % len(nr)
        op(p, "vector", lambda e, xap=xap, ni=ni: e.tensor_tensor(
            nr[ni][:, 0:n], xap, Br[:, 0:n], op=OP.mult),
            reads=[xb, Brb], writes=[nrb[ni]])
        dst, dstbuf = ydst_fn(c)
        op(p, "vector", lambda e, ni=ni, dst=dst: e.tensor_tensor(
            dst, nr[ni][:, 0:n], Bm[:, 0:n], op=OP.subtract),
            reads=[nrb[ni], Bmb], writes=([dstbuf] if dstbuf else ()))
        if post_fn is not None:
            post_fn(c)


def make_ln_env(p, ctx, nc, env, nxs=3, nxfull=0):
    sb = lambda name, shp, dt: ctx.enter_context(nc.sbuf_tensor(name, shp, dt))
    env["xs"] = [sb(f"ln_xs{i}", [128, 512], F32) for i in range(nxs)]
    env["xs_b"] = [Buf() for _ in range(nxs)]
    if nxfull:
        env["xfull"] = [sb(f"ln_xf{i}", [128, NCH * 512], F32)
                        for i in range(nxfull)]
        env["xfull_b"] = [Buf() for _ in range(nxfull)]
        env["xfull_i"] = [0]
    env["cs"] = [sb(f"ln_cs{i}", [128, 512], BF16) for i in range(2)]
    env["cs_b"] = [Buf() for _ in range(2)]
    env["cs2"] = [sb(f"ln_c2{i}", [128, 512], BF16) for i in range(2)]
    env["cs2_b"] = [Buf() for _ in range(2)]
    env["nrm"] = [sb(f"ln_nr{i}", [128, 512], F32) for i in range(2)]
    env["nrm_b"] = [Buf() for _ in range(2)]
    env["row1"] = sb("ln_r1", [1, 512], F32)
    env["row1_b"] = Buf()
    env["row2"] = sb("ln_r2", [1, 512], F32)
    env["row2_b"] = Buf()
    env["row3"] = sb("ln_r3", [1, 512], F32)
    env["row3_b"] = Buf()
    env["B_rstd"] = sb("ln_Br", [128, 512], F32)
    env["B_rstd_b"] = Buf()
    env["B_mur"] = sb("ln_Bm", [128, 512], F32)
    env["B_mur_b"] = Buf()
    env["ones"] = sb("ones", [128, 2], BF16)
    env["ones_b"] = Buf()
    ms = p.op("gpsimd", lambda e: e.memset(env["ones"][:, :], 1.0))
    env["ones_b"].did_write(ms)


def build_ffn(ncycles=3, debug=False, gf_tbs=NTB):
    nc = bass.Bass("TRN2", target_bir_lowering=False)
    xt = nc.dram_tensor("xt", [D, S], F32, kind="ExternalInput")
    w1 = nc.dram_tensor("w1", [D, D], F32, kind="ExternalInput")
    w2 = nc.dram_tensor("w2", [D, 512], F32, kind="ExternalInput")
    w3 = nc.dram_tensor("w3", [512, 256], F32, kind="ExternalInput")
    w4 = nc.dram_tensor("w4", [256, 1], F32, kind="ExternalInput")
    gw1 = nc.dram_tensor("gw1", [D, 512], F32, kind="ExternalInput")
    gw2 = nc.dram_tensor("gw2", [512, 256], F32, kind="ExternalInput")
    gw3 = nc.dram_tensor("gw3", [256, 1], F32, kind="ExternalInput")
    b1 = nc.dram_tensor("b1", [128, 8], F32, kind="ExternalInput")
    b2 = nc.dram_tensor("b2", [128, 4], F32, kind="ExternalInput")
    b3 = nc.dram_tensor("b3", [128, 2], F32, kind="ExternalInput")
    b4 = nc.dram_tensor("b4", [1, 1], F32, kind="ExternalInput")
    gb1 = nc.dram_tensor("gb1", [128, 4], F32, kind="ExternalInput")
    gb2 = nc.dram_tensor("gb2", [128, 2], F32, kind="ExternalInput")
    gb3 = nc.dram_tensor("gb3", [1, 1], F32, kind="ExternalInput")
    ar = nc.dram_tensor("ar", [1, 1], F32, kind="ExternalInput")
    upd = nc.dram_tensor("upd", [D, OWN], F32, kind="ExternalOutput")
    if debug:
        dbg_cells = nc.dram_tensor("dbg_cells", [1, 8], F32, kind="ExternalOutput")
        dbg_trow = nc.dram_tensor("dbg_trow", [1, 512], F32, kind="ExternalOutput")
        dbg_h1 = nc.dram_tensor("dbg_h1", [128, 512], F32, kind="ExternalOutput")
        dbg_mean = nc.dram_tensor("dbg_mean", [128, 8], F32, kind="ExternalOutput")
        dbg_ycs = nc.dram_tensor("dbg_ycs", [128, 512], F32, kind="ExternalOutput")
        dbg_red = nc.dram_tensor("dbg_red", [128, 1], F32, kind="ExternalOutput")
        dbg_hcol = nc.dram_tensor("dbg_hcol", [128, 8], F32, kind="ExternalOutput")

    with ExitStack() as ctx:
        sb = lambda name, shp, dt: ctx.enter_context(nc.sbuf_tensor(name, shp, dt))
        p = Prog(nc, ctx)
        env = {}
        make_ln_env(p, ctx, nc, env, nxs=2, nxfull=2)

        state = sb("state", [128, NCH * OWN], F32)
        state_bj = [Buf() for _ in range(NOJ)]
        sbf = sb("sbf", [128, NCH * 512], BF16)
        sbf_b = Buf()
        w1b, w1bb = sb("w1b", [128, NCH * D], BF16), Buf()
        w2b, w2bb = sb("w2b", [128, NCH * 512], BF16), Buf()
        w3b, w3bb = sb("w3b", [128, 4 * 256], BF16), Buf()
        w4b, w4bb = sb("w4b", [128, 2], BF16), Buf()
        gw1b, gw1bb = sb("gw1b", [128, NCH * 512], BF16), Buf()
        gw2b, gw2bb = sb("gw2b", [128, 4 * 256], BF16), Buf()
        gw3b, gw3bb = sb("gw3b", [128, 2], BF16), Buf()
        h1, h1b = sb("h1", [128, NCH * 512], BF16), Buf()
        h2, h2b = sb("h2", [128, 4 * 512], BF16), Buf()
        h3, h3b = sb("h3", [128, 2 * 512], BF16), Buf()
        b1c, b1cb = sb("b1c", [128, 8], F32), Buf()
        b2c, b2cb = sb("b2c", [128, 4], F32), Buf()
        b3c, b3cb = sb("b3c", [128, 2], F32), Buf()
        b4c, b4cb = sb("b4c", [1, 1], F32), Buf()
        gb1c, gb1cb = sb("gb1c", [128, 4], F32), Buf()
        gb2c, gb2cb = sb("gb2c", [128, 2], F32), Buf()
        gb3c, gb3cb = sb("gb3c", [1, 1], F32), Buf()
        arc, arcb = sb("arc", [1, 1], F32), Buf()
        ycs, ycsb = sb("ycs", [128, 512], BF16), Buf()
        junk, junkb = sb("junk", [128, 512], BF16), Buf()
        red, redb = sb("red", [128, 1], F32), Buf()
        colsum, colsumb = sb("colsum", [128, 8], F32), Buf()
        mean_bf, mean_bfb = sb("mean_bf", [128, 8], BF16), Buf()
        gcell, gcellb = sb("gcell", [1, 8], F32), Buf()
        acell, acellb = sb("acell", [1, 1], F32), Buf()
        bcell, bcellb = sb("bcell", [1, 1], F32), Buf()
        hcol, hcolb = sb("hcol", [128, 8], BF16), Buf()
        trow2 = [sb(f"trow{i}", [1, 512], F32) for i in range(2)]
        trowb2 = [Buf() for _ in range(2)]
        nrow2 = [sb(f"nrow{i}", [1, 512], F32) for i in range(2)]
        nrowb2 = [Buf() for _ in range(2)]
        B_cp2 = [sb(f"B_cp{i}", [128, 512], F32) for i in range(2)]
        B_cpb2 = [Buf() for _ in range(2)]
        B_cn2 = [sb(f"B_cn{i}", [128, 512], F32) for i in range(2)]
        B_cnb2 = [Buf() for _ in range(2)]
        tpr, tprb = sb("tpr", [128, 512], F32), Buf()
        t2, t2b = sb("t2", [128, 512], F32), Buf()
        tprg, tprgb = sb("tprg", [128, 512], F32), Buf()
        t2g, t2gb = sb("t2g", [128, 512], F32), Buf()
        us = [sb(f"us{i}", [128, 512], F32) for i in range(2)]
        usb = [Buf() for _ in range(2)]

        P_A = ctx.enter_context(nc.psum_tensor("P_A", [128, 1024], F32))
        P_B = ctx.enter_context(nc.psum_tensor("P_B", [128, 1024], F32))
        P_C = ctx.enter_context(nc.psum_tensor("P_C", [128, 512], F32))
        P_D = ctx.enter_context(nc.psum_tensor("P_D", [2, 1024], F32))
        P_E = ctx.enter_context(nc.psum_tensor("P_E", [128, 512], F32))
        pab = [Buf(), Buf()]
        pbb = [Buf(), Buf()]
        pcb, peb = Buf(), Buf()
        env["p_stat"], env["p_stat_b"] = P_D, Buf()
        pdb = env["p_stat_b"]

        for (dst, dstb, src, di, do) in (
                (w1b, w1bb, w1, D, D), (w2b, w2bb, w2, D, 512),
                (w3b, w3bb, w3, 512, 256), (w4b, w4bb, w4, 256, 1),
                (gw1b, gw1bb, gw1, D, 512), (gw2b, gw2bb, gw2, 512, 256),
                (gw3b, gw3bb, gw3, 256, 1)):
            op(p, "gpsimd", lambda e, dst=dst, src=src, di=di, do=do:
               e.dma_start(out=dst[:, :], in_=w_ap(src, di, do),
                           allow_slow_non_contiguous=(do == 1)),
               writes=[dstb], dma=True, dsem="gd1")
        for (dst, dstb, src) in ((b1c, b1cb, b1), (b2c, b2cb, b2),
                                 (b3c, b3cb, b3), (b4c, b4cb, b4),
                                 (gb1c, gb1cb, gb1), (gb2c, gb2cb, gb2),
                                 (gb3c, gb3cb, gb3), (arc, arcb, ar)):
            op(p, "sync", lambda e, dst=dst, src=src:
               e.dma_start(out=dst[:, :], in_=src[:, :]),
               writes=[dstb], dma=True)

        ms = p.op("gpsimd", lambda e: e.memset(colsum[:, :], 0.0))
        colsumb.did_write(ms)
        for tb in range(gf_tbs):
            own = tb < NOJ
            if own:
                def ydst(c, tb=tb):
                    a = (c * NOJ + tb) * 512
                    return state[:, a:a + 512], None
            else:
                def ydst(c):
                    return ycs[:, :], ycsb

            def post(c, tb=tb, own=own):
                if own:
                    a = (c * NOJ + tb) * 512
                    src_ap = state[:, a:a + 512]
                    tmpb = Buf()
                    tmpb.did_write(("dve", p.sems["dve"][1]))
                    rb = [tmpb]
                else:
                    src_ap = ycs[:, :]
                    rb = [ycsb]
                op(p, "scalar", lambda e, src_ap=src_ap: e.activation(
                    junk[:, :], src_ap, AF.Copy, bias=0.0,
                    scale=1.0, accum_out=red[:, 0:1]),
                    reads=rb, writes=[redb, junkb])
                op(p, "gpsimd", lambda e, c=c: e.tensor_tensor(
                    colsum[:, c:c + 1], colsum[:, c:c + 1], red[:, 0:1], op=OP.add),
                    reads=[redb, colsumb], writes=[colsumb])
            emit_ln(p, env, xt, S, tb * 512, ydst, post_fn=post)
            if own:
                state_bj[tb].did_write(("dve", p.sems["dve"][1]))

        op(p, "vector", lambda e: e.tensor_scalar(
            mean_bf[:, :], colsum[:, :], 1.0 / S, None, op0=OP.mult),
            reads=[colsumb], writes=[mean_bfb])

        for o in range(4):
            for c in range(NCH):
                op(p, "tensor", lambda e, o=o, c=c: e.matmul(
                    P_C[:, o:o + 1],
                    gw1b[:, c * 512 + o * 128: c * 512 + (o + 1) * 128],
                    mean_bf[:, c:c + 1], start=(c == 0), stop=(c == NCH - 1)),
                    reads=[gw1bb, mean_bfb], writes=([pcb] if (o == 0 and c == 0) else ()))
        pcb.did_write(("pe", p.sems["pe"][1]))
        for o in range(4):
            op(p, "scalar", lambda e, o=o: e.activation(
                hcol[:, o:o + 1], P_C[:, o:o + 1], AF.Gelu,
                bias=gb1c[:, o:o + 1], scale=1.0),
                reads=[pcb, gb1cb], writes=([hcolb] if o == 0 else ()))
        hcolb.did_write(("act", p.sems["act"][1]))
        for o in range(2):
            for c in range(4):
                op(p, "tensor", lambda e, o=o, c=c: e.matmul(
                    P_E[:, o:o + 1],
                    gw2b[:, c * 256 + o * 128: c * 256 + (o + 1) * 128],
                    hcol[:, c:c + 1], start=(c == 0), stop=(c == 3)),
                    reads=[gw2bb, hcolb], writes=([peb] if (o == 0 and c == 0) else ()))
        peb.did_write(("pe", p.sems["pe"][1]))
        for o in range(2):
            op(p, "scalar", lambda e, o=o: e.activation(
                hcol[:, 4 + o:5 + o], P_E[:, o:o + 1], AF.Gelu,
                bias=gb2c[:, o:o + 1], scale=1.0),
                reads=[peb, gb2cb], writes=[hcolb])
        for c in range(2):
            op(p, "tensor", lambda e, c=c: e.matmul(
                P_D[0:1, 0:1], gw3b[:, c:c + 1], hcol[:, 4 + c:5 + c],
                start=(c == 0), stop=(c == 1)),
                reads=[gw3bb, hcolb], writes=([pdb] if c == 0 else ()))
        pdb.did_write(("pe", p.sems["pe"][1]))
        op(p, "scalar", lambda e: e.activation(
            gcell[0:1, 0:1], P_D[0:1, 0:1], AF.Sigmoid, bias=gb3c[0:1, 0:1],
            scale=1.0), reads=[pdb, gb3cb], writes=[gcellb])
        op(p, "gpsimd", lambda e: e.tensor_tensor(
            gcell[0:1, 1:2], arc[0:1, 0:1], gcell[0:1, 0:1], op=OP.mult),
            reads=[arcb, gcellb], writes=[gcellb])
        op(p, "gpsimd", lambda e: e.tensor_scalar(
            gcell[0:1, 2:3], gcell[0:1, 1:2], 1.4, None, op0=OP.mult),
            reads=[gcellb], writes=[gcellb])
        op(p, "gpsimd", lambda e: e.tensor_tensor(
            gcell[0:1, 3:4], gcell[0:1, 2:3], arc[0:1, 0:1], op=OP.subtract),
            reads=[gcellb, arcb], writes=[gcellb])
        op(p, "gpsimd", lambda e: e.tensor_scalar(
            acell[0:1, 0:1], gcell[0:1, 3:4], 0.1, None, op0=OP.add),
            reads=[gcellb], writes=[acellb])
        op(p, "gpsimd", lambda e: e.tensor_scalar(
            bcell[0:1, 0:1], arc[0:1, 0:1], 0.6, None, op0=OP.mult),
            reads=[arcb], writes=[bcellb])
        if debug:
            op(p, "sync", lambda e: e.dma_start(out=dbg_cells[0:1, 0:1], in_=gcell[0:1, 0:1]),
               reads=[gcellb], dma=True)
            op(p, "sync", lambda e: e.dma_start(out=dbg_cells[0:1, 1:2], in_=acell[0:1, 0:1]),
               reads=[acellb], dma=True)
            op(p, "sync", lambda e: e.dma_start(out=dbg_cells[0:1, 2:3], in_=bcell[0:1, 0:1]),
               reads=[bcellb], dma=True)
            op(p, "vector", lambda e: e.tensor_copy(env["nrm"][0][:, 0:8], colsum[:, :]),
               reads=[colsumb], writes=[env["nrm_b"][0]])
            op(p, "vector", lambda e: e.tensor_copy(env["nrm"][1][:, 0:8], hcol[:, :]),
               reads=[hcolb], writes=[env["nrm_b"][1]])
            op(p, "sync", lambda e: e.dma_start(out=dbg_hcol[:, :], in_=env["nrm"][1][:, 0:8]),
               reads=[env["nrm_b"][1]], dma=True)
            op(p, "sync", lambda e: e.dma_start(out=dbg_mean[:, :], in_=env["nrm"][0][:, 0:8]),
               reads=[env["nrm_b"][0]], dma=True)


        def coupling_net(cyc, j):
            jp = j % 2
            trow, trowb = trow2[jp], trowb2[jp]
            nrow, nrowb = nrow2[jp], nrowb2[jp]
            for c in range(NCH):
                a = (c * NOJ + j) * 512
                op(p, "vector", lambda e, a=a, c=c: e.tensor_copy(
                    sbf[:, c * 512:(c + 1) * 512], state[:, a:a + 512]),
                    reads=[state_bj[j]], writes=([sbf_b] if c == 0 else ()))
            sbf_b.did_write(("dve", p.sems["dve"][1]))
            for o in range(NCH):
                pt = P_A if o % 2 == 0 else P_B
                ptb = (pab if o % 2 == 0 else pbb)[(o // 2) % 2]
                hf = ((o // 2) % 2) * 512
                for c in range(NCH):
                    op(p, "tensor", lambda e, pt=pt, hf=hf, o=o, c=c: e.matmul(
                        pt[:, hf:hf + 512],
                        w1b[:, c * D + o * 128: c * D + (o + 1) * 128],
                        sbf[:, c * 512:(c + 1) * 512],
                        start=(c == 0), stop=(c == NCH - 1)),
                        reads=[w1bb, sbf_b], writes=([ptb] if c == 0 else ()))
                ptb.did_write(("pe", p.sems["pe"][1]))
                op(p, "scalar", lambda e, pt=pt, hf=hf, o=o: e.activation(
                    h1[:, o * 512:(o + 1) * 512], pt[:, hf:hf + 512],
                    AF.Gelu, bias=b1c[:, o:o + 1], scale=1.0),
                    reads=[ptb, b1cb], writes=([h1b] if o == 0 else ()))
            h1b.did_write(("act", p.sems["act"][1]))
            for o in range(4):
                pt = P_A if o % 2 == 0 else P_B
                ptb = (pab if o % 2 == 0 else pbb)[0]
                for c in range(NCH):
                    op(p, "tensor", lambda e, pt=pt, o=o, c=c: e.matmul(
                        pt[:, 0:512],
                        w2b[:, c * 512 + o * 128: c * 512 + (o + 1) * 128],
                        h1[:, c * 512:(c + 1) * 512],
                        start=(c == 0), stop=(c == NCH - 1)),
                        reads=[w2bb, h1b], writes=([ptb] if c == 0 else ()))
                ptb.did_write(("pe", p.sems["pe"][1]))
                op(p, "scalar", lambda e, pt=pt, o=o: e.activation(
                    h2[:, o * 512:(o + 1) * 512], pt[:, 0:512],
                    AF.Gelu, bias=b2c[:, o:o + 1], scale=1.0),
                    reads=[ptb, b2cb], writes=([h2b] if o == 0 else ()))
            h2b.did_write(("act", p.sems["act"][1]))
            for o in range(2):
                pt = P_A if o % 2 == 0 else P_B
                ptb = (pab if o % 2 == 0 else pbb)[0]
                for c in range(4):
                    op(p, "tensor", lambda e, pt=pt, o=o, c=c: e.matmul(
                        pt[:, 0:512],
                        w3b[:, c * 256 + o * 128: c * 256 + (o + 1) * 128],
                        h2[:, c * 512:(c + 1) * 512],
                        start=(c == 0), stop=(c == 3)),
                        reads=[w3bb, h2b], writes=([ptb] if c == 0 else ()))
                ptb.did_write(("pe", p.sems["pe"][1]))
                op(p, "scalar", lambda e, pt=pt, o=o: e.activation(
                    h3[:, o * 512:(o + 1) * 512], pt[:, 0:512],
                    AF.Gelu, bias=b3c[:, o:o + 1], scale=1.0),
                    reads=[ptb, b3cb], writes=([h3b] if o == 0 else ()))
            h3b.did_write(("act", p.sems["act"][1]))
            for c in range(2):
                op(p, "tensor", lambda e, c=c: e.matmul(
                    P_D[0:1, 0:512], w4b[:, c:c + 1],
                    h3[:, c * 512:(c + 1) * 512],
                    start=(c == 0), stop=(c == 1)),
                    reads=[w4bb, h3b], writes=([pdb] if c == 0 else ()))
            pdb.did_write(("pe", p.sems["pe"][1]))
            op(p, "scalar", lambda e, trow=trow: e.activation(
                trow[0:1, :], P_D[0:1, 0:512], AF.Sigmoid,
                bias=b4c[0:1, 0:1], scale=1.0),
                reads=[pdb, b4cb], writes=[trowb])
            op(p, "vector", lambda e, trow=trow: e.tensor_scalar(
                trow[0:1, :], trow[0:1, :], bcell[0:1, 0:1], acell[0:1, 0:1],
                op0=OP.mult, op1=OP.add),
                reads=[trowb, bcellb, acellb], writes=[trowb])
            op(p, "vector", lambda e, trow=trow, nrow=nrow: e.tensor_scalar(
                nrow[0:1, :], trow[0:1, :], -1.0, None, op0=OP.mult),
                reads=[trowb], writes=[nrowb])
            op(p, "sync", lambda e, trow=trow, jp=jp: e.dma_start(
                out=B_cp2[jp][:, :], in_=bcast_ap(trow, 0, 128, 512)),
                reads=[trowb], writes=[B_cpb2[jp]], dma=True, dsem="gd2")
            op(p, "sync", lambda e, nrow=nrow, jp=jp: e.dma_start(
                out=B_cn2[jp][:, :], in_=bcast_ap(nrow, 0, 128, 512)),
                reads=[nrowb], writes=[B_cnb2[jp]], dma=True, dsem="gd2")

        def twist(cyc, j):
            jp = j % 2
            B_cp, B_cpb = B_cp2[jp], B_cpb2[jp]
            B_cn, B_cnb = B_cn2[jp], B_cnb2[jp]
            for m in range(4):
                eng = "vector"
                tp_, tpb_ = (tpr, tprb) if m < 2 else (tprg, tprgb)
                t_, tb_ = (t2, t2b) if m < 2 else (t2g, t2gb)
                sA, sAb = (B_cp, B_cpb) if m < 2 else (B_cn, B_cnb)
                sB, sBb = (B_cn, B_cnb) if m < 2 else (B_cp, B_cpb)
                a0 = (m * NOJ + j) * 512
                a4 = ((m + 4) * NOJ + j) * 512
                op(p, eng, lambda e, a0=a0, tp_=tp_: e.tensor_copy(
                    tp_[:, :], state[:, a0:a0 + 512]),
                    reads=[state_bj[j]], writes=[tpb_])
                op(p, eng, lambda e, a4=a4, sA=sA, t_=t_: e.tensor_tensor(
                    t_[:, :], state[:, a4:a4 + 512], sA[:, :], op=OP.mult),
                    reads=[state_bj[j], sAb], writes=[tb_])
                op(p, eng, lambda e, a0=a0, t_=t_: e.tensor_tensor(
                    state[:, a0:a0 + 512], state[:, a0:a0 + 512], t_[:, :],
                    op=OP.add), reads=[tb_, state_bj[j]], writes=())
                op(p, eng, lambda e, sB=sB, tp_=tp_, t_=t_: e.tensor_tensor(
                    t_[:, :], tp_[:, :], sB[:, :], op=OP.mult),
                    reads=[tpb_, sBb], writes=[tb_])
                op(p, eng, lambda e, a4=a4, t_=t_: e.tensor_tensor(
                    state[:, a4:a4 + 512], state[:, a4:a4 + 512], t_[:, :],
                    op=OP.add), reads=[tb_, state_bj[j]], writes=())
            state_bj[j].did_write(("dve", p.sems["dve"][1]))

        for cyc in range(ncycles):
            for j in range(NOJ):
                coupling_net(cyc, j)
                if j > 0:
                    twist(cyc, j - 1)
            twist(cyc, NOJ - 1)

        for j in range(NOJ):
            for c in range(NCH):
                i = c % 2
                u = c % 2
                a = (c * NOJ + j) * 512
                op(p, "sync", lambda e, i=i, c=c, j=j: e.dma_start(
                    out=env["xs"][i][:, :], in_=chunk_ap(xt, S, c, j * 512, 512)),
                    writes=[env["xs_b"][i]], dma=True, dsem=f"sd{i}")
                op(p, "vector", lambda e, i=i, u=u, a=a: e.tensor_tensor(
                    us[u][:, :], env["xs"][i][:, :], state[:, a:a + 512],
                    op=OP.add),
                    reads=[env["xs_b"][i], state_bj[j]], writes=[usb[u]])
                op(p, "sync", lambda e, u=u, c=c, j=j: e.dma_start(
                    out=bass.AP(upd, c * 128 * OWN + j * 512, [[OWN, 128], [1, 512]]),
                    in_=us[u][:, :]),
                    reads=[usb[u]], dma=True)
        p._wait("sync", "sdma", p.sems["sdma"][1])

        with nc.Block("ffn") as blk:
            p.finalize(blk)
    return nc


def _prep_bias_pc(b, nch):
    return np.ascontiguousarray(b.reshape(nch, 128).T).astype(np.float32)


def _own_cols(r):
    idx = []
    for j in range(NOJ):
        g = 2 * j + r
        idx.append(np.arange(g * 512, (g + 1) * 512))
    return np.concatenate(idx)


def run_ffn_launch(nc_ffn, xt_state, weights, want_res=False):
    in_maps = []
    for core in range(8):
        b, r = core // 2, core % 2
        cols = _own_cols(r)
        perm = np.concatenate([cols, _own_cols(1 - r)])
        m = {
            "xt": np.ascontiguousarray(xt_state[b][:, perm]),
            "w1": weights["cn_w1"], "w2": weights["cn_w2"],
            "w3": weights["cn_w3"], "w4": weights["cn_w4"],
            "gw1": weights["gc_w1"], "gw2": weights["gc_w2"],
            "gw3": weights["gc_w3"],
            "b1": _prep_bias_pc(weights["cn_b1"], 8),
            "b2": _prep_bias_pc(weights["cn_b2"], 4),
            "b3": _prep_bias_pc(weights["cn_b3"], 2),
            "b4": weights["cn_b4"].reshape(1, 1).astype(np.float32),
            "gb1": _prep_bias_pc(weights["gc_b1"], 4),
            "gb2": _prep_bias_pc(weights["gc_b2"], 2),
            "gb3": weights["gc_b3"].reshape(1, 1).astype(np.float32),
            "ar": weights["adaptive_range"].reshape(1, 1).astype(np.float32),
        }
        in_maps.append(m)
    res = run_bass_kernel_spmd(nc_ffn, in_maps, core_ids=list(range(8)))
    out = [x.copy() for x in xt_state]
    for core in range(8):
        b, r = core // 2, core % 2
        u = res.results[core]["upd"]
        for j in range(NOJ):
            g = 2 * j + r
            out[b][:, g * 512:(g + 1) * 512] = u[:, j * 512:(j + 1) * 512]
    if want_res:
        return out, res
    return out




def build_attn_a():
    nc = bass.Bass("TRN2", target_bir_lowering=False)
    xt = nc.dram_tensor("xt", [D, S], F32, kind="ExternalInput")
    xo = nc.dram_tensor("xo", [D, OWN], F32, kind="ExternalInput")
    masks = nc.dram_tensor("masks", [128, 8 * 512], BF16, kind="ExternalInput")
    wq = nc.dram_tensor("wq", [D, D], F32, kind="ExternalInput")
    wk = nc.dram_tensor("wk", [D, D], F32, kind="ExternalInput")
    wv = nc.dram_tensor("wv", [D, D], F32, kind="ExternalInput")
    wo = nc.dram_tensor("wo", [D, D], F32, kind="ExternalInput")
    bq = nc.dram_tensor("bq", [128, 8], F32, kind="ExternalInput")
    bk = nc.dram_tensor("bk", [128, 8], F32, kind="ExternalInput")
    bv = nc.dram_tensor("bv", [128, 8], F32, kind="ExternalInput")
    bo = nc.dram_tensor("bo", [128, 8], F32, kind="ExternalInput")
    upd = nc.dram_tensor("upd", [D, OWN], F32, kind="ExternalOutput")
    K_dram = nc.dram_tensor("K_dram", [NCH, 128, S], BF16, kind="Internal")

    with ExitStack() as ctx:
        sb = lambda name, shp, dt: ctx.enter_context(nc.sbuf_tensor(name, shp, dt))
        p = Prog(nc, ctx)
        env = {}
        make_ln_env(p, ctx, nc, env, nxs=2, nxfull=1)

        kslot = [sb(f"kslot{i}", [128, 512], BF16) for i in range(2)]
        kslotb = [Buf() for _ in range(2)]
        Kst = [sb(f"Kst{i}", [128, S], BF16) for i in range(2)]
        Kstb = [Buf() for _ in range(2)]
        Kdb = Buf()
        V_sb, Vb = sb("V_sb", [128, 32 * D], BF16), Buf()
        qbuf, Qb = sb("qbuf", [128, NCH * 512], BF16), Buf()
        ybuf2 = [sb(f"ybuf{i}", [128, NCH * 512], BF16) for i in range(2)]
        Yb2 = [Buf() for i in range(2)]
        wA, wAb = sb("wA", [128, NCH * D], BF16), Buf()
        wB, wBb = sb("wB", [128, NCH * D], BF16), Buf()
        msk, mskb = sb("msk", [128, 8 * 512], BF16), Buf()
        Ebuf = sb("Ebuf", [128, 3 * 1024], BF16)
        Eb = [Buf() for _ in range(3)]
        Z_sb, Zb = sb("Z_sb", [128, NCH * 512], BF16), Buf()
        Brec, Brecb = sb("Brec", [128, 512], F32), Buf()
        Brec2, Brec2b = sb("Brec2", [128, 512], F32), Buf()
        rrow, rrowb = sb("rrow", [1, 512], F32), Buf()
        rrow2, rrow2b = sb("rrow2", [1, 512], F32), Buf()
        bq_c, bq_cb = sb("bq_c", [128, 8], F32), Buf()
        bk_c, bk_cb = sb("bk_c", [128, 8], F32), Buf()
        bo_c, bo_cb = sb("bo_c", [128, 8], F32), Buf()
        bv_bf, bv_bfb = sb("bv_bf", [128, 8], BF16), Buf()
        btot, btotb = sb("btot", [128, 8], F32), Buf()
        us = [env["nrm"][i] for i in range(2)]
        usb = [env["nrm_b"][i] for i in range(2)]

        P_A = ctx.enter_context(nc.psum_tensor("P_A", [128, 1024], F32))
        P_B = ctx.enter_context(nc.psum_tensor("P_B", [128, 1024], F32))
        P_C = ctx.enter_context(nc.psum_tensor("P_C", [128, 512], F32))
        P_D = ctx.enter_context(nc.psum_tensor("P_D", [2, 1024], F32))
        P_E = ctx.enter_context(nc.psum_tensor("P_E", [128, 512], F32))
        pab, pbb, pcb, peb = Buf(), Buf(), Buf(), Buf()
        env["p_stat"], env["p_stat_b"] = P_D, Buf()
        pdb = env["p_stat_b"]
        rot = [(P_A, 0, pab), (P_A, 512, pab), (P_B, 0, pbb), (P_B, 512, pbb)]
        rotb = [Buf() for _ in range(4)]

        op(p, "gpsimd", lambda e: e.dma_start(out=wA[:, :], in_=w_ap(wk, D, D)),
           writes=[wAb], dma=True, dsem="gd1")
        op(p, "gpsimd", lambda e: e.dma_start(out=wB[:, :], in_=w_ap(wv, D, D)),
           writes=[wBb], dma=True, dsem="gd1")
        op(p, "gpsimd", lambda e: e.dma_start(out=bv_bf[:, :], in_=bv[:, :]),
           writes=[bv_bfb], dma=True, dsem="gd1")
        op(p, "sync", lambda e: e.dma_start(out=msk[:, :], in_=masks[:, :]),
           writes=[mskb], dma=True)
        for (dst, dstb, src) in ((bq_c, bq_cb, bq), (bk_c, bk_cb, bk),
                                 (bo_c, bo_cb, bo)):
            op(p, "sync", lambda e, dst=dst, src=src:
               e.dma_start(out=dst[:, :], in_=src[:, :]), writes=[dstb], dma=True)

        for tb in range(NTB):
            ybuf, Yb = ybuf2[tb % 2], Yb2[tb % 2]
            def ydst(c, ybuf=ybuf, Yb=Yb):
                return ybuf[:, c * 512:(c + 1) * 512], (Yb if c == 0 else None)
            emit_ln(p, env, xt, S, tb * 512, ydst)
            Yb.did_write(("dve", p.sems["dve"][1]))
            for o in range(NCH):
                pt, off, ptb = rot[o % 4][0], rot[o % 4][1], rotb[o % 4]
                for c in range(NCH):
                    op(p, "tensor", lambda e, pt=pt, off=off, o=o, c=c, ybuf=ybuf:
                       e.matmul(
                        pt[:, off:off + 512],
                        wA[:, c * D + o * 128: c * D + (o + 1) * 128],
                        ybuf[:, c * 512:(c + 1) * 512],
                        start=(c == 0), stop=(c == NCH - 1)),
                        reads=[wAb, Yb], writes=([ptb] if c == 0 else ()))
                ptb.did_write(("pe", p.sems["pe"][1]))
                ks = o % 2
                op(p, "scalar", lambda e, pt=pt, off=off, o=o, ks=ks: e.activation(
                    kslot[ks][:, :], pt[:, off:off + 512], AF.Identity,
                    bias=bk_c[:, o:o + 1], scale=1.0),
                    reads=[ptb, bk_cb], writes=[kslotb[ks]])
                op(p, "sync", lambda e, o=o, tb=tb, ks=ks: e.dma_start(
                    out=bass.AP(K_dram, o * 128 * S + tb * 512,
                                [[S, 128], [1, 512]]),
                    in_=kslot[ks][:, :]),
                    reads=[kslotb[ks]], writes=[Kdb], dma=True, dsem="sd5")
            for tt in range(4):
                for dblk in range(2):
                    k4 = (tt * 2 + dblk) % 4
                    pt, off, ptb = rot[k4][0], rot[k4][1], rotb[k4]
                    for c in range(NCH):
                        op(p, "tensor", lambda e, pt=pt, off=off, tt=tt, dblk=dblk,
                           c=c, ybuf=ybuf: e.matmul(
                            pt[:, off:off + 512],
                            ybuf[:, c * 512 + tt * 128: c * 512 + (tt + 1) * 128],
                            wB[:, c * D + dblk * 512: c * D + (dblk + 1) * 512],
                            start=(c == 0), stop=(c == NCH - 1)),
                            reads=[wBb, Yb], writes=([ptb] if c == 0 else ()))
                    ptb.did_write(("pe", p.sems["pe"][1]))
                    op(p, "scalar", lambda e, pt=pt, off=off, tt=tt, dblk=dblk, tb=tb:
                       e.activation(
                        V_sb[:, (tb * 4 + tt) * D + dblk * 512:
                             (tb * 4 + tt) * D + (dblk + 1) * 512],
                        pt[:, off:off + 512], AF.Copy, bias=0.0, scale=1.0),
                        reads=[ptb], writes=[Vb])

        op(p, "gpsimd", lambda e: e.dma_start(out=wA[:, :], in_=w_ap(wq, D, D)),
           writes=[wAb], dma=True, dsem="gd1")
        op(p, "gpsimd", lambda e: e.dma_start(out=wB[:, :], in_=w_ap(wo, D, D)),
           writes=[wBb], dma=True, dsem="gd1")
        for o in range(NCH):
            for c in range(NCH):
                op(p, "tensor", lambda e, o=o, c=c: e.matmul(
                    P_E[:, o:o + 1],
                    wB[:, c * D + o * 128: c * D + (o + 1) * 128],
                    bv_bf[:, c:c + 1], start=(c == 0), stop=(c == NCH - 1)),
                    reads=[wBb, bv_bfb],
                    writes=([peb] if (o == 0 and c == 0) else ()))
        peb.did_write(("pe", p.sems["pe"][1]))
        op(p, "vector", lambda e: e.tensor_tensor(
            btot[:, :], P_E[:, 0:8], bo_c[:, :], op=OP.add),
            reads=[peb, bo_cb], writes=[btotb])

        for j in range(NOJ):
            ybuf, Yb = ybuf2[j % 2], Yb2[j % 2]
            def ydst(c, ybuf=ybuf, Yb=Yb):
                return ybuf[:, c * 512:(c + 1) * 512], (Yb if c == 0 else None)
            emit_ln(p, env, xo, OWN, j * 512, ydst)
            Yb.did_write(("dve", p.sems["dve"][1]))
            for o in range(NCH):
                pt, off, ptb = rot[o % 4][0], rot[o % 4][1], rotb[o % 4]
                for c in range(NCH):
                    op(p, "tensor", lambda e, pt=pt, off=off, o=o, c=c, ybuf=ybuf:
                       e.matmul(
                        pt[:, off:off + 512],
                        wA[:, c * D + o * 128: c * D + (o + 1) * 128],
                        ybuf[:, c * 512:(c + 1) * 512],
                        start=(c == 0), stop=(c == NCH - 1)),
                        reads=[wAb, Yb], writes=([ptb] if c == 0 else ()))
                ptb.did_write(("pe", p.sems["pe"][1]))
                op(p, "scalar", lambda e, pt=pt, off=off, o=o: e.activation(
                    qbuf[:, o * 512:(o + 1) * 512], pt[:, off:off + 512],
                    AF.Identity, bias=bq_c[:, o:o + 1], scale=1.0),
                    reads=[ptb, bq_cb], writes=[Qb])
            Qb.did_write(("act", p.sems["act"][1]))
            nst = 8 * j + 8
            op(p, "sync", lambda e, nst=nst: e.dma_start(
                out=Kst[0][:, 0:nst * 128],
                in_=bass.AP(K_dram, 0, [[S, 128], [1, nst * 128]])),
                reads=[Kdb], writes=[Kstb[0]], dma=True, dsem="sd3")
            for hp in range(NCH):
                kb = hp % 2
                if hp + 1 < NCH:
                    op(p, "sync", lambda e, hp=hp, nst=nst: e.dma_start(
                        out=Kst[(hp + 1) % 2][:, 0:nst * 128],
                        in_=bass.AP(K_dram, (hp + 1) * 128 * S,
                                    [[S, 128], [1, nst * 128]])),
                        reads=[Kdb], writes=[Kstb[(hp + 1) % 2]], dma=True,
                        dsem=f"sd{3 + (hp + 1) % 2}")
                pz, pzb = (P_C, pcb) if hp % 2 == 0 else (P_E, peb)

                def sc_group(st, kb=kb, hp=hp, j=j):
                    psc, pscb = (P_A, pab) if st % 2 == 0 else (P_B, pbb)
                    op(p, "tensor", lambda e, psc=psc, kb=kb, hp=hp, st=st: e.matmul(
                        psc[:, 0:512],
                        Kst[kb][0:64, st * 128:(st + 1) * 128],
                        qbuf[0:64, hp * 512:(hp + 1) * 512],
                        start=True, stop=True, tile_position=(0, 0)),
                        reads=[Kstb[kb], Qb], writes=[pscb])
                    op(p, "tensor", lambda e, psc=psc, kb=kb, hp=hp, st=st: e.matmul(
                        psc[:, 512:1024],
                        Kst[kb][64:128, st * 128:(st + 1) * 128],
                        qbuf[64:128, hp * 512:(hp + 1) * 512],
                        start=True, stop=True, tile_position=(64, 0)),
                        reads=[Kstb[kb], Qb])
                    pscb.did_write(("pe", p.sems["pe"][1]))
                    if st >= 8 * j:
                        m = st - 8 * j
                        op(p, "vector", lambda e, psc=psc, m=m: e.tensor_tensor(
                            psc[:, 0:512], psc[:, 0:512],
                            msk[:, m * 512:(m + 1) * 512], op=OP.add),
                            reads=[mskb, pscb], writes=())
                        op(p, "vector", lambda e, psc=psc, m=m: e.tensor_tensor(
                            psc[:, 512:1024], psc[:, 512:1024],
                            msk[:, m * 512:(m + 1) * 512], op=OP.add),
                            reads=[mskb], writes=())
                        pscb.did_write(("dve", p.sems["dve"][1]))

                def consume(st, hp=hp, nst=nst, pz=pz, pzb=pzb):
                    psc, pscb = (P_A, pab) if st % 2 == 0 else (P_B, pbb)
                    es = st % 3
                    op(p, "scalar", lambda e, psc=psc, es=es: e.activation(
                        Ebuf[:, es * 1024:(es + 1) * 1024], psc[:, 0:1024],
                        AF.Exp, bias=0.0, scale=0.125),
                        reads=[pscb], writes=[Eb[es]])
                    op(p, "tensor", lambda e, es=es, st=st, nst=nst: e.matmul(
                        P_D[0:1, 0:512], env["ones"][:, 0:1],
                        Ebuf[:, es * 1024: es * 1024 + 512],
                        start=(st == 0), stop=(st == nst - 1)),
                        reads=[Eb[es], env["ones_b"]],
                        writes=([pdb] if st == 0 else ()))
                    op(p, "tensor", lambda e, es=es, st=st, nst=nst: e.matmul(
                        P_D[0:1, 512:1024], env["ones"][:, 0:1],
                        Ebuf[:, es * 1024 + 512: es * 1024 + 1024],
                        start=(st == 0), stop=(st == nst - 1)),
                        reads=[Eb[es], env["ones_b"]])
                    op(p, "tensor", lambda e, es=es, st=st, hp=hp, nst=nst, pz=pz:
                       e.matmul(
                        pz[0:64, 0:512],
                        V_sb[:, st * D + (2 * hp) * 64: st * D + (2 * hp) * 64 + 64],
                        Ebuf[:, es * 1024: es * 1024 + 512],
                        start=(st == 0), stop=(st == nst - 1),
                        tile_position=(0, 0), skip_group_check=True),
                        reads=[Eb[es], Vb], writes=([pzb] if st == 0 else ()))
                    op(p, "tensor", lambda e, es=es, st=st, hp=hp, nst=nst, pz=pz:
                       e.matmul(
                        pz[64:128, 0:512],
                        V_sb[:, st * D + (2 * hp + 1) * 64:
                             st * D + (2 * hp + 1) * 64 + 64],
                        Ebuf[:, es * 1024 + 512: es * 1024 + 1024],
                        start=(st == 0), stop=(st == nst - 1),
                        tile_position=(0, 64), skip_group_check=True),
                        reads=[Eb[es], Vb])

                sc_group(0)
                for st in range(nst):
                    if st + 1 < nst:
                        sc_group(st + 1)
                    consume(st)
                pdb.did_write(("pe", p.sems["pe"][1]))
                pzb.did_write(("pe", p.sems["pe"][1]))
                op(p, "vector", lambda e: e.reciprocal(
                    rrow[0:1, :], P_D[0:1, 0:512]), reads=[pdb], writes=[rrowb])
                op(p, "vector", lambda e: e.reciprocal(
                    rrow2[0:1, :], P_D[0:1, 512:1024]), reads=[pdb], writes=[rrow2b])
                op(p, "sync", lambda e: e.dma_start(
                    out=Brec[:, :], in_=bcast_ap(rrow, 0, 128, 512)),
                    reads=[rrowb], writes=[Brecb], dma=True, dsem="gd2")
                op(p, "sync", lambda e: e.dma_start(
                    out=Brec2[:, :], in_=bcast_ap(rrow2, 0, 128, 512)),
                    reads=[rrow2b], writes=[Brec2b], dma=True, dsem="gd2")
                op(p, "vector", lambda e, hp=hp, pz=pz: e.tensor_tensor(
                    Z_sb[0:64, hp * 512:(hp + 1) * 512], pz[0:64, 0:512],
                    Brec[0:64, :], op=OP.mult),
                    reads=[pzb, Brecb], writes=[Zb])
                op(p, "vector", lambda e, hp=hp, pz=pz: e.tensor_tensor(
                    Z_sb[64:128, hp * 512:(hp + 1) * 512], pz[64:128, 0:512],
                    Brec2[64:128, :], op=OP.mult),
                    reads=[pzb, Brec2b], writes=())
                Zb.did_write(("dve", p.sems["dve"][1]))
            for o in range(NCH):
                for hc in range(NCH):
                    op(p, "tensor", lambda e, o=o, hc=hc: e.matmul(
                        P_A[:, 0:512],
                        wB[:, hc * D + o * 128: hc * D + (o + 1) * 128],
                        Z_sb[:, hc * 512:(hc + 1) * 512],
                        start=(hc == 0), stop=(hc == NCH - 1)),
                        reads=[wBb, Zb], writes=([pab] if hc == 0 else ()))
                pab.did_write(("pe", p.sems["pe"][1]))
                i = o % 2
                u = o % 2
                op(p, "sync", lambda e, i=i, o=o, j=j: e.dma_start(
                    out=env["xs"][i][:, :], in_=chunk_ap(xo, OWN, o, j * 512, 512)),
                    writes=[env["xs_b"][i]], dma=True, dsem=f"sd{i}")
                op(p, "vector", lambda e, i=i, u=u, o=o: e.scalar_tensor_tensor(
                    us[u][:, :], P_A[:, 0:512], btot[:, o:o + 1],
                    env["xs"][i][:, :], op0=OP.add, op1=OP.add),
                    reads=[pab, btotb, env["xs_b"][i]], writes=[usb[u]])
                op(p, "sync", lambda e, u=u, o=o, j=j: e.dma_start(
                    out=bass.AP(upd, o * 128 * OWN + j * 512, [[OWN, 128], [1, 512]]),
                    in_=us[u][:, :]),
                    reads=[usb[u]], dma=True)
        p._wait("sync", "sdma", p.sems["sdma"][1])

        with nc.Block("attn_a") as blk:
            p.finalize(blk)
    return nc


def make_masks(r):
    out = np.zeros((128, 8 * 512), np.float32)
    i = np.arange(128)[:, None]
    jc = np.arange(512)[None, :]
    for m in range(8):
        if r == 0:
            if m < 4:
                pat = np.where(128 * m + i <= jc, 0.0, NEG)
            else:
                pat = np.full((128, 512), NEG, np.float32)
        else:
            if m < 4:
                pat = np.zeros((128, 512), np.float32)
            else:
                pat = np.where(128 * (m - 4) + i <= jc, 0.0, NEG)
        out[:, m * 512:(m + 1) * 512] = pat
    return out.astype(BFD)


def run_attn_a_launch(nc_a, xt_state, weights):
    in_maps = []
    for core in range(8):
        b, r = core // 2, core % 2
        cols = _own_cols(r)
        m = {
            "xt": xt_state[b],
            "xo": np.ascontiguousarray(xt_state[b][:, cols]),
            "masks": make_masks(r),
            "wq": weights["Wq"], "wk": weights["Wk"],
            "wv": weights["Wv"], "wo": weights["Wo"],
            "bq": _prep_bias_pc(weights["bq"], 8),
            "bk": _prep_bias_pc(weights["bk"], 8),
            "bv": _prep_bias_pc(weights["bv"], 8),
            "bo": _prep_bias_pc(weights["bo"], 8),
        }
        in_maps.append(m)
    res = run_bass_kernel_spmd(nc_a, in_maps, core_ids=list(range(8)))
    out = [x.copy() for x in xt_state]
    for core in range(8):
        b, r = core // 2, core % 2
        u = res.results[core]["upd"]
        for j in range(NOJ):
            g = 2 * j + r
            out[b][:, g * 512:(g + 1) * 512] = u[:, j * 512:(j + 1) * 512]
    return out




def build_attn_b():
    nc = bass.Bass("TRN2", target_bir_lowering=False)
    xt = nc.dram_tensor("xt", [D, S], F32, kind="ExternalInput")
    xo = nc.dram_tensor("xo", [D, OWN], F32, kind="ExternalInput")
    w = nc.dram_tensor("w", [H * D, D], F32, kind="ExternalInput")
    upd = nc.dram_tensor("upd", [D, OWN], F32, kind="ExternalOutput")
    wbf = nc.dram_tensor("wbf", [H, 128, NCH * D], BF16, kind="Internal")
    TBB = 256
    NJ2 = OWN // TBB

    with ExitStack() as ctx:
        sb = lambda name, shp, dt: ctx.enter_context(nc.sbuf_tensor(name, shp, dt))
        p = Prog(nc, ctx)
        env = {}
        make_ln_env(p, ctx, nc, env, nxs=2)

        Y_sb, Yb = sb("Y_sb", [128, NCH * S], BF16), Buf()
        Ytm, Ytmb = sb("Ytm", [128, 32 * D], BF16), Buf()
        yq, yqb = sb("yq", [128, NCH * TBB], BF16), Buf()
        Ebuf = sb("Ebuf", [128, 6 * TBB], BF16)
        Eb = [Buf() for _ in range(3)]
        G_sb = [sb(f"G_sb{i}", [128, NCH * TBB], BF16) for i in range(2)]
        Gb = [Buf() for _ in range(2)]
        outacc, oaccb = sb("outacc", [128, NCH * TBB], F32), Buf()
        wh = [sb(f"wh{i}", [128, NCH * D], BF16) for i in range(2)]
        whb = [Buf() for _ in range(2)]
        wdb = Buf()
        Brec, Brecb = sb("Brec", [128, TBB], F32), Buf()
        rrow, rrowb = sb("rrow", [1, TBB], F32), Buf()
        ident, identb = sb("ident", [128, 128], BF16), Buf()
        onesb_t, onesb_tb = sb("ones_bf", [128, 128], BF16), Buf()
        us = [env["nrm"][i] for i in range(2)]
        usb = [env["nrm_b"][i] for i in range(2)]

        P_A = ctx.enter_context(nc.psum_tensor("P_A", [128, 1024], F32))
        P_B = ctx.enter_context(nc.psum_tensor("P_B", [128, 1024], F32))
        P_C = ctx.enter_context(nc.psum_tensor("P_C", [128, 512], F32))
        P_E = ctx.enter_context(nc.psum_tensor("P_E", [128, 512], F32))
        P_F = ctx.enter_context(nc.psum_tensor("P_F", [128, 512], F32))
        P_D = ctx.enter_context(nc.psum_tensor("P_D", [2, 512], F32))
        pab, pbb, pcb, peb, pfb = Buf(), Buf(), Buf(), Buf(), Buf()
        env["p_stat"], env["p_stat_b"] = P_D, Buf()
        pdb = env["p_stat_b"]
        env["p_stat2"], env["p_stat2_b"] = P_F, pfb

        ms = p.op("gpsimd", lambda e: e.memset(onesb_t[:, :], 1.0))
        onesb_tb.did_write(ms)
        op(p, "gpsimd", lambda e: e.affine_select(
            ident[:, :], onesb_t[:, :], [[1, 128]], OP.is_equal, 0.0,
            base=0, channel_multiplier=-1),
            reads=[onesb_tb], writes=[identb])

        for h in range(H):
            k = h % 2
            op(p, "gpsimd", lambda e, h=h, k=k: e.dma_start(
                out=wh[k][:, :],
                in_=bass.AP(w, h * D * D, [[D, 128], [128 * D, NCH], [1, D]])),
                writes=[whb[k]], dma=True, dsem="gd1")
            op(p, "sync", lambda e, h=h, k=k: e.dma_start(
                out=bass.AP(wbf, h * 128 * NCH * D, [[NCH * D, 128], [1, NCH * D]]),
                in_=wh[k][:, :]),
                reads=[whb[k]], writes=([wdb] if h == 0 else ()), dma=True, dsem="sd5")
        wdb.did_write(("sd5", None))

        for tb in range(NTB):
            def ydst(c, tb=tb):
                return (Y_sb[:, c * S + tb * 512: c * S + (tb + 1) * 512],
                        (Yb if (c == 0 and tb == 0) else None))
            emit_ln(p, env, xt, S, tb * 512, ydst)
        Yb.did_write(("dve", p.sems["dve"][1]))

        trb = [pcb, peb]
        trt = [P_C, P_E]
        for tt in range(32):
            for c in range(NCH):
                k = (tt * NCH + c) % 2
                pv = trt[k][:, 0:64].bitcast(BF16)
                op(p, "tensor", lambda e, pv=pv, tt=tt, c=c: e.transpose(
                    pv, Y_sb[:, c * S + tt * 128: c * S + (tt + 1) * 128],
                    ident[:, :]),
                    reads=[Yb, identb], writes=[trb[k]])
                op(p, "vector", lambda e, pv=pv, tt=tt, c=c: e.tensor_copy(
                    Ytm[:, tt * D + c * 128: tt * D + (c + 1) * 128], pv),
                    reads=[trb[k]], writes=([Ytmb] if (tt == 0 and c == 0) else ()))
        Ytmb.did_write(("dve", p.sems["dve"][1]))

        def load_wh(h):
            k = h % 2
            op(p, "sync", lambda e, h=h, k=k: e.dma_start(
                out=wh[k][:, :],
                in_=bass.AP(wbf, h * 128 * NCH * D,
                            [[NCH * D, 128], [1, NCH * D]])),
                reads=[wdb], writes=[whb[k]], dma=True, dsem=f"sd{3 + k}")

        for j2 in range(NJ2):
            def ydst(c):
                return (yq[:, c * TBB:(c + 1) * TBB], (yqb if c == 0 else None))
            emit_ln(p, env, xo, OWN, j2 * TBB, ydst, n=TBB)
            yqb.did_write(("dve", p.sems["dve"][1]))
            op(p, "gpsimd", lambda e: e.memset(outacc[:, :], 0.0),
               writes=[oaccb])
            if j2 == 0:
                load_wh(0)

            def g_phase(h):
                gb = h % 2
                hr = (h % 2) * 64
                hc = h // 2

                def sc_pair(q):
                    psc, pscb2 = ((P_C, pcb), (P_E, peb), (P_F, pfb))[q % 3]
                    for k2 in range(2):
                        st = 2 * q + k2
                        op(p, "tensor", lambda e, psc=psc, hr=hr, hc=hc, st=st,
                           k2=k2: e.matmul(
                            psc[:, k2 * TBB:(k2 + 1) * TBB],
                            Y_sb[hr:hr + 64,
                                 hc * S + st * 128: hc * S + (st + 1) * 128],
                            yq[hr:hr + 64, hc * TBB:(hc + 1) * TBB],
                            start=True, stop=True,
                            tile_position=(hr, 0)),
                            reads=[Yb, yqb],
                            writes=([pscb2] if k2 == 0 else ()))
                    pscb2.did_write(("pe", p.sems["pe"][1]))

                def consume_pair(q):
                    psc, pscb2 = ((P_C, pcb), (P_E, peb), (P_F, pfb))[q % 3]
                    es = q % 3
                    op(p, "scalar", lambda e, psc=psc, es=es: e.activation(
                        Ebuf[:, es * 512:(es + 1) * 512], psc[:, 0:512],
                        AF.Exp, bias=0.0, scale=0.125),
                        reads=[pscb2], writes=[Eb[es]])
                    for k2 in range(2):
                        st = 2 * q + k2
                        eoff = es * 512 + k2 * TBB
                        op(p, "tensor", lambda e, eoff=eoff, st=st: e.matmul(
                            P_D[0:1, 0:TBB], env["ones"][:, 0:1],
                            Ebuf[:, eoff:eoff + TBB],
                            start=(st == 0), stop=(st == 31)),
                            reads=[Eb[es], env["ones_b"]],
                            writes=([pdb] if st == 0 else ()))
                        for c in range(NCH):
                            pt = P_A if c < 4 else P_B
                            ptb = pab if c < 4 else pbb
                            off = (c % 4) * TBB
                            op(p, "tensor", lambda e, pt=pt, off=off, st=st,
                               c=c, eoff=eoff: e.matmul(
                                pt[:, off:off + TBB],
                                Ytm[:, st * D + c * 128: st * D + (c + 1) * 128],
                                Ebuf[:, eoff:eoff + TBB],
                                start=(st == 0 and c % 2 == 0),
                                stop=(st == 31 and c % 2 == 1)),
                                reads=[Eb[es], Ytmb],
                                writes=([ptb] if (st == 0 and c == 0) else ()))

                sc_pair(0)
                sc_pair(1)
                for q in range(16):
                    if q < 14:
                        sc_pair(q + 2)
                    consume_pair(q)
                pab.did_write(("pe", p.sems["pe"][1]))
                pbb.did_write(("pe", p.sems["pe"][1]))
                pdb.did_write(("pe", p.sems["pe"][1]))
                for b2 in range(4):
                    pt = P_A if b2 < 2 else P_B
                    ptb = pab if b2 < 2 else pbb
                    off = (b2 % 2) * 512
                    op(p, "scalar", lambda e, pt=pt, off=off, b2=b2, gb=gb:
                       e.activation(
                        G_sb[gb][:, b2 * 512:(b2 + 1) * 512],
                        pt[:, off:off + 512], AF.Copy, bias=0.0, scale=1.0),
                        reads=[ptb], writes=([Gb[gb]] if b2 == 0 else ()))
                Gb[gb].did_write(("act", p.sems["act"][1]))

            def norm_phase(h):
                gb = h % 2
                op(p, "vector", lambda e: e.reciprocal(
                    rrow[0:1, :], P_D[0:1, 0:TBB]), reads=[pdb], writes=[rrowb])
                op(p, "sync", lambda e: e.dma_start(
                    out=Brec[:, :], in_=bcast_ap(rrow, 0, 128, TBB)),
                    reads=[rrowb], writes=[Brecb], dma=True, dsem="gd2")
                for c in range(NCH):
                    op(p, "vector", lambda e, c=c, gb=gb: e.tensor_tensor(
                        G_sb[gb][:, c * TBB:(c + 1) * TBB],
                        G_sb[gb][:, c * TBB:(c + 1) * TBB], Brec[:, :],
                        op=OP.mult),
                        reads=[Gb[gb], Brecb], writes=())
                Gb[gb].did_write(("dve", p.sems["dve"][1]))

            def w_phase(h):
                gb = h % 2
                for dc in range(8):
                    pt, ptb = (P_E, peb) if dc % 2 == 0 else (P_F, pfb)
                    for c in range(NCH):
                        op(p, "tensor", lambda e, pt=pt, dc=dc, c=c, gb=gb, h=h:
                           e.matmul(
                            pt[:, 0:TBB],
                            wh[h % 2][:, c * D + dc * 128: c * D + (dc + 1) * 128],
                            G_sb[gb][:, c * TBB:(c + 1) * TBB],
                            start=(c == 0), stop=(c == NCH - 1)),
                            reads=[whb[h % 2], Gb[gb]],
                            writes=([ptb] if c == 0 else ()))
                    ptb.did_write(("pe", p.sems["pe"][1]))
                    op(p, "vector", lambda e, pt=pt, dc=dc: e.tensor_tensor(
                        outacc[:, dc * TBB:(dc + 1) * TBB],
                        outacc[:, dc * TBB:(dc + 1) * TBB],
                        pt[:, 0:TBB], op=OP.add),
                        reads=[ptb, oaccb], writes=())
                oaccb.did_write(("dve", p.sems["dve"][1]))

            for h in range(H):
                g_phase(h)
                if h > 0:
                    w_phase(h - 1)
                if not (j2 == NJ2 - 1 and h == H - 1):
                    load_wh((h + 1) % H)
                norm_phase(h)
            w_phase(H - 1)
            for o in range(NCH):
                i = o % 2
                u = o % 2
                op(p, "sync", lambda e, i=i, o=o, j2=j2: e.dma_start(
                    out=env["xs"][i][:, 0:TBB],
                    in_=chunk_ap(xo, OWN, o, j2 * TBB, TBB)),
                    writes=[env["xs_b"][i]], dma=True, dsem=f"sd{i}")
                op(p, "vector", lambda e, i=i, u=u, o=o: e.tensor_tensor(
                    us[u][:, 0:TBB], env["xs"][i][:, 0:TBB],
                    outacc[:, o * TBB:(o + 1) * TBB], op=OP.add),
                    reads=[env["xs_b"][i], oaccb], writes=[usb[u]])
                op(p, "sync", lambda e, u=u, o=o, j2=j2: e.dma_start(
                    out=bass.AP(upd, o * 128 * OWN + j2 * TBB,
                                [[OWN, 128], [1, TBB]]),
                    in_=us[u][:, 0:TBB]),
                    reads=[usb[u]], dma=True)
        p._wait("sync", "sdma", p.sems["sdma"][1])

        with nc.Block("attn_b") as blk:
            p.finalize(blk)
    return nc


def run_attn_b_launch(nc_b, xt_state, weights):
    in_maps = []
    for core in range(8):
        b, r = core // 2, core % 2
        cols = _own_cols(r)
        in_maps.append({
            "xt": xt_state[b],
            "xo": np.ascontiguousarray(xt_state[b][:, cols]),
            "w": weights["weight"],
        })
    res = run_bass_kernel_spmd(nc_b, in_maps, core_ids=list(range(8)))
    out = [x.copy() for x in xt_state]
    for core in range(8):
        b, r = core // 2, core % 2
        u = res.results[core]["upd"]
        for j in range(NOJ):
            g = 2 * j + r
            out[b][:, g * 512:(g + 1) * 512] = u[:, j * 512:(j + 1) * 512]
    return out


_NC_CACHE = {}


def _get_nc(name, builder):
    if name not in _NC_CACHE:
        _NC_CACHE[name] = builder()
    return _NC_CACHE[name]


def kernel(**inputs):
    x = np.asarray(inputs["L0_attn_a_x"], np.float32)
    xt_state = [np.ascontiguousarray(x[b].T) for b in range(BATCH)]

    attn_a_w = {
        "Wq": inputs["L0_attn_a_Wq"], "bq": inputs["L0_attn_a_bq"],
        "Wk": inputs["L0_attn_a_Wk"], "bk": inputs["L0_attn_a_bk"],
        "Wv": inputs["L0_attn_a_Wv"], "bv": inputs["L0_attn_a_bv"],
        "Wo": inputs["L0_attn_a_Wo"], "bo": inputs["L0_attn_a_bo"],
    }
    ffn_w = {k[len("L0_ffn_"):]: np.asarray(v) for k, v in inputs.items()
             if k.startswith("L0_ffn_")}
    attn_b_w = {"weight": inputs["L0_attn_b_weight"]}

    nc_a = _get_nc("attn_a", build_attn_a)
    nc_f = _get_nc("ffn", build_ffn)
    nc_b = _get_nc("attn_b", build_attn_b)

    xt_state = run_attn_a_launch(nc_a, xt_state, attn_a_w)
    xt_state = run_ffn_launch(nc_f, xt_state, ffn_w)
    xt_state = run_attn_b_launch(nc_b, xt_state, attn_b_w)
    xt_state = run_ffn_launch(nc_f, xt_state, ffn_w)

    return np.ascontiguousarray(
        np.stack([xt.T for xt in xt_state])).astype(np.float32)


# version 11 (final)
# speedup vs baseline: 1.1260x; 1.0658x over previous
import numpy as np
import ml_dtypes
import concourse.bass as bass
import concourse.mybir as mybir
from concourse.bass_utils import run_bass_kernel_spmd
from contextlib import ExitStack

F32 = mybir.dt.float32
BF16 = mybir.dt.bfloat16
AF = mybir.ActivationFunctionType
OP = mybir.AluOpType
AX = mybir.AxisListType
BFD = ml_dtypes.bfloat16

D = 1024
S = 4096
BATCH = 4
H = 16
DH = 64
NCH = 8
NTB = 8
OWN = 2048
NOJ = 4
NEG = -3.0e38


_ENG_SEM = {"tensor": "pe", "vector": "dve", "scalar": "act",
            "gpsimd": "gp", "sync": "sp"}
SAME_ENGINE_WAITS = True


class Prog:

    ENGS = ("tensor", "vector", "scalar", "gpsimd", "sync")

    def __init__(self, nc, ctx):
        self.nc = nc
        self.ops = {e: [] for e in self.ENGS}
        self.sems = {}
        for s in ("pe", "dve", "act", "gp", "sp", "gdma", "sdma",
                  "sd0", "sd1", "sd2", "sd3", "sd4", "sd5",
                  "gd0", "gd1", "gd2"):
            h = ctx.enter_context(nc.semaphore(s))
            self.sems[s] = [h, 0]
        self.waited = {e: {} for e in self.ENGS}

    def _wait(self, eng, sname, v):
        if v is None:
            v = self.sems[sname][1]
        if v <= 0:
            return
        own = _ENG_SEM[eng]
        if sname == own and not SAME_ENGINE_WAITS:
            return
        w = self.waited[eng]
        if w.get(sname, 0) >= v:
            return
        w[sname] = v
        h = self.sems[sname][0]
        self.ops[eng].append(lambda e, h=h, v=v: e.wait_ge(h, v))

    def waits(self, eng, deps):
        for (sname, v) in deps:
            self._wait(eng, sname, v)

    def op(self, eng, fn, dma=False, dsem=None):
        if dma:
            sname = dsem or ("gdma" if eng == "gpsimd" else "sdma")
            k = 16
            prev = self.sems[sname][1]
            if prev > 0:
                self._wait(eng, sname, prev)
        else:
            sname = _ENG_SEM[eng]
            k = 1
        h = self.sems[sname][0]
        self.ops[eng].append(lambda e, fn=fn, h=h, k=k: fn(e).then_inc(h, k))
        self.sems[sname][1] += k
        if dma:
            return (sname, None)
        return (sname, self.sems[sname][1])

    def finalize(self, block):
        for eng in self.ENGS:
            if self.ops[eng]:
                getattr(block, eng)(
                    lambda e, eng=eng: [f(e) for f in self.ops[eng]])


class Buf:

    def __init__(self):
        self.w = None
        self.w2 = None
        self.r = {}

    def deps_for_read(self):
        d = []
        if self.w:
            d.append(self.w)
        if self.w2:
            d.append(self.w2)
        return d

    def deps_for_write(self):
        d = list(self.r.items())
        if self.w:
            d.append(self.w)
        if self.w2:
            d.append(self.w2)
        return d

    def did_read(self, ms):
        s, v = ms
        cur = self.r.get(s, 0)
        if cur is None:
            return
        if v is None or cur < v:
            self.r[s] = v

    def did_write(self, ms, ms2=None):
        self.w = ms
        self.w2 = ms2
        self.r = {}


def rd(p, eng, fn, *bufs, dma=False):
    for b in bufs:
        p.waits(eng, b.deps_for_read())
    ms = p.op(eng, fn, dma=dma)
    for b in bufs:
        b.did_read(ms)
    return ms


def op(p, eng, fn, reads=(), writes=(), dma=False, dsem=None):
    for b in reads:
        p.waits(eng, b.deps_for_read())
    for b in writes:
        p.waits(eng, b.deps_for_write())
    ms = p.op(eng, fn, dma=dma, dsem=dsem)
    for b in reads:
        b.did_read(ms)
    for b in writes:
        b.did_write(ms)
    return ms


def chunk_ap(t, W, crow, col0, n):
    return bass.AP(t, crow * 128 * W + col0, [[W, 128], [1, n]])


def w_ap(t, din, dout):
    return bass.AP(t, 0, [[dout, 128], [128 * dout, din // 128], [1, dout]])


def bcast_ap(t, offset, npart, n):
    return bass.AP(t, offset, [[1, 1], [0, npart], [1, n]])


def emit_ln(p, env, xsrc, W, col0, ydst_fn, n=512, post_fn=None):
    cs, csb = env["cs"], env["cs_b"]
    c2, c2b = env["cs2"], env["cs2_b"]
    pst, pstb = env["p_stat"], env["p_stat_b"]
    if "p_stat2" in env:
        pst2, pst2b = env["p_stat2"], env["p_stat2_b"]
        pst2ap = pst2[0:1, 0:n]
    else:
        pst2ap = pst[0:1, n:2 * n]
        pst2b = pstb
    ones = env["ones"]
    onesb = env["ones_b"]
    use_xfull = "xfull" in env
    if use_xfull:
        xf = env["xfull"]
        nxf = len(xf)
        xi = env["xfull_i"][0] = (env["xfull_i"][0] + 1) % nxf
        xt_, xtb = xf[xi], env["xfull_b"][xi]
        op(p, "sync", lambda e: e.dma_start(
            out=xt_[:, 0:NCH * n],
            in_=bass.AP(xsrc, col0, [[W, 128], [128 * W, NCH], [1, n]])),
            writes=[xtb], dma=True, dsem=f"sd{xi}")
        xs_of = lambda c: (xt_[:, c * n:(c + 1) * n], xtb)
    else:
        xs, xsb = env["xs"], env["xs_b"]

        def xs_of(c):
            i = c % len(xs)
            op(p, "sync", lambda e, i=i, c=c: e.dma_start(
                out=xs[i][:, 0:n], in_=chunk_ap(xsrc, W, c, col0, n)),
                writes=[xsb[i]], dma=True, dsem=f"sd{i}")
            return xs[i][:, 0:n], xsb[i]
    srcs = []
    for c in range(NCH):
        xap, xb = xs_of(c)
        srcs.append((xap, xb))
        op(p, "vector", lambda e, xap=xap, ci=c % len(cs): e.tensor_copy(
            cs[ci][:, 0:n], xap),
            reads=[xb], writes=[csb[c % len(cs)]])
        op(p, "scalar", lambda e, xap=xap, ci=c % len(c2): e.activation(
            c2[ci][:, 0:n], xap, AF.Square, bias=0.0, scale=1.0),
            reads=[xb], writes=[c2b[c % len(c2)]])
        op(p, "tensor", lambda e, ci=c % len(cs), c=c: e.matmul(
            pst[0:1, 0:n], ones[:, 0:1], cs[ci][:, 0:n],
            start=(c == 0), stop=(c == NCH - 1)),
            reads=[csb[c % len(cs)], onesb],
            writes=([pstb] if c == 0 else ()))
        mst = op(p, "tensor", lambda e, ci=c % len(c2), c=c, pst2ap=pst2ap:
                 e.matmul(
            pst2ap, ones[:, 0:1], c2[ci][:, 0:n],
            start=(c == 0), stop=(c == NCH - 1)),
            reads=[c2b[c % len(c2)], onesb],
            writes=([pst2b] if (c == 0 and pst2b is not pstb) else ()))
    pstb.did_write(mst)
    if pst2b is not pstb:
        pst2b.did_write(mst)
    r1, r1b = env["row1"], env["row1_b"]
    r2, r2b = env["row2"], env["row2_b"]
    r3, r3b = env["row3"], env["row3_b"]
    inv = 1.0 / D
    op(p, "vector", lambda e: e.tensor_scalar(
        r1[0:1, 0:n], pst[0:1, 0:n], inv, None, op0=OP.mult),
        reads=[pstb], writes=[r1b])
    op(p, "vector", lambda e, pst2ap=pst2ap: e.tensor_scalar(
        r2[0:1, 0:n], pst2ap, inv, None, op0=OP.mult),
        reads=[pst2b], writes=[r2b])
    op(p, "vector", lambda e: e.tensor_tensor(
        r3[0:1, 0:n], r1[0:1, 0:n], r1[0:1, 0:n], op=OP.mult),
        reads=[r1b], writes=[r3b])
    op(p, "vector", lambda e: e.tensor_tensor(
        r2[0:1, 0:n], r2[0:1, 0:n], r3[0:1, 0:n], op=OP.subtract),
        reads=[r3b], writes=[r2b])
    op(p, "vector", lambda e: e.tensor_scalar(
        r2[0:1, 0:n], r2[0:1, 0:n], 1e-5, None, op0=OP.add),
        reads=[r2b], writes=[r2b])
    op(p, "scalar", lambda e: e.activation(
        r3[0:1, 0:n], r2[0:1, 0:n], AF.Sqrt, bias=0.0, scale=1.0),
        reads=[r2b], writes=[r3b])
    op(p, "vector", lambda e: e.reciprocal(r3[0:1, 0:n], r3[0:1, 0:n]),
        reads=[r3b], writes=[r3b])
    op(p, "vector", lambda e: e.tensor_tensor(
        r2[0:1, 0:n], r1[0:1, 0:n], r3[0:1, 0:n], op=OP.mult),
        reads=[r1b, r3b], writes=[r2b])
    Br, Brb = env["B_rstd"], env["B_rstd_b"]
    Bm, Bmb = env["B_mur"], env["B_mur_b"]
    op(p, "sync", lambda e: e.dma_start(
        out=Br[:, 0:n], in_=bcast_ap(r3, 0, 128, n)),
        reads=[r3b], writes=[Brb], dma=True, dsem="gd0")
    op(p, "sync", lambda e: e.dma_start(
        out=Bm[:, 0:n], in_=bcast_ap(r2, 0, 128, n)),
        reads=[r2b], writes=[Bmb], dma=True, dsem="gd0")
    if ydst_fn is None:
        return
    nr, nrb = env["nrm"], env["nrm_b"]
    for c in range(NCH):
        if use_xfull:
            xap, xb = srcs[c]
        else:
            xap, xb = xs_of(c)
        ni = c % len(nr)
        op(p, "vector", lambda e, xap=xap, ni=ni: e.tensor_tensor(
            nr[ni][:, 0:n], xap, Br[:, 0:n], op=OP.mult),
            reads=[xb, Brb], writes=[nrb[ni]])
        dst, dstbuf = ydst_fn(c)
        op(p, "vector", lambda e, ni=ni, dst=dst: e.tensor_tensor(
            dst, nr[ni][:, 0:n], Bm[:, 0:n], op=OP.subtract),
            reads=[nrb[ni], Bmb], writes=([dstbuf] if dstbuf else ()))
        if post_fn is not None:
            post_fn(c)


def make_ln_env(p, ctx, nc, env, nxs=3, nxfull=0):
    sb = lambda name, shp, dt: ctx.enter_context(nc.sbuf_tensor(name, shp, dt))
    env["xs"] = [sb(f"ln_xs{i}", [128, 512], F32) for i in range(nxs)]
    env["xs_b"] = [Buf() for _ in range(nxs)]
    if nxfull:
        env["xfull"] = [sb(f"ln_xf{i}", [128, NCH * 512], F32)
                        for i in range(nxfull)]
        env["xfull_b"] = [Buf() for _ in range(nxfull)]
        env["xfull_i"] = [0]
    env["cs"] = [sb(f"ln_cs{i}", [128, 512], BF16) for i in range(2)]
    env["cs_b"] = [Buf() for _ in range(2)]
    env["cs2"] = [sb(f"ln_c2{i}", [128, 512], BF16) for i in range(2)]
    env["cs2_b"] = [Buf() for _ in range(2)]
    env["nrm"] = [sb(f"ln_nr{i}", [128, 512], F32) for i in range(2)]
    env["nrm_b"] = [Buf() for _ in range(2)]
    env["row1"] = sb("ln_r1", [1, 512], F32)
    env["row1_b"] = Buf()
    env["row2"] = sb("ln_r2", [1, 512], F32)
    env["row2_b"] = Buf()
    env["row3"] = sb("ln_r3", [1, 512], F32)
    env["row3_b"] = Buf()
    env["B_rstd"] = sb("ln_Br", [128, 512], F32)
    env["B_rstd_b"] = Buf()
    env["B_mur"] = sb("ln_Bm", [128, 512], F32)
    env["B_mur_b"] = Buf()
    env["ones"] = sb("ones", [128, 2], BF16)
    env["ones_b"] = Buf()
    ms = p.op("gpsimd", lambda e: e.memset(env["ones"][:, :], 1.0))
    env["ones_b"].did_write(ms)


def build_ffn(ncycles=3, debug=False, gf_tbs=NTB):
    nc = bass.Bass("TRN2", target_bir_lowering=False)
    xt = nc.dram_tensor("xt", [D, S], F32, kind="ExternalInput")
    w1 = nc.dram_tensor("w1", [D, D], F32, kind="ExternalInput")
    w2 = nc.dram_tensor("w2", [D, 512], F32, kind="ExternalInput")
    w3 = nc.dram_tensor("w3", [512, 256], F32, kind="ExternalInput")
    w4 = nc.dram_tensor("w4", [256, 1], F32, kind="ExternalInput")
    gw1 = nc.dram_tensor("gw1", [D, 512], F32, kind="ExternalInput")
    gw2 = nc.dram_tensor("gw2", [512, 256], F32, kind="ExternalInput")
    gw3 = nc.dram_tensor("gw3", [256, 1], F32, kind="ExternalInput")
    b1 = nc.dram_tensor("b1", [128, 8], F32, kind="ExternalInput")
    b2 = nc.dram_tensor("b2", [128, 4], F32, kind="ExternalInput")
    b3 = nc.dram_tensor("b3", [128, 2], F32, kind="ExternalInput")
    b4 = nc.dram_tensor("b4", [1, 1], F32, kind="ExternalInput")
    gb1 = nc.dram_tensor("gb1", [128, 4], F32, kind="ExternalInput")
    gb2 = nc.dram_tensor("gb2", [128, 2], F32, kind="ExternalInput")
    gb3 = nc.dram_tensor("gb3", [1, 1], F32, kind="ExternalInput")
    ar = nc.dram_tensor("ar", [1, 1], F32, kind="ExternalInput")
    upd = nc.dram_tensor("upd", [D, OWN], F32, kind="ExternalOutput")
    if debug:
        dbg_cells = nc.dram_tensor("dbg_cells", [1, 8], F32, kind="ExternalOutput")
        dbg_trow = nc.dram_tensor("dbg_trow", [1, 512], F32, kind="ExternalOutput")
        dbg_h1 = nc.dram_tensor("dbg_h1", [128, 512], F32, kind="ExternalOutput")
        dbg_mean = nc.dram_tensor("dbg_mean", [128, 8], F32, kind="ExternalOutput")
        dbg_ycs = nc.dram_tensor("dbg_ycs", [128, 512], F32, kind="ExternalOutput")
        dbg_red = nc.dram_tensor("dbg_red", [128, 1], F32, kind="ExternalOutput")
        dbg_hcol = nc.dram_tensor("dbg_hcol", [128, 8], F32, kind="ExternalOutput")

    with ExitStack() as ctx:
        sb = lambda name, shp, dt: ctx.enter_context(nc.sbuf_tensor(name, shp, dt))
        p = Prog(nc, ctx)
        env = {}
        make_ln_env(p, ctx, nc, env, nxs=2, nxfull=2)

        state = sb("state", [128, NCH * OWN], F32)
        state_bj = [Buf() for _ in range(NOJ)]
        sbf = sb("sbf", [128, NCH * 512], BF16)
        sbf_b = Buf()
        w1b, w1bb = sb("w1b", [128, NCH * D], BF16), Buf()
        w2b, w2bb = sb("w2b", [128, NCH * 512], BF16), Buf()
        w3b, w3bb = sb("w3b", [128, 4 * 256], BF16), Buf()
        w4b, w4bb = sb("w4b", [128, 2], BF16), Buf()
        gw1b, gw1bb = sb("gw1b", [128, NCH * 512], BF16), Buf()
        gw2b, gw2bb = sb("gw2b", [128, 4 * 256], BF16), Buf()
        gw3b, gw3bb = sb("gw3b", [128, 2], BF16), Buf()
        h1, h1b = sb("h1", [128, NCH * 512], BF16), Buf()
        h2, h2b = sb("h2", [128, 4 * 512], BF16), Buf()
        h3, h3b = sb("h3", [128, 2 * 512], BF16), Buf()
        b1c, b1cb = sb("b1c", [128, 8], F32), Buf()
        b2c, b2cb = sb("b2c", [128, 4], F32), Buf()
        b3c, b3cb = sb("b3c", [128, 2], F32), Buf()
        b4c, b4cb = sb("b4c", [1, 1], F32), Buf()
        gb1c, gb1cb = sb("gb1c", [128, 4], F32), Buf()
        gb2c, gb2cb = sb("gb2c", [128, 2], F32), Buf()
        gb3c, gb3cb = sb("gb3c", [1, 1], F32), Buf()
        arc, arcb = sb("arc", [1, 1], F32), Buf()
        ycs, ycsb = sb("ycs", [128, 512], BF16), Buf()
        junk, junkb = sb("junk", [128, 512], BF16), Buf()
        red, redb = sb("red", [128, 1], F32), Buf()
        colsum, colsumb = sb("colsum", [128, 8], F32), Buf()
        mean_bf, mean_bfb = sb("mean_bf", [128, 8], BF16), Buf()
        gcell, gcellb = sb("gcell", [1, 8], F32), Buf()
        acell, acellb = sb("acell", [1, 1], F32), Buf()
        bcell, bcellb = sb("bcell", [1, 1], F32), Buf()
        hcol, hcolb = sb("hcol", [128, 8], BF16), Buf()
        trow2 = [sb(f"trow{i}", [1, 512], F32) for i in range(2)]
        trowb2 = [Buf() for _ in range(2)]
        nrow2 = [sb(f"nrow{i}", [1, 512], F32) for i in range(2)]
        nrowb2 = [Buf() for _ in range(2)]
        B_cp2 = [sb(f"B_cp{i}", [128, 512], F32) for i in range(2)]
        B_cpb2 = [Buf() for _ in range(2)]
        B_cn2 = [sb(f"B_cn{i}", [128, 512], F32) for i in range(2)]
        B_cnb2 = [Buf() for _ in range(2)]
        tpr, tprb = sb("tpr", [128, 512], F32), Buf()
        t2, t2b = sb("t2", [128, 512], F32), Buf()
        tprg, tprgb = sb("tprg", [128, 512], F32), Buf()
        t2g, t2gb = sb("t2g", [128, 512], F32), Buf()
        us = [sb(f"us{i}", [128, 512], F32) for i in range(2)]
        usb = [Buf() for _ in range(2)]

        P_A = ctx.enter_context(nc.psum_tensor("P_A", [128, 1024], F32))
        P_B = ctx.enter_context(nc.psum_tensor("P_B", [128, 1024], F32))
        P_C = ctx.enter_context(nc.psum_tensor("P_C", [128, 512], F32))
        P_D = ctx.enter_context(nc.psum_tensor("P_D", [2, 1024], F32))
        P_E = ctx.enter_context(nc.psum_tensor("P_E", [128, 512], F32))
        pab = [Buf(), Buf()]
        pbb = [Buf(), Buf()]
        pcb, peb = Buf(), Buf()
        env["p_stat"], env["p_stat_b"] = P_D, Buf()
        pdb = env["p_stat_b"]

        for (dst, dstb, src, di, do) in (
                (w1b, w1bb, w1, D, D), (w2b, w2bb, w2, D, 512),
                (w3b, w3bb, w3, 512, 256), (w4b, w4bb, w4, 256, 1),
                (gw1b, gw1bb, gw1, D, 512), (gw2b, gw2bb, gw2, 512, 256),
                (gw3b, gw3bb, gw3, 256, 1)):
            op(p, "gpsimd", lambda e, dst=dst, src=src, di=di, do=do:
               e.dma_start(out=dst[:, :], in_=w_ap(src, di, do),
                           allow_slow_non_contiguous=(do == 1)),
               writes=[dstb], dma=True, dsem="gd1")
        for (dst, dstb, src) in ((b1c, b1cb, b1), (b2c, b2cb, b2),
                                 (b3c, b3cb, b3), (b4c, b4cb, b4),
                                 (gb1c, gb1cb, gb1), (gb2c, gb2cb, gb2),
                                 (gb3c, gb3cb, gb3), (arc, arcb, ar)):
            op(p, "sync", lambda e, dst=dst, src=src:
               e.dma_start(out=dst[:, :], in_=src[:, :]),
               writes=[dstb], dma=True)

        ms = p.op("gpsimd", lambda e: e.memset(colsum[:, :], 0.0))
        colsumb.did_write(ms)
        for tb in range(gf_tbs):
            own = tb < NOJ
            if own:
                def ydst(c, tb=tb):
                    a = (c * NOJ + tb) * 512
                    return state[:, a:a + 512], None
            else:
                def ydst(c):
                    return ycs[:, :], ycsb

            def post(c, tb=tb, own=own):
                if own:
                    a = (c * NOJ + tb) * 512
                    src_ap = state[:, a:a + 512]
                    tmpb = Buf()
                    tmpb.did_write(("dve", p.sems["dve"][1]))
                    rb = [tmpb]
                else:
                    src_ap = ycs[:, :]
                    rb = [ycsb]
                op(p, "scalar", lambda e, src_ap=src_ap: e.activation(
                    junk[:, :], src_ap, AF.Copy, bias=0.0,
                    scale=1.0, accum_out=red[:, 0:1]),
                    reads=rb, writes=[redb, junkb])
                op(p, "gpsimd", lambda e, c=c: e.tensor_tensor(
                    colsum[:, c:c + 1], colsum[:, c:c + 1], red[:, 0:1], op=OP.add),
                    reads=[redb, colsumb], writes=[colsumb])
            emit_ln(p, env, xt, S, tb * 512, ydst, post_fn=post)
            if own:
                state_bj[tb].did_write(("dve", p.sems["dve"][1]))

        op(p, "vector", lambda e: e.tensor_scalar(
            mean_bf[:, :], colsum[:, :], 1.0 / S, None, op0=OP.mult),
            reads=[colsumb], writes=[mean_bfb])

        for o in range(4):
            for c in range(NCH):
                op(p, "tensor", lambda e, o=o, c=c: e.matmul(
                    P_C[:, o:o + 1],
                    gw1b[:, c * 512 + o * 128: c * 512 + (o + 1) * 128],
                    mean_bf[:, c:c + 1], start=(c == 0), stop=(c == NCH - 1)),
                    reads=[gw1bb, mean_bfb], writes=([pcb] if (o == 0 and c == 0) else ()))
        pcb.did_write(("pe", p.sems["pe"][1]))
        for o in range(4):
            op(p, "scalar", lambda e, o=o: e.activation(
                hcol[:, o:o + 1], P_C[:, o:o + 1], AF.Gelu,
                bias=gb1c[:, o:o + 1], scale=1.0),
                reads=[pcb, gb1cb], writes=([hcolb] if o == 0 else ()))
        hcolb.did_write(("act", p.sems["act"][1]))
        for o in range(2):
            for c in range(4):
                op(p, "tensor", lambda e, o=o, c=c: e.matmul(
                    P_E[:, o:o + 1],
                    gw2b[:, c * 256 + o * 128: c * 256 + (o + 1) * 128],
                    hcol[:, c:c + 1], start=(c == 0), stop=(c == 3)),
                    reads=[gw2bb, hcolb], writes=([peb] if (o == 0 and c == 0) else ()))
        peb.did_write(("pe", p.sems["pe"][1]))
        for o in range(2):
            op(p, "scalar", lambda e, o=o: e.activation(
                hcol[:, 4 + o:5 + o], P_E[:, o:o + 1], AF.Gelu,
                bias=gb2c[:, o:o + 1], scale=1.0),
                reads=[peb, gb2cb], writes=[hcolb])
        for c in range(2):
            op(p, "tensor", lambda e, c=c: e.matmul(
                P_D[0:1, 0:1], gw3b[:, c:c + 1], hcol[:, 4 + c:5 + c],
                start=(c == 0), stop=(c == 1)),
                reads=[gw3bb, hcolb], writes=([pdb] if c == 0 else ()))
        pdb.did_write(("pe", p.sems["pe"][1]))
        op(p, "scalar", lambda e: e.activation(
            gcell[0:1, 0:1], P_D[0:1, 0:1], AF.Sigmoid, bias=gb3c[0:1, 0:1],
            scale=1.0), reads=[pdb, gb3cb], writes=[gcellb])
        op(p, "gpsimd", lambda e: e.tensor_tensor(
            gcell[0:1, 1:2], arc[0:1, 0:1], gcell[0:1, 0:1], op=OP.mult),
            reads=[arcb, gcellb], writes=[gcellb])
        op(p, "gpsimd", lambda e: e.tensor_scalar(
            gcell[0:1, 2:3], gcell[0:1, 1:2], 1.4, None, op0=OP.mult),
            reads=[gcellb], writes=[gcellb])
        op(p, "gpsimd", lambda e: e.tensor_tensor(
            gcell[0:1, 3:4], gcell[0:1, 2:3], arc[0:1, 0:1], op=OP.subtract),
            reads=[gcellb, arcb], writes=[gcellb])
        op(p, "gpsimd", lambda e: e.tensor_scalar(
            acell[0:1, 0:1], gcell[0:1, 3:4], 0.1, None, op0=OP.add),
            reads=[gcellb], writes=[acellb])
        op(p, "gpsimd", lambda e: e.tensor_scalar(
            bcell[0:1, 0:1], arc[0:1, 0:1], 0.6, None, op0=OP.mult),
            reads=[arcb], writes=[bcellb])
        if debug:
            op(p, "sync", lambda e: e.dma_start(out=dbg_cells[0:1, 0:1], in_=gcell[0:1, 0:1]),
               reads=[gcellb], dma=True)
            op(p, "sync", lambda e: e.dma_start(out=dbg_cells[0:1, 1:2], in_=acell[0:1, 0:1]),
               reads=[acellb], dma=True)
            op(p, "sync", lambda e: e.dma_start(out=dbg_cells[0:1, 2:3], in_=bcell[0:1, 0:1]),
               reads=[bcellb], dma=True)
            op(p, "vector", lambda e: e.tensor_copy(env["nrm"][0][:, 0:8], colsum[:, :]),
               reads=[colsumb], writes=[env["nrm_b"][0]])
            op(p, "vector", lambda e: e.tensor_copy(env["nrm"][1][:, 0:8], hcol[:, :]),
               reads=[hcolb], writes=[env["nrm_b"][1]])
            op(p, "sync", lambda e: e.dma_start(out=dbg_hcol[:, :], in_=env["nrm"][1][:, 0:8]),
               reads=[env["nrm_b"][1]], dma=True)
            op(p, "sync", lambda e: e.dma_start(out=dbg_mean[:, :], in_=env["nrm"][0][:, 0:8]),
               reads=[env["nrm_b"][0]], dma=True)


        def coupling_net(cyc, j):
            jp = j % 2
            trow, trowb = trow2[jp], trowb2[jp]
            nrow, nrowb = nrow2[jp], nrowb2[jp]
            for c in range(NCH):
                a = (c * NOJ + j) * 512
                op(p, "vector", lambda e, a=a, c=c: e.tensor_copy(
                    sbf[:, c * 512:(c + 1) * 512], state[:, a:a + 512]),
                    reads=[state_bj[j]], writes=([sbf_b] if c == 0 else ()))
            sbf_b.did_write(("dve", p.sems["dve"][1]))
            for o in range(NCH):
                pt = P_A if o % 2 == 0 else P_B
                ptb = (pab if o % 2 == 0 else pbb)[(o // 2) % 2]
                hf = ((o // 2) % 2) * 512
                for c in range(NCH):
                    op(p, "tensor", lambda e, pt=pt, hf=hf, o=o, c=c: e.matmul(
                        pt[:, hf:hf + 512],
                        w1b[:, c * D + o * 128: c * D + (o + 1) * 128],
                        sbf[:, c * 512:(c + 1) * 512],
                        start=(c == 0), stop=(c == NCH - 1)),
                        reads=[w1bb, sbf_b], writes=([ptb] if c == 0 else ()))
                ptb.did_write(("pe", p.sems["pe"][1]))
                op(p, "scalar", lambda e, pt=pt, hf=hf, o=o: e.activation(
                    h1[:, o * 512:(o + 1) * 512], pt[:, hf:hf + 512],
                    AF.Gelu, bias=b1c[:, o:o + 1], scale=1.0),
                    reads=[ptb, b1cb], writes=([h1b] if o == 0 else ()))
            h1b.did_write(("act", p.sems["act"][1]))
            for o in range(4):
                pt = P_A if o % 2 == 0 else P_B
                ptb = (pab if o % 2 == 0 else pbb)[0]
                for c in range(NCH):
                    op(p, "tensor", lambda e, pt=pt, o=o, c=c: e.matmul(
                        pt[:, 0:512],
                        w2b[:, c * 512 + o * 128: c * 512 + (o + 1) * 128],
                        h1[:, c * 512:(c + 1) * 512],
                        start=(c == 0), stop=(c == NCH - 1)),
                        reads=[w2bb, h1b], writes=([ptb] if c == 0 else ()))
                ptb.did_write(("pe", p.sems["pe"][1]))
                op(p, "scalar", lambda e, pt=pt, o=o: e.activation(
                    h2[:, o * 512:(o + 1) * 512], pt[:, 0:512],
                    AF.Gelu, bias=b2c[:, o:o + 1], scale=1.0),
                    reads=[ptb, b2cb], writes=([h2b] if o == 0 else ()))
            h2b.did_write(("act", p.sems["act"][1]))
            for o in range(2):
                pt = P_A if o % 2 == 0 else P_B
                ptb = (pab if o % 2 == 0 else pbb)[0]
                for c in range(4):
                    op(p, "tensor", lambda e, pt=pt, o=o, c=c: e.matmul(
                        pt[:, 0:512],
                        w3b[:, c * 256 + o * 128: c * 256 + (o + 1) * 128],
                        h2[:, c * 512:(c + 1) * 512],
                        start=(c == 0), stop=(c == 3)),
                        reads=[w3bb, h2b], writes=([ptb] if c == 0 else ()))
                ptb.did_write(("pe", p.sems["pe"][1]))
                op(p, "scalar", lambda e, pt=pt, o=o: e.activation(
                    h3[:, o * 512:(o + 1) * 512], pt[:, 0:512],
                    AF.Gelu, bias=b3c[:, o:o + 1], scale=1.0),
                    reads=[ptb, b3cb], writes=([h3b] if o == 0 else ()))
            h3b.did_write(("act", p.sems["act"][1]))
            for c in range(2):
                op(p, "tensor", lambda e, c=c: e.matmul(
                    P_D[0:1, 0:512], w4b[:, c:c + 1],
                    h3[:, c * 512:(c + 1) * 512],
                    start=(c == 0), stop=(c == 1)),
                    reads=[w4bb, h3b], writes=([pdb] if c == 0 else ()))
            pdb.did_write(("pe", p.sems["pe"][1]))
            op(p, "scalar", lambda e, trow=trow: e.activation(
                trow[0:1, :], P_D[0:1, 0:512], AF.Sigmoid,
                bias=b4c[0:1, 0:1], scale=1.0),
                reads=[pdb, b4cb], writes=[trowb])
            op(p, "vector", lambda e, trow=trow: e.tensor_scalar(
                trow[0:1, :], trow[0:1, :], bcell[0:1, 0:1], acell[0:1, 0:1],
                op0=OP.mult, op1=OP.add),
                reads=[trowb, bcellb, acellb], writes=[trowb])
            op(p, "vector", lambda e, trow=trow, nrow=nrow: e.tensor_scalar(
                nrow[0:1, :], trow[0:1, :], -1.0, None, op0=OP.mult),
                reads=[trowb], writes=[nrowb])
            op(p, "sync", lambda e, trow=trow, jp=jp: e.dma_start(
                out=B_cp2[jp][:, :], in_=bcast_ap(trow, 0, 128, 512)),
                reads=[trowb], writes=[B_cpb2[jp]], dma=True, dsem="gd2")
            op(p, "sync", lambda e, nrow=nrow, jp=jp: e.dma_start(
                out=B_cn2[jp][:, :], in_=bcast_ap(nrow, 0, 128, 512)),
                reads=[nrowb], writes=[B_cnb2[jp]], dma=True, dsem="gd2")

        def twist(cyc, j):
            jp = j % 2
            B_cp, B_cpb = B_cp2[jp], B_cpb2[jp]
            B_cn, B_cnb = B_cn2[jp], B_cnb2[jp]
            for m in range(4):
                eng = "vector"
                tp_, tpb_ = (tpr, tprb) if m < 2 else (tprg, tprgb)
                t_, tb_ = (t2, t2b) if m < 2 else (t2g, t2gb)
                sA, sAb = (B_cp, B_cpb) if m < 2 else (B_cn, B_cnb)
                sB, sBb = (B_cn, B_cnb) if m < 2 else (B_cp, B_cpb)
                a0 = (m * NOJ + j) * 512
                a4 = ((m + 4) * NOJ + j) * 512
                op(p, eng, lambda e, a0=a0, tp_=tp_: e.tensor_copy(
                    tp_[:, :], state[:, a0:a0 + 512]),
                    reads=[state_bj[j]], writes=[tpb_])
                op(p, eng, lambda e, a4=a4, sA=sA, t_=t_: e.tensor_tensor(
                    t_[:, :], state[:, a4:a4 + 512], sA[:, :], op=OP.mult),
                    reads=[state_bj[j], sAb], writes=[tb_])
                op(p, eng, lambda e, a0=a0, t_=t_: e.tensor_tensor(
                    state[:, a0:a0 + 512], state[:, a0:a0 + 512], t_[:, :],
                    op=OP.add), reads=[tb_, state_bj[j]], writes=())
                op(p, eng, lambda e, sB=sB, tp_=tp_, t_=t_: e.tensor_tensor(
                    t_[:, :], tp_[:, :], sB[:, :], op=OP.mult),
                    reads=[tpb_, sBb], writes=[tb_])
                op(p, eng, lambda e, a4=a4, t_=t_: e.tensor_tensor(
                    state[:, a4:a4 + 512], state[:, a4:a4 + 512], t_[:, :],
                    op=OP.add), reads=[tb_, state_bj[j]], writes=())
            state_bj[j].did_write(("dve", p.sems["dve"][1]))

        for cyc in range(ncycles):
            for j in range(NOJ):
                coupling_net(cyc, j)
                if j > 0:
                    twist(cyc, j - 1)
            twist(cyc, NOJ - 1)

        for j in range(NOJ):
            for c in range(NCH):
                i = c % 2
                u = c % 2
                a = (c * NOJ + j) * 512
                op(p, "sync", lambda e, i=i, c=c, j=j: e.dma_start(
                    out=env["xs"][i][:, :], in_=chunk_ap(xt, S, c, j * 512, 512)),
                    writes=[env["xs_b"][i]], dma=True, dsem=f"sd{i}")
                op(p, "vector", lambda e, i=i, u=u, a=a: e.tensor_tensor(
                    us[u][:, :], env["xs"][i][:, :], state[:, a:a + 512],
                    op=OP.add),
                    reads=[env["xs_b"][i], state_bj[j]], writes=[usb[u]])
                op(p, "sync", lambda e, u=u, c=c, j=j: e.dma_start(
                    out=bass.AP(upd, c * 128 * OWN + j * 512, [[OWN, 128], [1, 512]]),
                    in_=us[u][:, :]),
                    reads=[usb[u]], dma=True)
        p._wait("sync", "sdma", p.sems["sdma"][1])

        with nc.Block("ffn") as blk:
            p.finalize(blk)
    return nc


def _prep_bias_pc(b, nch):
    return np.ascontiguousarray(b.reshape(nch, 128).T).astype(np.float32)


def _own_cols(r):
    idx = []
    for j in range(NOJ):
        g = 2 * j + r
        idx.append(np.arange(g * 512, (g + 1) * 512))
    return np.concatenate(idx)


def run_ffn_launch(nc_ffn, xt_state, weights, want_res=False):
    in_maps = []
    for core in range(8):
        b, r = core // 2, core % 2
        cols = _own_cols(r)
        perm = np.concatenate([cols, _own_cols(1 - r)])
        m = {
            "xt": np.ascontiguousarray(xt_state[b][:, perm]),
            "w1": weights["cn_w1"], "w2": weights["cn_w2"],
            "w3": weights["cn_w3"], "w4": weights["cn_w4"],
            "gw1": weights["gc_w1"], "gw2": weights["gc_w2"],
            "gw3": weights["gc_w3"],
            "b1": _prep_bias_pc(weights["cn_b1"], 8),
            "b2": _prep_bias_pc(weights["cn_b2"], 4),
            "b3": _prep_bias_pc(weights["cn_b3"], 2),
            "b4": weights["cn_b4"].reshape(1, 1).astype(np.float32),
            "gb1": _prep_bias_pc(weights["gc_b1"], 4),
            "gb2": _prep_bias_pc(weights["gc_b2"], 2),
            "gb3": weights["gc_b3"].reshape(1, 1).astype(np.float32),
            "ar": weights["adaptive_range"].reshape(1, 1).astype(np.float32),
        }
        in_maps.append(m)
    res = run_bass_kernel_spmd(nc_ffn, in_maps, core_ids=list(range(8)))
    out = [x.copy() for x in xt_state]
    for core in range(8):
        b, r = core // 2, core % 2
        u = res.results[core]["upd"]
        for j in range(NOJ):
            g = 2 * j + r
            out[b][:, g * 512:(g + 1) * 512] = u[:, j * 512:(j + 1) * 512]
    if want_res:
        return out, res
    return out




def build_attn_a():
    nc = bass.Bass("TRN2", target_bir_lowering=False)
    xt = nc.dram_tensor("xt", [D, S], F32, kind="ExternalInput")
    xo = nc.dram_tensor("xo", [D, OWN], F32, kind="ExternalInput")
    masks = nc.dram_tensor("masks", [128, 8 * 512], BF16, kind="ExternalInput")
    wq = nc.dram_tensor("wq", [D, D], F32, kind="ExternalInput")
    wk = nc.dram_tensor("wk", [D, D], F32, kind="ExternalInput")
    wv = nc.dram_tensor("wv", [D, D], F32, kind="ExternalInput")
    wo = nc.dram_tensor("wo", [D, D], F32, kind="ExternalInput")
    bq = nc.dram_tensor("bq", [128, 8], F32, kind="ExternalInput")
    bk = nc.dram_tensor("bk", [128, 8], F32, kind="ExternalInput")
    bv = nc.dram_tensor("bv", [128, 8], F32, kind="ExternalInput")
    bo = nc.dram_tensor("bo", [128, 8], F32, kind="ExternalInput")
    upd = nc.dram_tensor("upd", [D, OWN], F32, kind="ExternalOutput")
    K_dram = nc.dram_tensor("K_dram", [NCH, 128, S], BF16, kind="Internal")

    with ExitStack() as ctx:
        sb = lambda name, shp, dt: ctx.enter_context(nc.sbuf_tensor(name, shp, dt))
        p = Prog(nc, ctx)
        env = {}
        make_ln_env(p, ctx, nc, env, nxs=2, nxfull=1)

        kslot = [sb(f"kslot{i}", [128, 512], BF16) for i in range(2)]
        kslotb = [Buf() for _ in range(2)]
        Kst = [sb(f"Kst{i}", [128, S], BF16) for i in range(2)]
        Kstb = [Buf() for _ in range(2)]
        Kdb = Buf()
        V_sb, Vb = sb("V_sb", [128, 32 * D], BF16), Buf()
        qbuf, Qb = sb("qbuf", [128, NCH * 512], BF16), Buf()
        ybuf2 = [sb(f"ybuf{i}", [128, NCH * 512], BF16) for i in range(2)]
        Yb2 = [Buf() for i in range(2)]
        wA, wAb = sb("wA", [128, NCH * D], BF16), Buf()
        wB, wBb = sb("wB", [128, NCH * D], BF16), Buf()
        msk, mskb = sb("msk", [128, 8 * 512], BF16), Buf()
        Ebuf = sb("Ebuf", [128, 3 * 1024], BF16)
        Eb = [Buf() for _ in range(3)]
        Z_sb, Zb = sb("Z_sb", [128, NCH * 512], BF16), Buf()
        Brec, Brecb = sb("Brec", [128, 512], F32), Buf()
        Brec2, Brec2b = sb("Brec2", [128, 512], F32), Buf()
        rrow, rrowb = sb("rrow", [1, 512], F32), Buf()
        rrow2, rrow2b = sb("rrow2", [1, 512], F32), Buf()
        bq_c, bq_cb = sb("bq_c", [128, 8], F32), Buf()
        bk_c, bk_cb = sb("bk_c", [128, 8], F32), Buf()
        bo_c, bo_cb = sb("bo_c", [128, 8], F32), Buf()
        bv_bf, bv_bfb = sb("bv_bf", [128, 8], BF16), Buf()
        btot, btotb = sb("btot", [128, 8], F32), Buf()
        us = [env["nrm"][i] for i in range(2)]
        usb = [env["nrm_b"][i] for i in range(2)]

        P_A = ctx.enter_context(nc.psum_tensor("P_A", [128, 1024], F32))
        P_B = ctx.enter_context(nc.psum_tensor("P_B", [128, 1024], F32))
        P_C = ctx.enter_context(nc.psum_tensor("P_C", [128, 512], F32))
        P_D = ctx.enter_context(nc.psum_tensor("P_D", [2, 1024], F32))
        P_E = ctx.enter_context(nc.psum_tensor("P_E", [128, 512], F32))
        pab, pbb, pcb, peb = Buf(), Buf(), Buf(), Buf()
        env["p_stat"], env["p_stat_b"] = P_D, Buf()
        pdb = env["p_stat_b"]
        rot = [(P_A, 0, pab), (P_A, 512, pab), (P_B, 0, pbb), (P_B, 512, pbb)]
        rotb = [Buf() for _ in range(4)]

        op(p, "gpsimd", lambda e: e.dma_start(out=wA[:, :], in_=w_ap(wk, D, D)),
           writes=[wAb], dma=True, dsem="gd1")
        op(p, "gpsimd", lambda e: e.dma_start(out=wB[:, :], in_=w_ap(wv, D, D)),
           writes=[wBb], dma=True, dsem="gd1")
        op(p, "gpsimd", lambda e: e.dma_start(out=bv_bf[:, :], in_=bv[:, :]),
           writes=[bv_bfb], dma=True, dsem="gd1")
        op(p, "sync", lambda e: e.dma_start(out=msk[:, :], in_=masks[:, :]),
           writes=[mskb], dma=True)
        for (dst, dstb, src) in ((bq_c, bq_cb, bq), (bk_c, bk_cb, bk),
                                 (bo_c, bo_cb, bo)):
            op(p, "sync", lambda e, dst=dst, src=src:
               e.dma_start(out=dst[:, :], in_=src[:, :]), writes=[dstb], dma=True)

        for tb in range(NTB):
            ybuf, Yb = ybuf2[tb % 2], Yb2[tb % 2]
            def ydst(c, ybuf=ybuf, Yb=Yb):
                return ybuf[:, c * 512:(c + 1) * 512], (Yb if c == 0 else None)
            emit_ln(p, env, xt, S, tb * 512, ydst)
            Yb.did_write(("dve", p.sems["dve"][1]))
            for o in range(NCH):
                pt, off, ptb = rot[o % 4][0], rot[o % 4][1], rotb[o % 4]
                for c in range(NCH):
                    op(p, "tensor", lambda e, pt=pt, off=off, o=o, c=c, ybuf=ybuf:
                       e.matmul(
                        pt[:, off:off + 512],
                        wA[:, c * D + o * 128: c * D + (o + 1) * 128],
                        ybuf[:, c * 512:(c + 1) * 512],
                        start=(c == 0), stop=(c == NCH - 1)),
                        reads=[wAb, Yb], writes=([ptb] if c == 0 else ()))
                ptb.did_write(("pe", p.sems["pe"][1]))
                ks = o % 2
                op(p, "scalar", lambda e, pt=pt, off=off, o=o, ks=ks: e.activation(
                    kslot[ks][:, :], pt[:, off:off + 512], AF.Identity,
                    bias=bk_c[:, o:o + 1], scale=1.0),
                    reads=[ptb, bk_cb], writes=[kslotb[ks]])
                op(p, "sync", lambda e, o=o, tb=tb, ks=ks: e.dma_start(
                    out=bass.AP(K_dram, o * 128 * S + tb * 512,
                                [[S, 128], [1, 512]]),
                    in_=kslot[ks][:, :]),
                    reads=[kslotb[ks]], writes=[Kdb], dma=True, dsem="sd5")
            for tt in range(4):
                for dblk in range(2):
                    k4 = (tt * 2 + dblk) % 4
                    pt, off, ptb = rot[k4][0], rot[k4][1], rotb[k4]
                    for c in range(NCH):
                        op(p, "tensor", lambda e, pt=pt, off=off, tt=tt, dblk=dblk,
                           c=c, ybuf=ybuf: e.matmul(
                            pt[:, off:off + 512],
                            ybuf[:, c * 512 + tt * 128: c * 512 + (tt + 1) * 128],
                            wB[:, c * D + dblk * 512: c * D + (dblk + 1) * 512],
                            start=(c == 0), stop=(c == NCH - 1)),
                            reads=[wBb, Yb], writes=([ptb] if c == 0 else ()))
                    ptb.did_write(("pe", p.sems["pe"][1]))
                    op(p, "scalar", lambda e, pt=pt, off=off, tt=tt, dblk=dblk, tb=tb:
                       e.activation(
                        V_sb[:, (tb * 4 + tt) * D + dblk * 512:
                             (tb * 4 + tt) * D + (dblk + 1) * 512],
                        pt[:, off:off + 512], AF.Copy, bias=0.0, scale=1.0),
                        reads=[ptb], writes=[Vb])

        op(p, "gpsimd", lambda e: e.dma_start(out=wA[:, :], in_=w_ap(wq, D, D)),
           writes=[wAb], dma=True, dsem="gd1")
        op(p, "gpsimd", lambda e: e.dma_start(out=wB[:, :], in_=w_ap(wo, D, D)),
           writes=[wBb], dma=True, dsem="gd1")
        for o in range(NCH):
            for c in range(NCH):
                op(p, "tensor", lambda e, o=o, c=c: e.matmul(
                    P_E[:, o:o + 1],
                    wB[:, c * D + o * 128: c * D + (o + 1) * 128],
                    bv_bf[:, c:c + 1], start=(c == 0), stop=(c == NCH - 1)),
                    reads=[wBb, bv_bfb],
                    writes=([peb] if (o == 0 and c == 0) else ()))
        peb.did_write(("pe", p.sems["pe"][1]))
        op(p, "vector", lambda e: e.tensor_tensor(
            btot[:, :], P_E[:, 0:8], bo_c[:, :], op=OP.add),
            reads=[peb, bo_cb], writes=[btotb])

        for j in range(NOJ):
            ybuf, Yb = ybuf2[j % 2], Yb2[j % 2]
            def ydst(c, ybuf=ybuf, Yb=Yb):
                return ybuf[:, c * 512:(c + 1) * 512], (Yb if c == 0 else None)
            emit_ln(p, env, xo, OWN, j * 512, ydst)
            Yb.did_write(("dve", p.sems["dve"][1]))
            for o in range(NCH):
                pt, off, ptb = rot[o % 4][0], rot[o % 4][1], rotb[o % 4]
                for c in range(NCH):
                    op(p, "tensor", lambda e, pt=pt, off=off, o=o, c=c, ybuf=ybuf:
                       e.matmul(
                        pt[:, off:off + 512],
                        wA[:, c * D + o * 128: c * D + (o + 1) * 128],
                        ybuf[:, c * 512:(c + 1) * 512],
                        start=(c == 0), stop=(c == NCH - 1)),
                        reads=[wAb, Yb], writes=([ptb] if c == 0 else ()))
                ptb.did_write(("pe", p.sems["pe"][1]))
                op(p, "scalar", lambda e, pt=pt, off=off, o=o: e.activation(
                    qbuf[:, o * 512:(o + 1) * 512], pt[:, off:off + 512],
                    AF.Identity, bias=bq_c[:, o:o + 1], scale=1.0),
                    reads=[ptb, bq_cb], writes=[Qb])
            Qb.did_write(("act", p.sems["act"][1]))
            nst = 8 * j + 8
            op(p, "sync", lambda e, nst=nst: e.dma_start(
                out=Kst[0][:, 0:nst * 128],
                in_=bass.AP(K_dram, 0, [[S, 128], [1, nst * 128]])),
                reads=[Kdb], writes=[Kstb[0]], dma=True, dsem="sd3")
            for hp in range(NCH):
                kb = hp % 2
                if hp + 1 < NCH:
                    op(p, "sync", lambda e, hp=hp, nst=nst: e.dma_start(
                        out=Kst[(hp + 1) % 2][:, 0:nst * 128],
                        in_=bass.AP(K_dram, (hp + 1) * 128 * S,
                                    [[S, 128], [1, nst * 128]])),
                        reads=[Kdb], writes=[Kstb[(hp + 1) % 2]], dma=True,
                        dsem=f"sd{3 + (hp + 1) % 2}")
                pz, pzb = (P_C, pcb) if hp % 2 == 0 else (P_E, peb)

                def sc_group(st, kb=kb, hp=hp, j=j):
                    psc, pscb = (P_A, pab) if st % 2 == 0 else (P_B, pbb)
                    op(p, "tensor", lambda e, psc=psc, kb=kb, hp=hp, st=st: e.matmul(
                        psc[:, 0:512],
                        Kst[kb][0:64, st * 128:(st + 1) * 128],
                        qbuf[0:64, hp * 512:(hp + 1) * 512],
                        start=True, stop=True, tile_position=(0, 0)),
                        reads=[Kstb[kb], Qb], writes=[pscb])
                    op(p, "tensor", lambda e, psc=psc, kb=kb, hp=hp, st=st: e.matmul(
                        psc[:, 512:1024],
                        Kst[kb][64:128, st * 128:(st + 1) * 128],
                        qbuf[64:128, hp * 512:(hp + 1) * 512],
                        start=True, stop=True, tile_position=(64, 0)),
                        reads=[Kstb[kb], Qb])
                    pscb.did_write(("pe", p.sems["pe"][1]))
                    if st >= 8 * j:
                        m = st - 8 * j
                        op(p, "vector", lambda e, psc=psc, m=m: e.tensor_tensor(
                            psc[:, 0:512], psc[:, 0:512],
                            msk[:, m * 512:(m + 1) * 512], op=OP.add),
                            reads=[mskb, pscb], writes=())
                        op(p, "vector", lambda e, psc=psc, m=m: e.tensor_tensor(
                            psc[:, 512:1024], psc[:, 512:1024],
                            msk[:, m * 512:(m + 1) * 512], op=OP.add),
                            reads=[mskb], writes=())
                        pscb.did_write(("dve", p.sems["dve"][1]))

                def consume(st, hp=hp, nst=nst, pz=pz, pzb=pzb):
                    psc, pscb = (P_A, pab) if st % 2 == 0 else (P_B, pbb)
                    es = st % 3
                    op(p, "scalar", lambda e, psc=psc, es=es: e.activation(
                        Ebuf[:, es * 1024:(es + 1) * 1024], psc[:, 0:1024],
                        AF.Exp, bias=0.0, scale=0.125),
                        reads=[pscb], writes=[Eb[es]])
                    op(p, "tensor", lambda e, es=es, st=st, nst=nst: e.matmul(
                        P_D[0:1, 0:512], env["ones"][:, 0:1],
                        Ebuf[:, es * 1024: es * 1024 + 512],
                        start=(st == 0), stop=(st == nst - 1)),
                        reads=[Eb[es], env["ones_b"]],
                        writes=([pdb] if st == 0 else ()))
                    op(p, "tensor", lambda e, es=es, st=st, nst=nst: e.matmul(
                        P_D[0:1, 512:1024], env["ones"][:, 0:1],
                        Ebuf[:, es * 1024 + 512: es * 1024 + 1024],
                        start=(st == 0), stop=(st == nst - 1)),
                        reads=[Eb[es], env["ones_b"]])
                    op(p, "tensor", lambda e, es=es, st=st, hp=hp, nst=nst, pz=pz:
                       e.matmul(
                        pz[0:64, 0:512],
                        V_sb[:, st * D + (2 * hp) * 64: st * D + (2 * hp) * 64 + 64],
                        Ebuf[:, es * 1024: es * 1024 + 512],
                        start=(st == 0), stop=(st == nst - 1),
                        tile_position=(0, 0), skip_group_check=True),
                        reads=[Eb[es], Vb], writes=([pzb] if st == 0 else ()))
                    op(p, "tensor", lambda e, es=es, st=st, hp=hp, nst=nst, pz=pz:
                       e.matmul(
                        pz[64:128, 0:512],
                        V_sb[:, st * D + (2 * hp + 1) * 64:
                             st * D + (2 * hp + 1) * 64 + 64],
                        Ebuf[:, es * 1024 + 512: es * 1024 + 1024],
                        start=(st == 0), stop=(st == nst - 1),
                        tile_position=(0, 64), skip_group_check=True),
                        reads=[Eb[es], Vb])

                sc_group(0)
                for st in range(nst):
                    if st + 1 < nst:
                        sc_group(st + 1)
                    consume(st)
                pdb.did_write(("pe", p.sems["pe"][1]))
                pzb.did_write(("pe", p.sems["pe"][1]))
                op(p, "vector", lambda e: e.reciprocal(
                    rrow[0:1, :], P_D[0:1, 0:512]), reads=[pdb], writes=[rrowb])
                op(p, "vector", lambda e: e.reciprocal(
                    rrow2[0:1, :], P_D[0:1, 512:1024]), reads=[pdb], writes=[rrow2b])
                op(p, "sync", lambda e: e.dma_start(
                    out=Brec[:, :], in_=bcast_ap(rrow, 0, 128, 512)),
                    reads=[rrowb], writes=[Brecb], dma=True, dsem="gd2")
                op(p, "sync", lambda e: e.dma_start(
                    out=Brec2[:, :], in_=bcast_ap(rrow2, 0, 128, 512)),
                    reads=[rrow2b], writes=[Brec2b], dma=True, dsem="gd2")
                op(p, "vector", lambda e, hp=hp, pz=pz: e.tensor_tensor(
                    Z_sb[0:64, hp * 512:(hp + 1) * 512], pz[0:64, 0:512],
                    Brec[0:64, :], op=OP.mult),
                    reads=[pzb, Brecb], writes=[Zb])
                op(p, "vector", lambda e, hp=hp, pz=pz: e.tensor_tensor(
                    Z_sb[64:128, hp * 512:(hp + 1) * 512], pz[64:128, 0:512],
                    Brec2[64:128, :], op=OP.mult),
                    reads=[pzb, Brec2b], writes=())
                Zb.did_write(("dve", p.sems["dve"][1]))
            for o in range(NCH):
                for hc in range(NCH):
                    op(p, "tensor", lambda e, o=o, hc=hc: e.matmul(
                        P_A[:, 0:512],
                        wB[:, hc * D + o * 128: hc * D + (o + 1) * 128],
                        Z_sb[:, hc * 512:(hc + 1) * 512],
                        start=(hc == 0), stop=(hc == NCH - 1)),
                        reads=[wBb, Zb], writes=([pab] if hc == 0 else ()))
                pab.did_write(("pe", p.sems["pe"][1]))
                i = o % 2
                u = o % 2
                op(p, "sync", lambda e, i=i, o=o, j=j: e.dma_start(
                    out=env["xs"][i][:, :], in_=chunk_ap(xo, OWN, o, j * 512, 512)),
                    writes=[env["xs_b"][i]], dma=True, dsem=f"sd{i}")
                op(p, "vector", lambda e, i=i, u=u, o=o: e.scalar_tensor_tensor(
                    us[u][:, :], P_A[:, 0:512], btot[:, o:o + 1],
                    env["xs"][i][:, :], op0=OP.add, op1=OP.add),
                    reads=[pab, btotb, env["xs_b"][i]], writes=[usb[u]])
                op(p, "sync", lambda e, u=u, o=o, j=j: e.dma_start(
                    out=bass.AP(upd, o * 128 * OWN + j * 512, [[OWN, 128], [1, 512]]),
                    in_=us[u][:, :]),
                    reads=[usb[u]], dma=True)
        p._wait("sync", "sdma", p.sems["sdma"][1])

        with nc.Block("attn_a") as blk:
            p.finalize(blk)
    return nc


def make_masks(r):
    out = np.zeros((128, 8 * 512), np.float32)
    i = np.arange(128)[:, None]
    jc = np.arange(512)[None, :]
    for m in range(8):
        if r == 0:
            if m < 4:
                pat = np.where(128 * m + i <= jc, 0.0, NEG)
            else:
                pat = np.full((128, 512), NEG, np.float32)
        else:
            if m < 4:
                pat = np.zeros((128, 512), np.float32)
            else:
                pat = np.where(128 * (m - 4) + i <= jc, 0.0, NEG)
        out[:, m * 512:(m + 1) * 512] = pat
    return out.astype(BFD)


def run_attn_a_launch(nc_a, xt_state, weights):
    in_maps = []
    for core in range(8):
        b, r = core // 2, core % 2
        cols = _own_cols(r)
        m = {
            "xt": xt_state[b],
            "xo": np.ascontiguousarray(xt_state[b][:, cols]),
            "masks": make_masks(r),
            "wq": weights["Wq"], "wk": weights["Wk"],
            "wv": weights["Wv"], "wo": weights["Wo"],
            "bq": _prep_bias_pc(weights["bq"], 8),
            "bk": _prep_bias_pc(weights["bk"], 8),
            "bv": _prep_bias_pc(weights["bv"], 8),
            "bo": _prep_bias_pc(weights["bo"], 8),
        }
        in_maps.append(m)
    res = run_bass_kernel_spmd(nc_a, in_maps, core_ids=list(range(8)))
    out = [x.copy() for x in xt_state]
    for core in range(8):
        b, r = core // 2, core % 2
        u = res.results[core]["upd"]
        for j in range(NOJ):
            g = 2 * j + r
            out[b][:, g * 512:(g + 1) * 512] = u[:, j * 512:(j + 1) * 512]
    return out




def build_attn_b():
    nc = bass.Bass("TRN2", target_bir_lowering=False)
    xt = nc.dram_tensor("xt", [D, S], F32, kind="ExternalInput")
    xo = nc.dram_tensor("xo", [D, OWN], F32, kind="ExternalInput")
    w = nc.dram_tensor("w", [H * D, D], F32, kind="ExternalInput")
    upd = nc.dram_tensor("upd", [D, OWN], F32, kind="ExternalOutput")
    wbf = nc.dram_tensor("wbf", [H, 128, NCH * D], BF16, kind="Internal")
    TBB = 256
    NJ2 = OWN // TBB

    with ExitStack() as ctx:
        sb = lambda name, shp, dt: ctx.enter_context(nc.sbuf_tensor(name, shp, dt))
        p = Prog(nc, ctx)
        env = {}
        make_ln_env(p, ctx, nc, env, nxs=2)

        Y_sb, Yb = sb("Y_sb", [128, NCH * S], BF16), Buf()
        Ytm, Ytmb = sb("Ytm", [128, 32 * D], BF16), Buf()
        yq, yqb = sb("yq", [128, NCH * TBB], BF16), Buf()
        Ebuf = sb("Ebuf", [128, 6 * TBB], BF16)
        Eb = [Buf() for _ in range(3)]
        G_sb = [sb(f"G_sb{i}", [128, NCH * TBB], BF16) for i in range(2)]
        Gb = [Buf() for _ in range(2)]
        outacc, oaccb = sb("outacc", [128, NCH * TBB], F32), Buf()
        wh = [sb(f"wh{i}", [128, NCH * D], BF16) for i in range(2)]
        whb = [Buf() for _ in range(2)]
        wdb = Buf()
        Brec, Brecb = sb("Brec", [128, TBB], F32), Buf()
        rrow, rrowb = sb("rrow", [1, TBB], F32), Buf()
        ident, identb = sb("ident", [128, 128], BF16), Buf()
        onesb_t, onesb_tb = sb("ones_bf", [128, 128], BF16), Buf()
        us = [env["nrm"][i] for i in range(2)]
        usb = [env["nrm_b"][i] for i in range(2)]

        P_A = ctx.enter_context(nc.psum_tensor("P_A", [128, 1024], F32))
        P_B = ctx.enter_context(nc.psum_tensor("P_B", [128, 1024], F32))
        P_C = ctx.enter_context(nc.psum_tensor("P_C", [128, 512], F32))
        P_E = ctx.enter_context(nc.psum_tensor("P_E", [128, 512], F32))
        P_F = ctx.enter_context(nc.psum_tensor("P_F", [128, 512], F32))
        P_D = ctx.enter_context(nc.psum_tensor("P_D", [2, 512], F32))
        pab, pbb, pcb, peb, pfb = Buf(), Buf(), Buf(), Buf(), Buf()
        env["p_stat"], env["p_stat_b"] = P_D, Buf()
        pdb = env["p_stat_b"]
        env["p_stat2"], env["p_stat2_b"] = P_F, pfb

        ms = p.op("gpsimd", lambda e: e.memset(onesb_t[:, :], 1.0))
        onesb_tb.did_write(ms)
        op(p, "gpsimd", lambda e: e.affine_select(
            ident[:, :], onesb_t[:, :], [[1, 128]], OP.is_equal, 0.0,
            base=0, channel_multiplier=-1),
            reads=[onesb_tb], writes=[identb])

        for h in range(H):
            k = h % 2
            op(p, "gpsimd", lambda e, h=h, k=k: e.dma_start(
                out=wh[k][:, :],
                in_=bass.AP(w, h * D * D, [[D, 128], [128 * D, NCH], [1, D]])),
                writes=[whb[k]], dma=True, dsem="gd1")
            op(p, "sync", lambda e, h=h, k=k: e.dma_start(
                out=bass.AP(wbf, h * 128 * NCH * D, [[NCH * D, 128], [1, NCH * D]]),
                in_=wh[k][:, :]),
                reads=[whb[k]], writes=([wdb] if h == 0 else ()), dma=True, dsem="sd5")
        wdb.did_write(("sd5", None))

        for tb in range(NTB):
            def ydst(c, tb=tb):
                return (Y_sb[:, c * S + tb * 512: c * S + (tb + 1) * 512],
                        (Yb if (c == 0 and tb == 0) else None))
            emit_ln(p, env, xt, S, tb * 512, ydst)
        Yb.did_write(("dve", p.sems["dve"][1]))

        trb = [pcb, peb]
        trt = [P_C, P_E]
        for tt in range(32):
            for c in range(NCH):
                k = (tt * NCH + c) % 2
                pv = trt[k][:, 0:64].bitcast(BF16)
                op(p, "tensor", lambda e, pv=pv, tt=tt, c=c: e.transpose(
                    pv, Y_sb[:, c * S + tt * 128: c * S + (tt + 1) * 128],
                    ident[:, :]),
                    reads=[Yb, identb], writes=[trb[k]])
                op(p, "vector", lambda e, pv=pv, tt=tt, c=c: e.tensor_copy(
                    Ytm[:, tt * D + c * 128: tt * D + (c + 1) * 128], pv),
                    reads=[trb[k]], writes=([Ytmb] if (tt == 0 and c == 0) else ()))
        Ytmb.did_write(("dve", p.sems["dve"][1]))

        def load_wh(h):
            k = h % 2
            op(p, "sync", lambda e, h=h, k=k: e.dma_start(
                out=wh[k][:, :],
                in_=bass.AP(wbf, h * 128 * NCH * D,
                            [[NCH * D, 128], [1, NCH * D]])),
                reads=[wdb], writes=[whb[k]], dma=True, dsem=f"sd{3 + k}")

        for j2 in range(NJ2):
            def ydst(c):
                return (yq[:, c * TBB:(c + 1) * TBB], (yqb if c == 0 else None))
            emit_ln(p, env, xo, OWN, j2 * TBB, ydst, n=TBB)
            yqb.did_write(("dve", p.sems["dve"][1]))
            op(p, "gpsimd", lambda e: e.memset(outacc[:, :], 0.0),
               writes=[oaccb])
            if j2 == 0:
                load_wh(0)

            def g_phase(h):
                gb = h % 2
                hr = (h % 2) * 64
                hc = h // 2

                def sc_pair(q):
                    psc, pscb2 = ((P_C, pcb), (P_E, peb), (P_F, pfb))[q % 3]
                    for k2 in range(2):
                        st = 2 * q + k2
                        op(p, "tensor", lambda e, psc=psc, hr=hr, hc=hc, st=st,
                           k2=k2: e.matmul(
                            psc[:, k2 * TBB:(k2 + 1) * TBB],
                            Y_sb[hr:hr + 64,
                                 hc * S + st * 128: hc * S + (st + 1) * 128],
                            yq[hr:hr + 64, hc * TBB:(hc + 1) * TBB],
                            start=True, stop=True,
                            tile_position=(hr, 0)),
                            reads=[Yb, yqb],
                            writes=([pscb2] if k2 == 0 else ()))
                    pscb2.did_write(("pe", p.sems["pe"][1]))

                def consume_pair(q):
                    psc, pscb2 = ((P_C, pcb), (P_E, peb), (P_F, pfb))[q % 3]
                    es = q % 3
                    op(p, "scalar", lambda e, psc=psc, es=es: e.activation(
                        Ebuf[:, es * 512:(es + 1) * 512], psc[:, 0:512],
                        AF.Exp, bias=0.0, scale=0.125),
                        reads=[pscb2], writes=[Eb[es]])
                    op(p, "tensor", lambda e, es=es, q=q: e.matmul(
                        P_D[0:1, 0:512], env["ones"][:, 0:1],
                        Ebuf[:, es * 512:(es + 1) * 512],
                        start=(q == 0), stop=(q == 15)),
                        reads=[Eb[es], env["ones_b"]],
                        writes=([pdb] if q == 0 else ()))
                    for k2 in range(2):
                        st = 2 * q + k2
                        eoff = es * 512 + k2 * TBB
                        for c in range(NCH):
                            pt = P_A if c < 4 else P_B
                            ptb = pab if c < 4 else pbb
                            off = (c % 4) * TBB
                            op(p, "tensor", lambda e, pt=pt, off=off, st=st,
                               c=c, eoff=eoff: e.matmul(
                                pt[:, off:off + TBB],
                                Ytm[:, st * D + c * 128: st * D + (c + 1) * 128],
                                Ebuf[:, eoff:eoff + TBB],
                                start=(st == 0 and c % 2 == 0),
                                stop=(st == 31 and c % 2 == 1)),
                                reads=[Eb[es], Ytmb],
                                writes=([ptb] if (st == 0 and c == 0) else ()))

                sc_pair(0)
                sc_pair(1)
                for q in range(16):
                    if q < 14:
                        sc_pair(q + 2)
                    consume_pair(q)
                pab.did_write(("pe", p.sems["pe"][1]))
                pbb.did_write(("pe", p.sems["pe"][1]))
                pdb.did_write(("pe", p.sems["pe"][1]))
                for b2 in range(4):
                    pt = P_A if b2 < 2 else P_B
                    ptb = pab if b2 < 2 else pbb
                    off = (b2 % 2) * 512
                    op(p, "scalar", lambda e, pt=pt, off=off, b2=b2, gb=gb:
                       e.activation(
                        G_sb[gb][:, b2 * 512:(b2 + 1) * 512],
                        pt[:, off:off + 512], AF.Copy, bias=0.0, scale=1.0),
                        reads=[ptb], writes=([Gb[gb]] if b2 == 0 else ()))
                Gb[gb].did_write(("act", p.sems["act"][1]))

            def norm_phase(h):
                gb = h % 2
                op(p, "vector", lambda e: e.tensor_copy(
                    rrow[0:1, :], P_D[0:1, 0:TBB]), reads=[pdb], writes=[rrowb])
                op(p, "vector", lambda e: e.tensor_tensor(
                    rrow[0:1, :], rrow[0:1, :], P_D[0:1, TBB:2 * TBB],
                    op=OP.add), reads=[pdb, rrowb], writes=[rrowb])
                op(p, "vector", lambda e: e.reciprocal(
                    rrow[0:1, :], rrow[0:1, :]), reads=[rrowb], writes=[rrowb])
                op(p, "sync", lambda e: e.dma_start(
                    out=Brec[:, :], in_=bcast_ap(rrow, 0, 128, TBB)),
                    reads=[rrowb], writes=[Brecb], dma=True, dsem="gd2")
                for c in range(NCH):
                    op(p, "vector", lambda e, c=c, gb=gb: e.tensor_tensor(
                        G_sb[gb][:, c * TBB:(c + 1) * TBB],
                        G_sb[gb][:, c * TBB:(c + 1) * TBB], Brec[:, :],
                        op=OP.mult),
                        reads=[Gb[gb], Brecb], writes=())
                Gb[gb].did_write(("dve", p.sems["dve"][1]))

            def w_phase(h):
                gb = h % 2
                for dc in range(8):
                    pt, ptb = (P_E, peb) if dc % 2 == 0 else (P_F, pfb)
                    for c in range(NCH):
                        op(p, "tensor", lambda e, pt=pt, dc=dc, c=c, gb=gb, h=h:
                           e.matmul(
                            pt[:, 0:TBB],
                            wh[h % 2][:, c * D + dc * 128: c * D + (dc + 1) * 128],
                            G_sb[gb][:, c * TBB:(c + 1) * TBB],
                            start=(c == 0), stop=(c == NCH - 1)),
                            reads=[whb[h % 2], Gb[gb]],
                            writes=([ptb] if c == 0 else ()))
                    ptb.did_write(("pe", p.sems["pe"][1]))
                    op(p, "vector", lambda e, pt=pt, dc=dc: e.tensor_tensor(
                        outacc[:, dc * TBB:(dc + 1) * TBB],
                        outacc[:, dc * TBB:(dc + 1) * TBB],
                        pt[:, 0:TBB], op=OP.add),
                        reads=[ptb, oaccb], writes=())
                oaccb.did_write(("dve", p.sems["dve"][1]))

            for h in range(H):
                g_phase(h)
                if h > 0:
                    w_phase(h - 1)
                if not (j2 == NJ2 - 1 and h == H - 1):
                    load_wh((h + 1) % H)
                norm_phase(h)
            w_phase(H - 1)
            for o in range(NCH):
                i = o % 2
                u = o % 2
                op(p, "sync", lambda e, i=i, o=o, j2=j2: e.dma_start(
                    out=env["xs"][i][:, 0:TBB],
                    in_=chunk_ap(xo, OWN, o, j2 * TBB, TBB)),
                    writes=[env["xs_b"][i]], dma=True, dsem=f"sd{i}")
                op(p, "vector", lambda e, i=i, u=u, o=o: e.tensor_tensor(
                    us[u][:, 0:TBB], env["xs"][i][:, 0:TBB],
                    outacc[:, o * TBB:(o + 1) * TBB], op=OP.add),
                    reads=[env["xs_b"][i], oaccb], writes=[usb[u]])
                op(p, "sync", lambda e, u=u, o=o, j2=j2: e.dma_start(
                    out=bass.AP(upd, o * 128 * OWN + j2 * TBB,
                                [[OWN, 128], [1, TBB]]),
                    in_=us[u][:, 0:TBB]),
                    reads=[usb[u]], dma=True)
        p._wait("sync", "sdma", p.sems["sdma"][1])

        with nc.Block("attn_b") as blk:
            p.finalize(blk)
    return nc


def run_attn_b_launch(nc_b, xt_state, weights):
    in_maps = []
    for core in range(8):
        b, r = core // 2, core % 2
        cols = _own_cols(r)
        in_maps.append({
            "xt": xt_state[b],
            "xo": np.ascontiguousarray(xt_state[b][:, cols]),
            "w": weights["weight"],
        })
    res = run_bass_kernel_spmd(nc_b, in_maps, core_ids=list(range(8)))
    out = [x.copy() for x in xt_state]
    for core in range(8):
        b, r = core // 2, core % 2
        u = res.results[core]["upd"]
        for j in range(NOJ):
            g = 2 * j + r
            out[b][:, g * 512:(g + 1) * 512] = u[:, j * 512:(j + 1) * 512]
    return out


_NC_CACHE = {}


def _get_nc(name, builder):
    if name not in _NC_CACHE:
        _NC_CACHE[name] = builder()
    return _NC_CACHE[name]


def kernel(**inputs):
    x = np.asarray(inputs["L0_attn_a_x"], np.float32)
    xt_state = [np.ascontiguousarray(x[b].T) for b in range(BATCH)]

    attn_a_w = {
        "Wq": inputs["L0_attn_a_Wq"], "bq": inputs["L0_attn_a_bq"],
        "Wk": inputs["L0_attn_a_Wk"], "bk": inputs["L0_attn_a_bk"],
        "Wv": inputs["L0_attn_a_Wv"], "bv": inputs["L0_attn_a_bv"],
        "Wo": inputs["L0_attn_a_Wo"], "bo": inputs["L0_attn_a_bo"],
    }
    ffn_w = {k[len("L0_ffn_"):]: np.asarray(v) for k, v in inputs.items()
             if k.startswith("L0_ffn_")}
    attn_b_w = {"weight": inputs["L0_attn_b_weight"]}

    nc_a = _get_nc("attn_a", build_attn_a)
    nc_f = _get_nc("ffn", build_ffn)
    nc_b = _get_nc("attn_b", build_attn_b)

    xt_state = run_attn_a_launch(nc_a, xt_state, attn_a_w)
    xt_state = run_ffn_launch(nc_f, xt_state, ffn_w)
    xt_state = run_attn_b_launch(nc_b, xt_state, attn_b_w)
    xt_state = run_ffn_launch(nc_f, xt_state, ffn_w)

    return np.ascontiguousarray(
        np.stack([xt.T for xt in xt_state])).astype(np.float32)
